# version 05
# baseline (speedup 1.0000x reference)
import contextlib
import numpy as np
import concourse.bass as bass
import concourse.mybir as mybir

F32 = mybir.dt.float32
I32 = mybir.dt.int32
AF = mybir.ActivationFunctionType
ALU = mybir.AluOpType
AX = mybir.AxisListType

NT = 64
NPASS = 4
TPP = 8
EPS = 1e-5
NEG = -1.0e30


def build_nc(debug=False, pipe=True, pool_short=False, trunc=None):
    nc = bass.Bass("TRN2", target_bir_lowering=False)

    x_d = nc.dram_tensor("x", [8192, 1024], F32, kind="ExternalInput").ap()
    wq_d = nc.dram_tensor("wq_t", [128, 8192], F32, kind="ExternalInput").ap()
    wk_d = nc.dram_tensor("wk_t", [128, 8192], F32, kind="ExternalInput").ap()
    wv_d = nc.dram_tensor("wv_t", [128, 8192], F32, kind="ExternalInput").ap()
    wp_d = nc.dram_tensor("wp_t", [128, 8192], F32, kind="ExternalInput").ap()
    w1_d = nc.dram_tensor("w1_t", [128, 2048], F32, kind="ExternalInput").ap()
    b1_d = nc.dram_tensor("b1_t", [128, 2], F32, kind="ExternalInput").ap()
    w2_d = nc.dram_tensor("w2_t", [128, 16], F32, kind="ExternalInput").ap()
    b2_d = nc.dram_tensor("b2_t", [1, 8], F32, kind="ExternalInput").ap()
    wa_d = nc.dram_tensor("wa_r", [32768, 1024], F32, kind="ExternalInput").ap()
    wb_d = nc.dram_tensor("wb_r", [32768, 1024], F32, kind="ExternalInput").ap()
    cid_d = nc.dram_tensor("cident", [128, 128], F32, kind="ExternalInput").ap()
    cmask_d = nc.dram_tensor("cmask", [128, 256], F32, kind="ExternalInput").ap()
    ciofc_d = nc.dram_tensor("ciota_fc", [128, 32], F32, kind="ExternalInput").ap()
    conr_d = nc.dram_tensor("cones_r", [1, 128], F32, kind="ExternalInput").ap()
    conc_d = nc.dram_tensor("cones_c", [128, 1], F32, kind="ExternalInput").ap()
    clow_d = nc.dram_tensor("clow8", [8, 8], F32, kind="ExternalInput").ap()
    cup_d = nc.dram_tensor("cup8", [8, 8], F32, kind="ExternalInput").ap()
    ciop_d = nc.dram_tensor("ciota_p", [128, 1], F32, kind="ExternalInput").ap()
    cyidx_d = nc.dram_tensor("cyidx", [128, 32], I32, kind="ExternalInput").ap()
    ceps_d = nc.dram_tensor("ceps", [128, 1], F32, kind="ExternalInput").ap()

    yscr_d = nc.dram_tensor("y_scr", [8192, 1024], F32, kind="Internal").ap()
    out_d = nc.dram_tensor("out", [4096, 1024], F32, kind="ExternalOutput").ap()
    if debug:
        dbg_d = nc.dram_tensor("dbg", [128, 16], F32, kind="ExternalOutput").ap()

    es = contextlib.ExitStack()
    sb = lambda name, shape, dt=F32: es.enter_context(nc.sbuf_tensor(name, shape, dt))
    sem = lambda name: es.enter_context(nc.semaphore(name))

    cident = sb("s_cident", [128, 128])
    cmask = sb("s_cmask", [128, 256])
    ciofc = sb("s_ciota_fc", [128, 32])
    conr = sb("s_cones_r", [1, 128])
    conc = sb("s_cones_c", [128, 1])
    clow8 = sb("s_clow8", [8, 8])
    cup8 = sb("s_cup8", [8, 8])
    ciop = sb("s_ciota_p", [128, 1])
    cyidx = sb("s_cyidx", [128, 32], I32)
    ceps = sb("s_ceps", [128, 1])
    b2r = sb("s_b2r", [1, 8])
    b1f = sb("s_b1f", [128, 2])
    w1f = sb("s_w1f", [128, 2048])
    w2f = sb("s_w2f", [128, 16])

    x_in = sb("x_in", [128, 2048])
    bnst = sb("bnst", [128, 24])
    bnag = sb("bnag", [128, 8])
    rstdt = sb("rstdt", [128, 2])
    sqtt = sb("sqtt", [128, 2])
    pooled_acc = sb("pooled_acc", [1, 1024])
    pool_tmp = sb("pool_tmp", [1, 1024])

    r_pool_f = sb("r_pool_f", [128, 8])
    r_h = sb("r_h", [128, 2])
    r_logit = sb("r_logit", [1, 8])
    r_mx = sb("r_mx", [1, 1])
    r_mxn = sb("r_mxn", [1, 1])
    r_ex = sb("r_ex", [1, 8])
    r_es = sb("r_es", [1, 1])
    r_er = sb("r_er", [1, 1])
    r_probs = sb("r_probs", [1, 8])
    r_A = sb("r_A", [8, 8])
    r_B = sb("r_B", [8, 8])
    r_gt = sb("r_gt", [8, 8])
    r_rank = sb("r_rank", [8, 1])
    r_sel = sb("r_sel", [8, 1])
    r_wun = sb("r_wun", [8, 1])
    r_rZ = sb("r_rZ", [1, 1])
    r_oh = sb("r_oh", [8, 3])
    r_wsel = sb("r_wsel", [1, 3])
    r_id = sb("r_id", [1, 3])
    w_bc = sb("w_bc", [128, 3])
    e_bc = sb("e_bc", [128, 3])
    idx_f = sb("idx_f", [128, 96])
    idx_i = sb("idx_i", [128, 96], I32)

    ps_t = es.enter_context(nc.psum_tensor("ps", [128, 4096], F32))

    def PB(bank, lo=0, n=512, p0=0, np_=128):
        return ps_t[p0:p0 + np_, bank * 512 + lo: bank * 512 + lo + n]

    s_x0, s_x1 = sem("s_x0"), sem("s_x1")
    s_wq, s_wkv, s_wp, s_cst = sem("s_wq"), sem("s_wkv"), sem("s_wp"), sem("s_cst")
    s_ys0, s_ys1 = sem("s_ys0"), sem("s_ys1")
    s_yr0, s_yr1 = sem("s_yr0"), sem("s_yr1")
    s_ga0, s_ga1 = sem("s_ga0"), sem("s_ga1")
    s_out = sem("s_out")

    a_sq = sem("a_sq")
    v_var1, a_ln1, v_u1 = sem("v_var1"), sem("a_ln1"), sem("v_u1")
    t_tr, v_tr = sem("t_tr"), sem("v_tr")
    t_q, v_q = sem("t_q"), sem("v_q")
    t_kv, v_kv = sem("t_kv"), sem("v_kv")
    t_lg = sem("t_lg")
    v_sub, a_exp, v_w = sem("v_sub"), sem("a_exp"), sem("v_w")
    t_wt, v_wt, t_oh = sem("t_wt"), sem("v_wt"), sem("t_oh")
    t_o, v_o = sem("t_o"), sem("v_o")
    t_proj, v_y = sem("t_proj"), sem("v_y")
    v_var2, a_ln2, v_u2 = sem("v_var2"), sem("a_ln2"), sem("v_u2")
    t_pool, v_pool = sem("t_pool"), sem("v_pool")
    r_pe, r_dv, r_ac, rdone = sem("r_pe"), sem("r_dv"), sem("r_ac"), sem("rdone")
    v_yb = sem("v_yb")
    v_u23, t_tr3, v_tr3 = sem("v_u23"), sem("t_tr3"), sem("v_tr3")
    a_sq3, v_var3, a_ln3 = sem("a_sq3"), sem("v_var3"), sem("a_ln3")
    t_z, a_g, t_o3, v_ev = sem("t_z"), sem("a_g"), sem("t_o3"), sem("v_ev")
    t_wfree = sem("t_wfree")

    NCST = 14

    def BK(k, i):
        return (k % 2) * 4 + i

    es1 = contextlib.ExitStack()
    sb1 = lambda name, shape, dt=F32: es1.enter_context(nc.sbuf_tensor(name, shape, dt))
    wq_s = sb1("wq_s", [128, 8192])
    wp_s = sb1("wp_s", [128, 8192])
    wkv_s = sb1("wkv_s", [128, 8192])
    u1 = sb1("u1", [128, 2048])
    u1f = sb1("u1f", [128, 2048])
    qf = sb1("qf", [128, 2048])
    k16f = sb1("k16f", [128, 128])
    k16t = sb1("k16t", [16, 1024])
    v16t = sb1("v16t", [16, 1024])
    msub = sb1("msub", [128, 512])
    mxg = sb1("mxg", [128, 32])
    exw = sb1("exw", [128, 512])
    esum = sb1("esum", [128, 32])
    erc = sb1("erc", [128, 32])
    w_tm = sb1("w_tm", [128, 512])
    wTh = sb1("wTh", [16, 2048])
    o_f = sb1("o_f", [128, 2048])
    y_t = sb1("y_t", [128, 2048])
    u2 = sb1("u2", [128, 2048])

    blk1 = es1.enter_context(nc.Block("phase1"))

    @blk1.sync
    def _(sync):
        for dst, src in [(cident, cid_d), (cmask, cmask_d), (ciofc, ciofc_d),
                         (conr, conr_d), (conc, conc_d), (clow8, clow_d),
                         (cup8, cup_d), (ciop, ciop_d), (cyidx, cyidx_d),
                         (ceps, ceps_d),
                         (b2r, b2_d), (b1f, b1_d), (w1f, w1_d), (w2f, w2_d)]:
            sync.dma_start(out=dst[:, :], in_=src[:, :]).then_inc(s_cst, 16)
        sync.dma_start(out=x_in[:, 0:1024], in_=x_d[0:128, :]).then_inc(s_x0, 16)
        sync.dma_start(out=x_in[:, 1024:2048], in_=x_d[128:256, :]).then_inc(s_x1, 16)
        sync.dma_start(out=wq_s[:, :], in_=wq_d[:, :]).then_inc(s_wq, 16)
        sync.dma_start(out=wkv_s[:, :], in_=wk_d[:, :]).then_inc(s_wkv, 16)
        sync.dma_start(out=wp_s[:, :], in_=wp_d[:, :]).then_inc(s_wp, 16)
        sync.wait_ge(t_kv, 1)
        sync.dma_start(out=wkv_s[:, :], in_=wv_d[:, :]).then_inc(s_wkv, 16)
        for k in range(2, NT if trunc is None else trunc):
            pi = k % 2
            sync.wait_ge(v_y, k - 1)
            sync.dma_start(out=x_in[:, pi * 1024:(pi + 1) * 1024],
                           in_=x_d[k * 128:(k + 1) * 128, :]).then_inc(
                               s_x0 if pi == 0 else s_x1, 16)
        if trunc is not None:
            sync.wait_ge(v_y, trunc)
            sync.dma_start(out=out_d[0:128, :],
                           in_=y_t[:, 0:1024]).then_inc(s_out, 16)
            sync.wait_ge(s_out, 16)

    @blk1.gpsimd
    def _(g):
        for k in range(NT if trunc is None else trunc):
            pi = k % 2
            g.wait_ge(v_y, k + 1)
            g.dma_start(yscr_d[k * 128:(k + 1) * 128, :],
                        y_t[:, pi * 1024:(pi + 1) * 1024]).then_inc(
                            s_ys0 if pi == 0 else s_ys1, 16)

    @blk1.tensor
    def _(t):
        t.wait_ge(s_cst, 16 * NCST)

        def AA(k):
            return (k % 2) * 4

        def tr_stage(k):
            pi = k % 2
            A = AA(k)
            u1_p = u1[:, pi * 1024:(pi + 1) * 1024]
            t.wait_ge(v_u1, k + 1)
            if k >= 2:
                t.wait_ge(v_y, k - 1)
            for ic in range(8):
                bank = A if ic < 4 else A + 1
                inst = t.transpose(PB(bank, (ic % 4) * 128, 128),
                                   u1_p[:, ic * 128:(ic + 1) * 128], cident[:, :])
            inst.then_inc(t_tr, 1)

        def q_stage(k):
            pi = k % 2
            A = AA(k)
            u1f_p = u1f[:, pi * 1024:(pi + 1) * 1024]
            if k == 0:
                t.wait_ge(s_wq, 16)
            t.wait_ge(v_tr, k + 1)
            if k >= 2:
                t.wait_ge(v_y, k - 1)
            for oc in range(8):
                bank = A + 2 if oc < 4 else A
                lo = (oc % 4) * 128
                for ic in range(8):
                    inst = t.matmul(PB(bank, lo, 128),
                                    wq_s[:, ic * 1024 + oc * 128: ic * 1024 + (oc + 1) * 128],
                                    u1f_p[:, ic * 128:(ic + 1) * 128],
                                    start=(ic == 0), stop=(ic == 7))
            inst.then_inc(t_q, 1)

        def kv_block():
            t.wait_ge(s_wkv, 16)
            t.wait_ge(v_q, 1)
            for half in range(2):
                for ic in range(8):
                    inst = t.matmul(PB(1 + half, 0, 512, 0, 16),
                                    u1f[:, ic * 128: ic * 128 + 16],
                                    wkv_s[:, ic * 1024 + half * 512: ic * 1024 + (half + 1) * 512],
                                    start=(ic == 0), stop=(ic == 7))
            inst.then_inc(t_kv, 1)
            t.wait_ge(s_wkv, 32)
            t.wait_ge(v_kv, 1)
            for half in range(2):
                for ic in range(8):
                    inst = t.matmul(PB(1 + half, 0, 512, 0, 16),
                                    u1f[:, ic * 128: ic * 128 + 16],
                                    wkv_s[:, ic * 1024 + half * 512: ic * 1024 + (half + 1) * 512],
                                    start=(ic == 0), stop=(ic == 7))
            inst.then_inc(t_kv, 1)
            t.wait_ge(v_kv, 2)
            t.wait_ge(v_tr, 1)
            for ic in range(8):
                inst = t.transpose(PB(0, ic * 16, 16),
                                   k16t[0:16, ic * 128:(ic + 1) * 128],
                                   cident[0:16, 0:16])
            inst.then_inc(t_kv, 1)
            t.wait_ge(v_kv, 3)

        def logits_stage(k):
            pi = k % 2
            A = AA(k)
            qf_p = qf[:, pi * 1024:(pi + 1) * 1024]
            t.wait_ge(v_q, k + 1)
            for h in range(16):
                po = (h % 2) * 64
                ch = h // 2
                inst = t.matmul(PB(A + 1, h * 16, 16),
                                qf_p[po:po + 64, ch * 128:(ch + 1) * 128],
                                k16f[po:po + 64, ch * 16:(ch + 1) * 16],
                                start=True, stop=True)
            inst.then_inc(t_lg, 1)

        def pool_stage(k):
            pi = k % 2
            u2_pp = u2[:, pi * 1024:(pi + 1) * 1024]
            t.wait_ge(v_u2, k + 1)
            if pool_short:
                t.wait_ge(v_pool, k)
            st_ = (k == 0) if not pool_short else True
            sp_ = (k == NT - 1) if not pool_short else True
            inst = t.matmul(PB(3, 0, 512, 0, 1), conc[:, :], u2_pp[:, 0:512],
                            start=st_, stop=sp_)
            inst = t.matmul(PB(7, 0, 512, 0, 1), conc[:, :], u2_pp[:, 512:1024],
                            start=st_, stop=sp_)
            inst.then_inc(t_pool, 1)

        def wT_o_proj_stage(k, wt_base):
            pi = k % 2
            A = AA(k)
            wtm_p = w_tm[:, pi * 256:(pi + 1) * 256]
            of_p = o_f[:, pi * 1024:(pi + 1) * 1024]
            t.wait_ge(v_w, k + 1)
            t.wait_ge(v_sub, k + 1)
            for hh in range(8):
                bank = A + 2 if hh < 4 else A
                inst = t.transpose(PB(bank, (hh % 4) * 128, 128, 0, 16),
                                   wtm_p[:, hh * 16:(hh + 1) * 16], cident[:, :])
            inst.then_inc(t_wt, 1)
            if k >= 1:
                pool_stage(k - 1)
            t.wait_ge(v_wt, wt_base + 1)
            for hh in range(8):
                h = 8 + hh
                bank = A + 2 if hh < 4 else A
                inst = t.transpose(PB(bank, (hh % 4) * 128, 128, 0, 16),
                                   wtm_p[:, h * 16:(h + 1) * 16], cident[:, :])
            inst.then_inc(t_wt, 1)
            for h in range(8):
                p8 = h // 2
                inst = t.matmul(PB(A + 1, p8 * 128, 128, (h % 2) * 64, 64),
                                v16t[0:16, h * 64:(h + 1) * 64],
                                wTh[0:16, h * 128:(h + 1) * 128],
                                start=True, stop=True)
            inst.then_inc(t_oh, 1)
            t.wait_ge(v_wt, wt_base + 2)
            for h in range(8, 16):
                p8 = h // 2 - 4
                inst = t.matmul(PB(A + 2, p8 * 128, 128, (h % 2) * 64, 64),
                                v16t[0:16, h * 64:(h + 1) * 64],
                                wTh[0:16, h * 128:(h + 1) * 128],
                                start=True, stop=True)
            inst.then_inc(t_oh, 1)
            if k == 0:
                t.wait_ge(s_wp, 16)
            t.wait_ge(v_o, k + 1)
            for half in range(2):
                bank = A + 1 if half == 0 else A + 2
                for ic in range(8):
                    inst = t.matmul(PB(bank, 0, 512),
                                    of_p[:, ic * 128:(ic + 1) * 128],
                                    wp_s[:, ic * 1024 + half * 512: ic * 1024 + (half + 1) * 512],
                                    start=(ic == 0), stop=(ic == 7))
            inst.then_inc(t_proj, 1)

        NR = NT if trunc is None else trunc
        if pipe:
            tr_stage(0)
            q_stage(0)
            kv_block()
            logits_stage(0)
            for k in range(NR):
                if k + 1 < NR:
                    tr_stage(k + 1)
                    q_stage(k + 1)
                    logits_stage(k + 1)
                wT_o_proj_stage(k, 2 * k)
            if trunc is None:
                pool_stage(NT - 1)
        else:
            for k in range(NT):
                tr_stage(k)
                q_stage(k)
                if k == 0:
                    kv_block()
                logits_stage(k)
                wT_o_proj_stage(k, 2 * k)
            pool_stage(NT - 1)

        if trunc is not None:
            return
        t.wait_ge(r_dv, 1)
        for ic in range(8):
            inst = t.matmul(PB(0, ic, 1), pool_tmp[0:1, ic * 128:(ic + 1) * 128],
                            conc[0:1, 0:1], start=True, stop=True)
        inst.then_inc(r_pe, 1)
        t.wait_ge(r_dv, 2)
        for fc2 in range(2):
            for ic in range(8):
                inst = t.matmul(PB(1, fc2, 1),
                                w1f[:, ic * 256 + fc2 * 128: ic * 256 + (fc2 + 1) * 128],
                                r_pool_f[:, ic:ic + 1],
                                start=(ic == 0), stop=(ic == 7))
        inst.then_inc(r_pe, 1)
        t.wait_ge(r_ac, 1)
        for fc2 in range(2):
            inst = t.matmul(PB(2, 0, 8, 0, 1), r_h[:, fc2:fc2 + 1],
                            w2f[:, fc2 * 8:(fc2 + 1) * 8],
                            start=(fc2 == 0), stop=(fc2 == 1))
        inst.then_inc(r_pe, 1)
        t.wait_ge(r_dv, 4)
        inst = t.matmul(PB(3, 0, 8, 0, 8), conr[0:1, 0:8], r_probs[0:1, :],
                        start=True, stop=True)
        inst = t.matmul(PB(3, 8, 8, 0, 8), r_probs[0:1, :], conr[0:1, 0:8],
                        start=True, stop=True)
        inst.then_inc(r_pe, 1)
        t.wait_ge(r_dv, 5)
        inst = t.matmul(PB(4, 0, 1, 0, 1), r_wun[:, :], conc[0:8, 0:1],
                        start=True, stop=True)
        inst = t.matmul(PB(4, 1, 1, 0, 8), cup8[:, :], r_sel[:, :],
                        start=True, stop=True)
        inst.then_inc(r_pe, 1)
        t.wait_ge(r_dv, 6)
        inst = t.matmul(PB(5, 0, 3, 0, 1), r_wun[:, :], r_oh[:, :],
                        start=True, stop=True)
        inst = t.matmul(PB(5, 3, 3, 0, 1), ciop[0:8, 0:1], r_oh[:, :],
                        start=True, stop=True)
        inst.then_inc(r_pe, 1)
        t.wait_ge(r_dv, 7)
        inst = t.matmul(PB(6, 0, 3), conr[:, :], r_wsel[0:1, :],
                        start=True, stop=True)
        inst = t.matmul(PB(6, 3, 3), conr[:, :], r_id[0:1, :],
                        start=True, stop=True)
        inst.then_inc(r_pe, 1)

    @blk1.vector
    def _(v):
        v.wait_ge(s_cst, 16 * NCST)

        def AA(k):
            return (k % 2) * 4

        def ln(dst, src_x, pi, base, var_sem, var_val, ln_sem, u_sem):
            if var_val >= 3:
                v.wait_ge(ln_sem, var_val - 2)
            v.bn_stats(bnst[:, pi * 12:pi * 12 + 6], src_x[:, 0:512])
            v.bn_stats(bnst[:, pi * 12 + 6:pi * 12 + 12], src_x[:, 512:1024])
            v.drain()
            inst = v.bn_aggr(bnag[:, base:base + 2], bnst[:, pi * 12:pi * 12 + 12])
            v.drain()
            inst.then_inc(var_sem, 1)
            v.wait_ge(ln_sem, var_val)
            v.reciprocal(rstdt[:, pi:pi + 1], sqtt[:, pi:pi + 1])
            v.drain()
            inst = v.tensor_scalar(dst, src_x, bnag[:, base:base + 1],
                                   rstdt[:, pi:pi + 1], ALU.subtract, ALU.mult)
            inst.then_inc(u_sem, 1)

        def ln1(k):
            pi = k % 2
            v.wait_ge(s_x0 if pi == 0 else s_x1, 16 * (k // 2 + 1))
            ln(u1[:, pi * 1024:(pi + 1) * 1024], x_in[:, pi * 1024:(pi + 1) * 1024],
               pi, pi * 2, v_var1, k + 1, a_ln1, v_u1)

        def kv_evacs():
            v.wait_ge(t_kv, 1)
            v.tensor_copy(k16t[0:16, 0:512], PB(1, 0, 512, 0, 16))
            inst = v.tensor_copy(k16t[0:16, 512:1024], PB(2, 0, 512, 0, 16))
            inst.then_inc(v_kv, 1)
            v.wait_ge(t_kv, 2)
            v.tensor_copy(v16t[0:16, 0:512], PB(1, 0, 512, 0, 16))
            inst = v.tensor_copy(v16t[0:16, 512:1024], PB(2, 0, 512, 0, 16))
            inst.then_inc(v_kv, 1)
            v.wait_ge(t_kv, 3)
            inst = v.tensor_copy(k16f[:, :], PB(0, 0, 128))
            inst.then_inc(v_kv, 1)

        def softmax(k):
            pi = k % 2
            A = AA(k)
            v.wait_ge(t_lg, k + 1)
            lg_ap = bass.AP(ps_t, (A + 1) * 512, [[4096, 128], [16, 16], [1, 16]])
            ms_g = bass.AP(msub, pi * 256, [[512, 128], [16, 16], [1, 16]])
            mx_b = bass.AP(mxg, pi * 16, [[32, 128], [1, 16], [0, 16]])
            if k == 0:
                v.tensor_add(msub[:, pi * 256:(pi + 1) * 256],
                             PB(A + 1, 0, 256), cmask[:, :])
            else:
                v.tensor_copy(msub[:, pi * 256:(pi + 1) * 256], PB(A + 1, 0, 256))
            v.drain()
            v.reduce_max(mxg[:, pi * 16:(pi + 1) * 16], ms_g, AX.X)
            v.drain()
            inst = v.tensor_tensor(ms_g, ms_g, mx_b, ALU.subtract)
            inst.then_inc(v_sub, 1)
            v.wait_ge(a_exp, k + 1)
            ex_gr = bass.AP(exw, pi * 256, [[512, 128], [16, 16], [1, 16]])
            v.tensor_reduce(esum[:, pi * 16:(pi + 1) * 16], ex_gr, AX.X, ALU.add)
            v.drain()
            v.reciprocal(erc[:, pi * 16:(pi + 1) * 16], esum[:, pi * 16:(pi + 1) * 16])
            v.drain()
            er_b = bass.AP(erc, pi * 16, [[32, 128], [1, 16], [0, 16]])
            ex_g = bass.AP(exw, pi * 256, [[512, 128], [16, 16], [1, 16]])
            w_g = bass.AP(w_tm, pi * 256, [[512, 128], [16, 16], [1, 16]])
            inst = v.tensor_tensor(w_g, ex_g, er_b, ALU.mult)
            inst.then_inc(v_w, 1)

        def wT_evacs(k, wt_base):
            A = AA(k)
            for grp in range(2):
                v.wait_ge(t_wt, wt_base + grp + 1)
                for hh in range(8):
                    h = grp * 8 + hh
                    bank = A + 2 if hh < 4 else A
                    inst = v.tensor_copy(wTh[0:16, h * 128:(h + 1) * 128],
                                         PB(bank, (hh % 4) * 128, 128, 0, 16))
                inst.then_inc(v_wt, 1)

        def y_ln2(k):
            pi = k % 2
            A = AA(k)
            x_p = x_in[:, pi * 1024:(pi + 1) * 1024]
            y_p = y_t[:, pi * 1024:(pi + 1) * 1024]
            u2_p = u2[:, pi * 1024:(pi + 1) * 1024]
            v.wait_ge(t_proj, k + 1)
            if k >= 2:
                v.wait_ge(s_ys0 if pi == 0 else s_ys1, 16 * (k // 2))
            v.tensor_add(y_p[:, 0:512], x_p[:, 0:512], PB(A + 1, 0, 512))
            inst = v.tensor_add(y_p[:, 512:1024], x_p[:, 512:1024], PB(A + 2, 0, 512))
            inst.then_inc(v_y, 1)
            v.drain()
            ln(u2_p, y_p, pi, 4 + pi * 2, v_var2, k + 1, a_ln2, v_u2)
            if pool_short:
                v.wait_ge(t_pool, k + 1)
                if k == 0:
                    v.tensor_copy(pooled_acc[0:1, 0:512], PB(3, 0, 512, 0, 1))
                    inst = v.tensor_copy(pooled_acc[0:1, 512:1024], PB(7, 0, 512, 0, 1))
                else:
                    v.tensor_add(pooled_acc[0:1, 0:512], pooled_acc[0:1, 0:512],
                                 PB(3, 0, 512, 0, 1))
                    inst = v.tensor_add(pooled_acc[0:1, 512:1024],
                                        pooled_acc[0:1, 512:1024],
                                        PB(7, 0, 512, 0, 1))
                inst.then_inc(v_pool, 1)

        NR = NT if trunc is None else trunc
        if pipe:
            ln1(0)
            kv_evacs()
            for k in range(NR):
                if k + 1 < NR:
                    ln1(k + 1)
                softmax(k)
                wT_evacs(k, 2 * k)
                y_ln2(k)
        else:
            ln1(0)
            for k in range(NT):
                tr_evac(k)
                q_evac(k)
                if k == 0:
                    kv_evacs()
                softmax(k)
                if k + 1 < NT:
                    ln1(k + 1)
                wT_evacs(k, 2 * k)
                o_evac(k)
                y_ln2(k)

        if trunc is not None:
            return
        if pool_short:
            pass
        v.wait_ge(t_pool, NT)
        if pool_short:
            v.wait_ge(v_pool, NT)
            v.drain()
            v.tensor_scalar_mul(pool_tmp[0:1, :], pooled_acc[0:1, :], 1.0 / 8192.0)
            inst = v.drain()
        else:
            v.drain()
            v.tensor_scalar_mul(pool_tmp[0:1, 0:512], PB(3, 0, 512, 0, 1), 1.0 / 8192.0)
            inst = v.tensor_scalar_mul(pool_tmp[0:1, 512:1024], PB(7, 0, 512, 0, 1),
                                       1.0 / 8192.0)
            v.drain()
        inst.then_inc(r_dv, 1)
        v.wait_ge(r_pe, 1)
        inst = v.tensor_copy(r_pool_f[:, :], PB(0, 0, 8))
        inst.then_inc(r_dv, 1)
        v.wait_ge(r_pe, 3)
        v.tensor_add(r_logit[0:1, :], PB(2, 0, 8, 0, 1), b2r[0:1, :])
        v.drain()
        v.reduce_max(r_mx[0:1, :], r_logit[0:1, :], AX.X)
        v.drain()
        inst = v.tensor_scalar_mul(r_mxn[0:1, :], r_mx[0:1, :], -1.0)
        inst.then_inc(r_dv, 1)
        v.wait_ge(r_ac, 2)
        v.reciprocal(r_er[0:1, :], r_es[0:1, :])
        v.drain()
        v.tensor_scalar(r_probs[0:1, :], r_ex[0:1, :], r_er[0:1, 0:1], None, ALU.mult)
        inst = v.drain()
        inst.then_inc(r_dv, 1)
        v.wait_ge(r_pe, 4)
        v.tensor_copy(r_A[:, :], PB(3, 0, 8, 0, 8))
        v.tensor_copy(r_B[:, :], PB(3, 8, 8, 0, 8))
        v.drain()
        v.tensor_tensor(r_gt[:, :], r_A[:, :], r_B[:, :], ALU.is_gt)
        v.drain()
        v.tensor_tensor(r_A[:, :], r_A[:, :], r_B[:, :], ALU.is_equal)
        v.drain()
        v.tensor_mul(r_A[:, :], r_A[:, :], clow8[:, :])
        v.drain()
        v.tensor_add(r_gt[:, :], r_gt[:, :], r_A[:, :])
        v.drain()
        v.tensor_reduce(r_rank[:, :], r_gt[:, :], AX.X, ALU.add)
        v.drain()
        v.tensor_single_scalar(r_sel[:, :], r_rank[:, :], 3.0, ALU.is_lt)
        v.drain()
        v.tensor_mul(r_wun[:, :], r_sel[:, :], r_B[:, 0:1])
        inst = v.drain()
        inst.then_inc(r_dv, 1)
        v.wait_ge(r_pe, 5)
        v.reciprocal(r_rZ[0:1, :], PB(4, 0, 1, 0, 1))
        for s3 in range(3):
            v.tensor_single_scalar(r_oh[:, s3:s3 + 1], PB(4, 1, 1, 0, 8),
                                   float(s3), ALU.is_equal)
        v.drain()
        sel_b = bass.AP(r_sel, 0, [[1, 8], [0, 3]])
        v.tensor_tensor(r_oh[:, :], r_oh[:, :], sel_b, ALU.mult)
        inst = v.drain()
        inst.then_inc(r_dv, 1)
        v.wait_ge(r_pe, 6)
        v.tensor_scalar(r_wsel[0:1, :], PB(5, 0, 3, 0, 1), r_rZ[0:1, 0:1], None,
                        ALU.mult)
        v.tensor_copy(r_id[0:1, :], PB(5, 3, 3, 0, 1))
        inst = v.drain()
        inst.then_inc(r_dv, 1)
        v.wait_ge(r_pe, 7)
        v.tensor_copy(w_bc[:, :], PB(6, 0, 3))
        v.tensor_copy(e_bc[:, :], PB(6, 3, 3))
        v.drain()
        v.tensor_scalar_mul(e_bc[:, :], e_bc[:, :], 4096.0)
        v.drain()
        for s3 in range(3):
            v.tensor_scalar(idx_f[:, s3 * 32:(s3 + 1) * 32], ciofc[:, :],
                            e_bc[:, s3:s3 + 1], None, ALU.add)
        v.drain()
        v.tensor_copy(idx_i[:, :], idx_f[:, :])
        inst = v.drain()
        inst.then_inc(rdone, 1)

    @blk1.scalar
    def _(a):
        a.wait_ge(s_cst, 16 * NCST)
        NRs = NT if trunc is None else trunc

        def AAs(k):
            return (k % 2) * 4

        def sqrt1(k):
            pi = k % 2
            a.wait_ge(v_var1, k + 1)
            inst = a.activation(sqtt[:, pi:pi + 1], bnag[:, pi * 2 + 1:pi * 2 + 2],
                                AF.Sqrt, bias=ceps[:, 0:1])
            inst.then_inc(a_ln1, 1)

        def tr_evac(k):
            pi = k % 2
            A = AAs(k)
            u1f_p = u1f[:, pi * 1024:(pi + 1) * 1024]
            a.wait_ge(t_tr, k + 1)
            a.activation(u1f_p[:, 0:512], PB(A, 0, 512), AF.Copy)
            inst = a.activation(u1f_p[:, 512:1024], PB(A + 1, 0, 512), AF.Copy)
            inst.then_inc(v_tr, 1)

        def q_evac(k):
            pi = k % 2
            A = AAs(k)
            qf_p = qf[:, pi * 1024:(pi + 1) * 1024]
            a.wait_ge(t_q, k + 1)
            a.activation(qf_p[:, 0:512], PB(A + 2, 0, 512), AF.Copy)
            inst = a.activation(qf_p[:, 512:1024], PB(A, 0, 512), AF.Copy)
            inst.then_inc(v_q, 1)

        def o_evac(k):
            pi = k % 2
            A = AAs(k)
            of_p = o_f[:, pi * 1024:(pi + 1) * 1024]
            a.wait_ge(t_oh, 2 * (k + 1))
            a.activation(of_p[:, 0:512], PB(A + 1, 0, 512), AF.Copy)
            inst = a.activation(of_p[:, 512:1024], PB(A + 2, 0, 512), AF.Copy)
            inst.then_inc(v_o, 1)

        sqrt1(0)
        tr_evac(0)
        q_evac(0)
        for k in range(NRs):
            pi = k % 2
            if k + 1 < NRs:
                sqrt1(k + 1)
                tr_evac(k + 1)
                q_evac(k + 1)
            a.wait_ge(v_sub, k + 1)
            inst = a.activation(exw[:, pi * 256:(pi + 1) * 256],
                                msub[:, pi * 256:(pi + 1) * 256], AF.Exp,
                                scale=0.125)
            inst.then_inc(a_exp, 1)
            o_evac(k)
            a.wait_ge(v_var2, k + 1)
            inst = a.activation(sqtt[:, pi:pi + 1], bnag[:, 4 + pi * 2 + 1:4 + pi * 2 + 2],
                                AF.Sqrt, bias=ceps[:, 0:1])
            inst.then_inc(a_ln2, 1)

        if trunc is not None:
            return
        a.wait_ge(r_pe, 2)
        for fc2 in range(2):
            inst = a.activation(r_h[:, fc2:fc2 + 1], PB(1, fc2, 1), AF.Tanh,
                                bias=b1f[:, fc2:fc2 + 1])
        inst.then_inc(r_ac, 1)
        a.wait_ge(r_dv, 3)
        inst = a.activation(r_ex[0:1, :], r_logit[0:1, :], AF.Exp,
                            bias=r_mxn[0:1, 0:1], accum_out=r_es[0:1, 0:1])
        inst.then_inc(r_ac, 1)

    es1.close()

    es3 = contextlib.ExitStack()
    sb3 = lambda name, shape, dt=F32: es3.enter_context(nc.sbuf_tensor(name, shape, dt))
    u_fm3 = sb3("u_fm3", [128, 8192])
    acc3 = sb3("acc3", [128, 8192])
    wa_buf = sb3("wa_buf", [128, 8192])
    wb_buf = sb3("wb_buf", [128, 8192])
    g_sb = sb3("g_sb", [128, 512])
    u2t3 = sb3("u2t3", [128, 2048])

    blk2 = es3.enter_context(nc.Block("phase3"))

    @blk2.gpsimd
    def _(g):
        if trunc is not None:
            return
        g.wait_ge(rdone, 1)
        g.wait_ge(s_ys0, 16 * NT // 2)
        g.wait_ge(s_ys1, 16 * NT // 2)
        nblk = 0
        for p in range(NPASS):
            for r in range(TPP):
                gi = p * TPP + r
                pi = gi % 2
                if gi >= 2:
                    g.wait_ge(v_yb, gi - 1)
                g.indirect_dma_start(
                    x_in[:, pi * 1024:(pi + 1) * 1024], None,
                    yscr_d[:, :],
                    bass.IndirectOffsetOnAxis(cyidx[:, gi:gi + 1], 0),
                ).then_inc(s_yr0 if pi == 0 else s_yr1, 16)
            for s in range(3):
                for fcb in range(8):
                    pi = nblk % 2
                    if nblk >= 2:
                        g.wait_ge(t_o3, 16 * (nblk - 1))
                    for j in range(4):
                        fc = fcb * 4 + j
                        g.indirect_dma_start(
                            wa_buf[:, pi * 4096 + j * 1024: pi * 4096 + (j + 1) * 1024],
                            None, wa_d[:, :],
                            bass.IndirectOffsetOnAxis(
                                idx_i[:, s * 32 + fc:s * 32 + fc + 1], 0),
                        ).then_inc(s_ga0 if pi == 0 else s_ga1, 16)
                        g.indirect_dma_start(
                            wb_buf[:, pi * 4096 + j * 1024: pi * 4096 + (j + 1) * 1024],
                            None, wb_d[:, :],
                            bass.IndirectOffsetOnAxis(
                                idx_i[:, s * 32 + fc:s * 32 + fc + 1], 0),
                        ).then_inc(s_ga0 if pi == 0 else s_ga1, 16)
                    nblk += 1

    @blk2.sync
    def _(sync):
        if trunc is not None:
            return
        for p in range(NPASS):
            sync.wait_ge(v_ev, 192 * (p + 1))
            for r in range(TPP):
                sync.dma_start(
                    out_d[(p * TPP + r) * 128:(p * TPP + r + 1) * 128, :],
                    in_=acc3[:, r * 1024:(r + 1) * 1024]).then_inc(s_out, 16)
        sync.wait_ge(s_out, 16 * NPASS * TPP)

    @blk2.tensor
    def _(t):
        if trunc is not None:
            return
        blk_cnt = 0
        for p in range(NPASS):
            for r in range(TPP):
                gi = p * TPP + r
                pi = gi % 2
                t.wait_ge(v_u23, gi + 1)
                if gi >= 1:
                    t.wait_ge(v_tr3, 2 * gi)
                for ic in range(8):
                    bank = 3 if ic < 4 else 7
                    inst = t.transpose(PB(bank, (ic % 4) * 128, 128),
                                       u2t3[:, pi * 1024 + ic * 128: pi * 1024 + (ic + 1) * 128],
                                       cident[:, :])
                    if ic == 3:
                        inst.then_inc(t_tr3, 1)
                inst.then_inc(t_tr3, 1)
            t.wait_ge(v_tr3, 2 * TPP * (p + 1))
            for s in range(3):
                for fcb in range(8):
                    pi = blk_cnt % 2
                    t.wait_ge(s_ga0 if pi == 0 else s_ga1,
                              128 * (blk_cnt // 2 + 1))
                    for tq in range(4):
                        unit = ((p * 3 + s) * 8 + fcb) * 4 + tq
                        for j in range(4):
                            J = unit * 4 + j
                            zb = 0 if J % 2 == 0 else 4
                            if J >= 2:
                                t.wait_ge(a_g, J - 1)
                            for ic in range(8):
                                inst = t.matmul(
                                    PB(zb, 0, 256),
                                    wa_buf[:, pi * 4096 + j * 1024 + ic * 128: pi * 4096 + j * 1024 + (ic + 1) * 128],
                                    u_fm3[:, ic * 1024 + tq * 256: ic * 1024 + tq * 256 + 256],
                                    start=(ic == 0), stop=(ic == 7))
                            inst.then_inc(t_z, 1)
                            t.wait_ge(a_g, J + 1)
                            if j == 0 and unit >= 1:
                                t.wait_ge(v_ev, 2 * unit)
                            for th in range(2):
                                obase = 512 + th * 2048
                                for nh in range(2):
                                    inst = t.matmul(
                                        ps_t[:, obase + nh * 512: obase + (nh + 1) * 512],
                                        g_sb[:, (J % 2) * 256 + th * 128:(J % 2) * 256 + th * 128 + 128],
                                        wb_buf[:, pi * 4096 + j * 1024 + nh * 512: pi * 4096 + j * 1024 + (nh + 1) * 512],
                                        start=(j == 0), stop=(j == 3))
                            inst.then_inc(t_o3, 1)
                    blk_cnt += 1

    @blk2.vector
    def _(v):
        if trunc is not None:
            return
        for p in range(NPASS):
            if p >= 1:
                v.wait_ge(s_out, 128 * p)
            for r in range(TPP):
                gi = p * TPP + r
                pi = gi % 2
                y_p = x_in[:, pi * 1024:(pi + 1) * 1024]
                u23_p = u2t3[:, pi * 1024:(pi + 1) * 1024]
                v.wait_ge(s_yr0 if pi == 0 else s_yr1, 16 * (gi // 2 + 1))
                if gi >= 2:
                    v.wait_ge(a_ln3, gi - 1)
                v.bn_stats(bnst[:, pi * 12:pi * 12 + 6], y_p[:, 0:512])
                v.bn_stats(bnst[:, pi * 12 + 6:pi * 12 + 12], y_p[:, 512:1024])
                v.drain()
                inst = v.bn_aggr(bnag[:, pi * 2:pi * 2 + 2],
                                 bnst[:, pi * 12:pi * 12 + 12])
                v.drain()
                inst.then_inc(v_var3, 1)
                v.wait_ge(a_ln3, gi + 1)
                v.reciprocal(rstdt[:, pi:pi + 1], sqtt[:, pi:pi + 1])
                v.drain()
                inst = v.tensor_scalar(u23_p, y_p, bnag[:, pi * 2:pi * 2 + 1],
                                       rstdt[:, pi:pi + 1], ALU.subtract, ALU.mult)
                inst.then_inc(v_u23, 1)
                v.drain()
                inst = v.tensor_add(acc3[:, r * 1024:(r + 1) * 1024], y_p, u23_p)
                inst.then_inc(v_yb, 1)
                v.wait_ge(t_tr3, 2 * gi + 1)
                for ic in range(4):
                    v.tensor_copy(u_fm3[:, ic * 1024 + r * 128: ic * 1024 + (r + 1) * 128],
                                  PB(3, ic * 128, 128))
                v.wait_ge(t_tr3, 2 * gi + 2)
                for ic in range(4, 8):
                    inst = v.tensor_copy(
                        u_fm3[:, ic * 1024 + r * 128: ic * 1024 + (r + 1) * 128],
                        PB(7, (ic - 4) * 128, 128))
                inst.then_inc(v_tr3, 2)
            for s in range(3):
                for fcb in range(8):
                    for tq in range(4):
                        unit = ((p * 3 + s) * 8 + fcb) * 4 + tq
                        v.wait_ge(t_o3, 4 * (unit + 1))
                        for th in range(2):
                            obase = 512 + th * 2048
                            tile_local = tq * 2 + th
                            inst = v.scalar_tensor_tensor(
                                acc3[:, tile_local * 1024:(tile_local + 1) * 1024],
                                ps_t[:, obase:obase + 1024],
                                w_bc[:, s:s + 1],
                                acc3[:, tile_local * 1024:(tile_local + 1) * 1024],
                                ALU.mult, ALU.add)
                            inst.then_inc(v_ev, 1)

    @blk2.scalar
    def _(a):
        if trunc is not None:
            return
        for p in range(NPASS):
            for r in range(TPP):
                gi = p * TPP + r
                pi = gi % 2
                a.wait_ge(v_var3, gi + 1)
                inst = a.activation(sqtt[:, pi:pi + 1], bnag[:, pi * 2 + 1:pi * 2 + 2],
                                    AF.Sqrt, bias=ceps[:, 0:1])
                inst.then_inc(a_ln3, 1)
            for s in range(3):
                for fcb in range(8):
                    for tq in range(4):
                        unit = ((p * 3 + s) * 8 + fcb) * 4 + tq
                        for j in range(4):
                            J = unit * 4 + j
                            zb = 0 if J % 2 == 0 else 4
                            a.wait_ge(t_z, J + 1)
                            if J >= 2:
                                a.wait_ge(t_o3, J - 1)
                            inst = a.activation(
                                g_sb[:, (J % 2) * 256:(J % 2 + 1) * 256],
                                PB(zb, 0, 256), AF.Gelu_apprx_tanh)
                            inst.then_inc(a_g, 1)

    es3.close()
    es.close()
    return nc


def prep_inputs(inputs):
    W_attn = np.asarray(inputs["L0_attn_W_attn"], dtype=np.float32)
    W_proj = np.asarray(inputs["L0_attn_W_proj"], dtype=np.float32)
    W1 = np.asarray(inputs["L0_ffn_W1"], dtype=np.float32)
    b1 = np.asarray(inputs["L0_ffn_b1"], dtype=np.float32)
    W2 = np.asarray(inputs["L0_ffn_W2"], dtype=np.float32)
    b2 = np.asarray(inputs["L0_ffn_b2"], dtype=np.float32)
    Wa = np.asarray(inputs["L0_ffn_Wa"], dtype=np.float32)
    Wb = np.asarray(inputs["L0_ffn_Wb"], dtype=np.float32)

    d = {}
    d["wq_t"] = np.ascontiguousarray(
        W_attn[0:1024].T.reshape(8, 128, 8, 128).transpose(1, 0, 2, 3).reshape(128, 8192))
    d["wk_t"] = np.ascontiguousarray(
        W_attn[1024:2048].T.reshape(8, 128, 1024).transpose(1, 0, 2).reshape(128, 8192))
    d["wv_t"] = np.ascontiguousarray(
        W_attn[2048:3072].T.reshape(8, 128, 1024).transpose(1, 0, 2).reshape(128, 8192))
    d["wp_t"] = np.ascontiguousarray(
        W_proj.T.reshape(8, 128, 1024).transpose(1, 0, 2).reshape(128, 8192))
    d["w1_t"] = np.ascontiguousarray(
        W1.reshape(8, 128, 2, 128).transpose(1, 0, 2, 3).reshape(128, 2048))
    d["b1_t"] = np.ascontiguousarray(b1.reshape(2, 128).T)
    d["w2_t"] = np.ascontiguousarray(
        W2.reshape(2, 128, 8).transpose(1, 0, 2).reshape(128, 16))
    d["b2_t"] = np.ascontiguousarray(b2.reshape(1, 8))
    d["wa_r"] = np.ascontiguousarray(
        Wa.reshape(8, 8, 128, 32, 128).transpose(0, 3, 2, 1, 4).reshape(32768, 1024))
    d["wb_r"] = np.ascontiguousarray(Wb.reshape(32768, 1024))
    d["cident"] = np.eye(128, dtype=np.float32)
    t_ar = np.arange(128)
    j_ar = np.arange(16)
    m = np.where(j_ar[None, :] <= t_ar[:, None], 0.0, NEG).astype(np.float32)
    d["cmask"] = np.ascontiguousarray(np.tile(m, (1, 16)))
    p_ar = np.arange(128, dtype=np.float32)
    d["ciota_fc"] = np.ascontiguousarray(
        (p_ar[:, None] + 128.0 * np.arange(32, dtype=np.float32)[None, :]).astype(np.float32))
    d["cones_r"] = np.ones((1, 128), dtype=np.float32)
    d["cones_c"] = np.ones((128, 1), dtype=np.float32)
    i8 = np.arange(8)
    d["clow8"] = (i8[None, :] < i8[:, None]).astype(np.float32)
    d["cup8"] = (i8[:, None] < i8[None, :]).astype(np.float32)
    d["ciota_p"] = p_ar.reshape(128, 1).astype(np.float32)
    d["ceps"] = np.full((128, 1), EPS, dtype=np.float32)
    return d


def core_inputs(inputs, shared, c):
    x = np.asarray(inputs["L0_attn_x"], dtype=np.float32)
    b, half = c // 2, c % 2
    m = dict(shared)
    m["x"] = np.ascontiguousarray(x[b].reshape(8192, 1024))
    base = half * 4096
    rows = (base + np.arange(4096)).reshape(32, 128).T
    m["cyidx"] = np.ascontiguousarray(rows.astype(np.int32))
    return m


from concourse.bass_utils import run_bass_kernel_spmd


def kernel(**inputs):
    shared = prep_inputs(inputs)
    in_maps = [core_inputs(inputs, shared, c) for c in range(8)]
    nc = build_nc()
    res = run_bass_kernel_spmd(nc, in_maps, core_ids=list(range(8)))
    out = np.empty((4, 8192, 1024), dtype=np.float32)
    for c in range(8):
        b, half = c // 2, c % 2
        out[b, half * 4096:(half + 1) * 4096, :] = res.results[c]["out"]
    return out


# version 06
# speedup vs baseline: 1.0407x; 1.0407x over previous
import contextlib
import numpy as np
import concourse.bass as bass
import concourse.mybir as mybir

F32 = mybir.dt.float32
I32 = mybir.dt.int32
AF = mybir.ActivationFunctionType
ALU = mybir.AluOpType
AX = mybir.AxisListType

NT = 64
NPASS = 4
TPP = 8
EPS = 1e-5
NEG = -1.0e30


def build_nc(debug=False, pipe=True, pool_short=False, trunc=None):
    nc = bass.Bass("TRN2", target_bir_lowering=False)

    x_d = nc.dram_tensor("x", [8192, 1024], F32, kind="ExternalInput").ap()
    wq_d = nc.dram_tensor("wq_t", [128, 8192], F32, kind="ExternalInput").ap()
    wk_d = nc.dram_tensor("wk_t", [128, 8192], F32, kind="ExternalInput").ap()
    wv_d = nc.dram_tensor("wv_t", [128, 8192], F32, kind="ExternalInput").ap()
    wp_d = nc.dram_tensor("wp_t", [128, 8192], F32, kind="ExternalInput").ap()
    w1_d = nc.dram_tensor("w1_t", [128, 2048], F32, kind="ExternalInput").ap()
    b1_d = nc.dram_tensor("b1_t", [128, 2], F32, kind="ExternalInput").ap()
    w2_d = nc.dram_tensor("w2_t", [128, 16], F32, kind="ExternalInput").ap()
    b2_d = nc.dram_tensor("b2_t", [1, 8], F32, kind="ExternalInput").ap()
    wa_d = nc.dram_tensor("wa_r", [32768, 1024], F32, kind="ExternalInput").ap()
    wb_d = nc.dram_tensor("wb_r", [32768, 1024], F32, kind="ExternalInput").ap()
    cid_d = nc.dram_tensor("cident", [128, 128], F32, kind="ExternalInput").ap()
    cmask_d = nc.dram_tensor("cmask", [128, 256], F32, kind="ExternalInput").ap()
    ciofc_d = nc.dram_tensor("ciota_fc", [128, 32], F32, kind="ExternalInput").ap()
    conr_d = nc.dram_tensor("cones_r", [1, 128], F32, kind="ExternalInput").ap()
    conc_d = nc.dram_tensor("cones_c", [128, 1], F32, kind="ExternalInput").ap()
    clow_d = nc.dram_tensor("clow8", [8, 8], F32, kind="ExternalInput").ap()
    cup_d = nc.dram_tensor("cup8", [8, 8], F32, kind="ExternalInput").ap()
    ciop_d = nc.dram_tensor("ciota_p", [128, 1], F32, kind="ExternalInput").ap()
    cyidx_d = nc.dram_tensor("cyidx", [128, 32], I32, kind="ExternalInput").ap()
    ceps_d = nc.dram_tensor("ceps", [128, 1], F32, kind="ExternalInput").ap()

    yscr_d = nc.dram_tensor("y_scr", [8192, 1024], F32, kind="Internal").ap()
    out_d = nc.dram_tensor("out", [4096, 1024], F32, kind="ExternalOutput").ap()
    if debug:
        dbg_d = nc.dram_tensor("dbg", [128, 16], F32, kind="ExternalOutput").ap()

    es = contextlib.ExitStack()
    sb = lambda name, shape, dt=F32: es.enter_context(nc.sbuf_tensor(name, shape, dt))
    sem = lambda name: es.enter_context(nc.semaphore(name))

    cident = sb("s_cident", [128, 128])
    cmask = sb("s_cmask", [128, 256])
    ciofc = sb("s_ciota_fc", [128, 32])
    conr = sb("s_cones_r", [1, 128])
    conc = sb("s_cones_c", [128, 1])
    clow8 = sb("s_clow8", [8, 8])
    cup8 = sb("s_cup8", [8, 8])
    ciop = sb("s_ciota_p", [128, 1])
    cyidx = sb("s_cyidx", [128, 32], I32)
    ceps = sb("s_ceps", [128, 1])
    b2r = sb("s_b2r", [1, 8])
    b1f = sb("s_b1f", [128, 2])
    w1f = sb("s_w1f", [128, 2048])
    w2f = sb("s_w2f", [128, 16])

    x_in = sb("x_in", [128, 4096])
    bnst = sb("bnst", [128, 24])
    bnag = sb("bnag", [128, 8])
    rstdt = sb("rstdt", [128, 4])
    sqtt = sb("sqtt", [128, 4])
    pooled_acc = sb("pooled_acc", [1, 1024])
    pool_tmp = sb("pool_tmp", [1, 1024])

    r_pool_f = sb("r_pool_f", [128, 8])
    r_h = sb("r_h", [128, 2])
    r_logit = sb("r_logit", [1, 8])
    r_mx = sb("r_mx", [1, 1])
    r_mxn = sb("r_mxn", [1, 1])
    r_ex = sb("r_ex", [1, 8])
    r_es = sb("r_es", [1, 1])
    r_er = sb("r_er", [1, 1])
    r_probs = sb("r_probs", [1, 8])
    r_A = sb("r_A", [8, 8])
    r_B = sb("r_B", [8, 8])
    r_gt = sb("r_gt", [8, 8])
    r_rank = sb("r_rank", [8, 1])
    r_sel = sb("r_sel", [8, 1])
    r_wun = sb("r_wun", [8, 1])
    r_rZ = sb("r_rZ", [1, 1])
    r_oh = sb("r_oh", [8, 3])
    r_wsel = sb("r_wsel", [1, 3])
    r_id = sb("r_id", [1, 3])
    w_bc = sb("w_bc", [128, 3])
    e_bc = sb("e_bc", [128, 3])
    idx_f = sb("idx_f", [128, 96])
    idx_i = sb("idx_i", [128, 96], I32)

    ps_t = es.enter_context(nc.psum_tensor("ps", [128, 4096], F32))

    def PB(bank, lo=0, n=512, p0=0, np_=128):
        return ps_t[p0:p0 + np_, bank * 512 + lo: bank * 512 + lo + n]

    s_xs = [sem(f"s_x{i}") for i in range(4)]
    s_wq, s_wkv, s_wp, s_cst = sem("s_wq"), sem("s_wkv"), sem("s_wp"), sem("s_cst")
    s_ys0, s_ys1 = sem("s_ys0"), sem("s_ys1")
    s_yr0, s_yr1 = sem("s_yr0"), sem("s_yr1")
    s_ga0, s_ga1 = sem("s_ga0"), sem("s_ga1")
    s_out = sem("s_out")

    a_sq = sem("a_sq")
    v_var1, a_ln1, v_u1 = sem("v_var1"), sem("a_ln1"), sem("v_u1")
    t_tr, v_tr = sem("t_tr"), sem("v_tr")
    t_q, v_q = sem("t_q"), sem("v_q")
    t_kv, v_kv = sem("t_kv"), sem("v_kv")
    t_lg = sem("t_lg")
    v_sub, a_exp, v_w = sem("v_sub"), sem("a_exp"), sem("v_w")
    t_wt, v_wt, t_oh = sem("t_wt"), sem("v_wt"), sem("t_oh")
    t_o, v_o = sem("t_o"), sem("v_o")
    t_proj, v_y = sem("t_proj"), sem("v_y")
    v_var2, a_ln2, v_u2 = sem("v_var2"), sem("a_ln2"), sem("v_u2")
    t_pool, v_pool = sem("t_pool"), sem("v_pool")
    r_pe, r_dv, r_ac, rdone = sem("r_pe"), sem("r_dv"), sem("r_ac"), sem("rdone")
    v_yb = sem("v_yb")
    v_u23, t_tr3, v_tr3 = sem("v_u23"), sem("t_tr3"), sem("v_tr3")
    a_sq3, v_var3, a_ln3 = sem("a_sq3"), sem("v_var3"), sem("a_ln3")
    t_z, a_g, t_o3, v_ev = sem("t_z"), sem("a_g"), sem("t_o3"), sem("v_ev")
    t_wfree = sem("t_wfree")

    NCST = 14

    def BK(k, i):
        return (k % 2) * 4 + i

    es1 = contextlib.ExitStack()
    sb1 = lambda name, shape, dt=F32: es1.enter_context(nc.sbuf_tensor(name, shape, dt))
    wq_s = sb1("wq_s", [128, 8192])
    wp_s = sb1("wp_s", [128, 8192])
    wkv_s = sb1("wkv_s", [128, 4096])
    u1 = sb1("u1", [128, 4096])
    u1f = sb1("u1f", [128, 2048])
    qf = sb1("qf", [128, 2048])
    k16f = sb1("k16f", [128, 128])
    k16t = sb1("k16t", [16, 1024])
    v16t = sb1("v16t", [16, 1024])
    msub = sb1("msub", [128, 512])
    mxg = sb1("mxg", [128, 32])
    exw = sb1("exw", [128, 512])
    esum = sb1("esum", [128, 32])
    erc = sb1("erc", [128, 32])
    w_tm = sb1("w_tm", [128, 512])
    wTh = sb1("wTh", [16, 2048])
    o_f = sb1("o_f", [128, 2048])
    y_t = sb1("y_t", [128, 2048])
    u2 = sb1("u2", [128, 2048])

    blk1 = es1.enter_context(nc.Block("phase1"))

    @blk1.sync
    def _(sync):
        for dst, src in [(cident, cid_d), (cmask, cmask_d), (ciofc, ciofc_d),
                         (conr, conr_d), (conc, conc_d), (clow8, clow_d),
                         (cup8, cup_d), (ciop, ciop_d), (cyidx, cyidx_d),
                         (ceps, ceps_d),
                         (b2r, b2_d), (b1f, b1_d), (w1f, w1_d), (w2f, w2_d)]:
            sync.dma_start(out=dst[:, :], in_=src[:, :]).then_inc(s_cst, 16)
        sync.dma_start(out=x_in[:, 0:1024], in_=x_d[0:128, :]).then_inc(s_xs[0], 16)
        sync.dma_start(out=x_in[:, 1024:2048], in_=x_d[128:256, :]).then_inc(s_xs[1], 16)
        sync.dma_start(out=wq_s[:, :], in_=wq_d[:, :]).then_inc(s_wq, 16)
        sync.dma_start(out=wkv_s[:, :], in_=wk_d[:, 0:4096]).then_inc(s_wkv, 16)
        sync.dma_start(out=wp_s[:, :], in_=wp_d[:, :]).then_inc(s_wp, 16)
        sync.wait_ge(t_kv, 1)
        sync.dma_start(out=wkv_s[:, :], in_=wk_d[:, 4096:8192]).then_inc(s_wkv, 16)
        sync.wait_ge(t_kv, 2)
        sync.dma_start(out=wkv_s[:, :], in_=wv_d[:, 0:4096]).then_inc(s_wkv, 16)
        sync.wait_ge(t_kv, 3)
        sync.dma_start(out=wkv_s[:, :], in_=wv_d[:, 4096:8192]).then_inc(s_wkv, 16)
        for k in range(2, NT if trunc is None else trunc):
            sl = k % 4
            if k >= 4:
                sync.wait_ge(v_y, k - 3)
            sync.dma_start(out=x_in[:, sl * 1024:(sl + 1) * 1024],
                           in_=x_d[k * 128:(k + 1) * 128, :]).then_inc(
                               s_xs[sl], 16)
        if trunc is not None:
            sync.wait_ge(v_y, trunc)
            sync.dma_start(out=out_d[0:128, :],
                           in_=y_t[:, 0:1024]).then_inc(s_out, 16)
            sync.wait_ge(s_out, 16)

    @blk1.gpsimd
    def _(g):
        for k in range(NT if trunc is None else trunc):
            pi = k % 2
            g.wait_ge(v_y, k + 1)
            g.dma_start(yscr_d[k * 128:(k + 1) * 128, :],
                        y_t[:, pi * 1024:(pi + 1) * 1024]).then_inc(
                            s_ys0 if pi == 0 else s_ys1, 16)

    @blk1.tensor
    def _(t):
        t.wait_ge(s_cst, 16 * NCST)

        def AA(k):
            return (k % 2) * 4

        def tr_stage(k):
            pi = k % 2
            A = AA(k)
            u1_p = u1[:, (k % 4) * 1024:(k % 4 + 1) * 1024]
            t.wait_ge(v_u1, k + 1)
            if k >= 2:
                t.wait_ge(v_y, k - 1)
            for ic in range(8):
                bank = A if ic < 4 else A + 1
                inst = t.transpose(PB(bank, (ic % 4) * 128, 128),
                                   u1_p[:, ic * 128:(ic + 1) * 128], cident[:, :])
            inst.then_inc(t_tr, 1)

        def q_stage(k):
            pi = k % 2
            A = AA(k)
            u1f_p = u1f[:, pi * 1024:(pi + 1) * 1024]
            if k == 0:
                t.wait_ge(s_wq, 16)
            t.wait_ge(v_tr, k + 1)
            if k >= 2:
                t.wait_ge(v_y, k - 1)
            for oc in range(8):
                bank = A + 2 if oc < 4 else A
                lo = (oc % 4) * 128
                for ic in range(8):
                    inst = t.matmul(PB(bank, lo, 128),
                                    wq_s[:, ic * 1024 + oc * 128: ic * 1024 + (oc + 1) * 128],
                                    u1f_p[:, ic * 128:(ic + 1) * 128],
                                    start=(ic == 0), stop=(ic == 7))
            inst.then_inc(t_q, 1)

        def kv_block():
            t.wait_ge(v_q, 1)
            for part in range(4):
                t.wait_ge(s_wkv, 16 * (part + 1))
                if part == 2:
                    t.wait_ge(v_kv, 1)
                for half in range(2):
                    for icl in range(4):
                        ic = (part % 2) * 4 + icl
                        inst = t.matmul(PB(1 + half, 0, 512, 0, 16),
                                        u1f[:, ic * 128: ic * 128 + 16],
                                        wkv_s[:, icl * 1024 + half * 512: icl * 1024 + (half + 1) * 512],
                                        start=(ic == 0), stop=(ic == 7))
                inst.then_inc(t_kv, 1)
            t.wait_ge(v_kv, 2)
            t.wait_ge(v_tr, 1)
            for ic in range(8):
                inst = t.transpose(PB(0, ic * 16, 16),
                                   k16t[0:16, ic * 128:(ic + 1) * 128],
                                   cident[0:16, 0:16])
            inst.then_inc(t_kv, 1)
            t.wait_ge(v_kv, 3)

        def logits_stage(k):
            pi = k % 2
            A = AA(k)
            qf_p = qf[:, pi * 1024:(pi + 1) * 1024]
            t.wait_ge(v_q, k + 1)
            for h in range(16):
                po = (h % 2) * 64
                ch = h // 2
                inst = t.matmul(PB(A + 1, h * 16, 16),
                                qf_p[po:po + 64, ch * 128:(ch + 1) * 128],
                                k16f[po:po + 64, ch * 16:(ch + 1) * 16],
                                start=True, stop=True)
            inst.then_inc(t_lg, 1)

        def pool_stage(k):
            pi = k % 2
            u2_pp = u2[:, pi * 1024:(pi + 1) * 1024]
            t.wait_ge(v_u2, k + 1)
            if pool_short:
                t.wait_ge(v_pool, k)
            st_ = (k == 0) if not pool_short else True
            sp_ = (k == NT - 1) if not pool_short else True
            inst = t.matmul(PB(3, 0, 512, 0, 1), conc[:, :], u2_pp[:, 0:512],
                            start=st_, stop=sp_)
            inst = t.matmul(PB(7, 0, 512, 0, 1), conc[:, :], u2_pp[:, 512:1024],
                            start=st_, stop=sp_)
            inst.then_inc(t_pool, 1)

        def wT_o_proj_stage(k, wt_base):
            pi = k % 2
            A = AA(k)
            wtm_p = w_tm[:, pi * 256:(pi + 1) * 256]
            of_p = o_f[:, pi * 1024:(pi + 1) * 1024]
            t.wait_ge(v_w, k + 1)
            t.wait_ge(v_sub, k + 1)
            for hh in range(8):
                bank = A + 2 if hh < 4 else A
                inst = t.transpose(PB(bank, (hh % 4) * 128, 128, 0, 16),
                                   wtm_p[:, hh * 16:(hh + 1) * 16], cident[:, :])
            inst.then_inc(t_wt, 1)
            if k >= 1:
                pool_stage(k - 1)
            t.wait_ge(v_wt, wt_base + 1)
            for hh in range(8):
                h = 8 + hh
                bank = A + 2 if hh < 4 else A
                inst = t.transpose(PB(bank, (hh % 4) * 128, 128, 0, 16),
                                   wtm_p[:, h * 16:(h + 1) * 16], cident[:, :])
            inst.then_inc(t_wt, 1)
            for h in range(8):
                p8 = h // 2
                inst = t.matmul(PB(A + 1, p8 * 128, 128, (h % 2) * 64, 64),
                                v16t[0:16, h * 64:(h + 1) * 64],
                                wTh[0:16, h * 128:(h + 1) * 128],
                                start=True, stop=True)
            inst.then_inc(t_oh, 1)
            t.wait_ge(v_wt, wt_base + 2)
            for h in range(8, 16):
                p8 = h // 2 - 4
                inst = t.matmul(PB(A + 2, p8 * 128, 128, (h % 2) * 64, 64),
                                v16t[0:16, h * 64:(h + 1) * 64],
                                wTh[0:16, h * 128:(h + 1) * 128],
                                start=True, stop=True)
            inst.then_inc(t_oh, 1)
            if k == 0:
                t.wait_ge(s_wp, 16)
            t.wait_ge(v_o, k + 1)
            for half in range(2):
                bank = A + 1 if half == 0 else A + 2
                for ic in range(8):
                    inst = t.matmul(PB(bank, 0, 512),
                                    of_p[:, ic * 128:(ic + 1) * 128],
                                    wp_s[:, ic * 1024 + half * 512: ic * 1024 + (half + 1) * 512],
                                    start=(ic == 0), stop=(ic == 7))
            inst.then_inc(t_proj, 1)

        NR = NT if trunc is None else trunc
        if pipe:
            tr_stage(0)
            q_stage(0)
            kv_block()
            logits_stage(0)
            for k in range(NR):
                if k + 1 < NR:
                    tr_stage(k + 1)
                    q_stage(k + 1)
                    logits_stage(k + 1)
                wT_o_proj_stage(k, 2 * k)
            if trunc is None:
                pool_stage(NT - 1)
        else:
            for k in range(NT):
                tr_stage(k)
                q_stage(k)
                if k == 0:
                    kv_block()
                logits_stage(k)
                wT_o_proj_stage(k, 2 * k)
            pool_stage(NT - 1)

        if trunc is not None:
            return
        t.wait_ge(r_dv, 1)
        for ic in range(8):
            inst = t.matmul(PB(0, ic, 1), pool_tmp[0:1, ic * 128:(ic + 1) * 128],
                            conc[0:1, 0:1], start=True, stop=True)
        inst.then_inc(r_pe, 1)
        t.wait_ge(r_dv, 2)
        for fc2 in range(2):
            for ic in range(8):
                inst = t.matmul(PB(1, fc2, 1),
                                w1f[:, ic * 256 + fc2 * 128: ic * 256 + (fc2 + 1) * 128],
                                r_pool_f[:, ic:ic + 1],
                                start=(ic == 0), stop=(ic == 7))
        inst.then_inc(r_pe, 1)
        t.wait_ge(r_ac, 1)
        for fc2 in range(2):
            inst = t.matmul(PB(2, 0, 8, 0, 1), r_h[:, fc2:fc2 + 1],
                            w2f[:, fc2 * 8:(fc2 + 1) * 8],
                            start=(fc2 == 0), stop=(fc2 == 1))
        inst.then_inc(r_pe, 1)
        t.wait_ge(r_dv, 4)
        inst = t.matmul(PB(3, 0, 8, 0, 8), conr[0:1, 0:8], r_probs[0:1, :],
                        start=True, stop=True)
        inst = t.matmul(PB(3, 8, 8, 0, 8), r_probs[0:1, :], conr[0:1, 0:8],
                        start=True, stop=True)
        inst.then_inc(r_pe, 1)
        t.wait_ge(r_dv, 5)
        inst = t.matmul(PB(4, 0, 1, 0, 1), r_wun[:, :], conc[0:8, 0:1],
                        start=True, stop=True)
        inst = t.matmul(PB(4, 1, 1, 0, 8), cup8[:, :], r_sel[:, :],
                        start=True, stop=True)
        inst.then_inc(r_pe, 1)
        t.wait_ge(r_dv, 6)
        inst = t.matmul(PB(5, 0, 3, 0, 1), r_wun[:, :], r_oh[:, :],
                        start=True, stop=True)
        inst = t.matmul(PB(5, 3, 3, 0, 1), ciop[0:8, 0:1], r_oh[:, :],
                        start=True, stop=True)
        inst.then_inc(r_pe, 1)
        t.wait_ge(r_dv, 7)
        inst = t.matmul(PB(6, 0, 3), conr[:, :], r_wsel[0:1, :],
                        start=True, stop=True)
        inst = t.matmul(PB(6, 3, 3), conr[:, :], r_id[0:1, :],
                        start=True, stop=True)
        inst.then_inc(r_pe, 1)

    @blk1.vector
    def _(v):
        v.wait_ge(s_cst, 16 * NCST)

        def AA(k):
            return (k % 2) * 4

        def ln(dst, src_x, pi, base, sq, var_sem, var_val, ln_sem, u_sem):
            if var_val >= 3:
                v.wait_ge(ln_sem, var_val - 2)
            v.bn_stats(bnst[:, pi * 12:pi * 12 + 6], src_x[:, 0:512])
            v.bn_stats(bnst[:, pi * 12 + 6:pi * 12 + 12], src_x[:, 512:1024])
            v.drain()
            inst = v.bn_aggr(bnag[:, base:base + 2], bnst[:, pi * 12:pi * 12 + 12])
            v.drain()
            inst.then_inc(var_sem, 1)
            v.wait_ge(ln_sem, var_val)
            v.reciprocal(rstdt[:, sq:sq + 1], sqtt[:, sq:sq + 1])
            v.drain()
            inst = v.tensor_scalar(dst, src_x, bnag[:, base:base + 1],
                                   rstdt[:, sq:sq + 1], ALU.subtract, ALU.mult)
            inst.then_inc(u_sem, 1)

        def ln1(k):
            pi = k % 2
            v.wait_ge(s_xs[k % 4], 16 * (k // 4 + 1))
            ln(u1[:, (k % 4) * 1024:(k % 4 + 1) * 1024],
               x_in[:, (k % 4) * 1024:(k % 4 + 1) * 1024],
               pi, pi * 2, pi, v_var1, k + 1, a_ln1, v_u1)

        def kv_evacs():
            v.wait_ge(t_kv, 2)
            v.tensor_copy(k16t[0:16, 0:512], PB(1, 0, 512, 0, 16))
            inst = v.tensor_copy(k16t[0:16, 512:1024], PB(2, 0, 512, 0, 16))
            inst.then_inc(v_kv, 1)
            v.wait_ge(t_kv, 4)
            v.tensor_copy(v16t[0:16, 0:512], PB(1, 0, 512, 0, 16))
            inst = v.tensor_copy(v16t[0:16, 512:1024], PB(2, 0, 512, 0, 16))
            inst.then_inc(v_kv, 1)
            v.wait_ge(t_kv, 5)
            inst = v.tensor_copy(k16f[:, :], PB(0, 0, 128))
            inst.then_inc(v_kv, 1)

        def softmax(k):
            pi = k % 2
            A = AA(k)
            v.wait_ge(t_lg, k + 1)
            lg_ap = bass.AP(ps_t, (A + 1) * 512, [[4096, 128], [16, 16], [1, 16]])
            ms_g = bass.AP(msub, pi * 256, [[512, 128], [16, 16], [1, 16]])
            mx_b = bass.AP(mxg, pi * 16, [[32, 128], [1, 16], [0, 16]])
            if k == 0:
                v.tensor_add(msub[:, pi * 256:(pi + 1) * 256],
                             PB(A + 1, 0, 256), cmask[:, :])
            else:
                v.tensor_copy(msub[:, pi * 256:(pi + 1) * 256], PB(A + 1, 0, 256))
            v.drain()
            v.reduce_max(mxg[:, pi * 16:(pi + 1) * 16], ms_g, AX.X)
            v.drain()
            inst = v.tensor_tensor(ms_g, ms_g, mx_b, ALU.subtract)
            inst.then_inc(v_sub, 1)
            v.wait_ge(a_exp, k + 1)
            ex_gr = bass.AP(exw, pi * 256, [[512, 128], [16, 16], [1, 16]])
            v.tensor_reduce(esum[:, pi * 16:(pi + 1) * 16], ex_gr, AX.X, ALU.add)
            v.drain()
            v.reciprocal(erc[:, pi * 16:(pi + 1) * 16], esum[:, pi * 16:(pi + 1) * 16])
            v.drain()
            er_b = bass.AP(erc, pi * 16, [[32, 128], [1, 16], [0, 16]])
            ex_g = bass.AP(exw, pi * 256, [[512, 128], [16, 16], [1, 16]])
            w_g = bass.AP(w_tm, pi * 256, [[512, 128], [16, 16], [1, 16]])
            inst = v.tensor_tensor(w_g, ex_g, er_b, ALU.mult)
            inst.then_inc(v_w, 1)

        def wT_evacs(k, wt_base):
            A = AA(k)
            for grp in range(2):
                v.wait_ge(t_wt, wt_base + grp + 1)
                for hh in range(8):
                    h = grp * 8 + hh
                    bank = A + 2 if hh < 4 else A
                    inst = v.tensor_copy(wTh[0:16, h * 128:(h + 1) * 128],
                                         PB(bank, (hh % 4) * 128, 128, 0, 16))
                inst.then_inc(v_wt, 1)

        def y_ln2(k):
            pi = k % 2
            A = AA(k)
            x_p = x_in[:, (k % 4) * 1024:(k % 4 + 1) * 1024]
            y_p = y_t[:, pi * 1024:(pi + 1) * 1024]
            u2_p = u2[:, pi * 1024:(pi + 1) * 1024]
            v.wait_ge(t_proj, k + 1)
            if k >= 2:
                v.wait_ge(s_ys0 if pi == 0 else s_ys1, 16 * (k // 2))
            v.tensor_add(y_p[:, 0:512], x_p[:, 0:512], PB(A + 1, 0, 512))
            inst = v.tensor_add(y_p[:, 512:1024], x_p[:, 512:1024], PB(A + 2, 0, 512))
            inst.then_inc(v_y, 1)
            v.drain()
            ln(u2_p, y_p, pi, 4 + pi * 2, 2 + pi, v_var2, k + 1, a_ln2, v_u2)
            if pool_short:
                v.wait_ge(t_pool, k + 1)
                if k == 0:
                    v.tensor_copy(pooled_acc[0:1, 0:512], PB(3, 0, 512, 0, 1))
                    inst = v.tensor_copy(pooled_acc[0:1, 512:1024], PB(7, 0, 512, 0, 1))
                else:
                    v.tensor_add(pooled_acc[0:1, 0:512], pooled_acc[0:1, 0:512],
                                 PB(3, 0, 512, 0, 1))
                    inst = v.tensor_add(pooled_acc[0:1, 512:1024],
                                        pooled_acc[0:1, 512:1024],
                                        PB(7, 0, 512, 0, 1))
                inst.then_inc(v_pool, 1)

        NR = NT if trunc is None else trunc
        if pipe:
            ln1(0)
            ln1(1)
            kv_evacs()
            for k in range(NR):
                if k + 2 < NR:
                    ln1(k + 2)
                softmax(k)
                wT_evacs(k, 2 * k)
                y_ln2(k)
        else:
            ln1(0)
            for k in range(NT):
                tr_evac(k)
                q_evac(k)
                if k == 0:
                    kv_evacs()
                softmax(k)
                if k + 1 < NT:
                    ln1(k + 1)
                wT_evacs(k, 2 * k)
                o_evac(k)
                y_ln2(k)

        if trunc is not None:
            return
        if pool_short:
            pass
        v.wait_ge(t_pool, NT)
        if pool_short:
            v.wait_ge(v_pool, NT)
            v.drain()
            v.tensor_scalar_mul(pool_tmp[0:1, :], pooled_acc[0:1, :], 1.0 / 8192.0)
            inst = v.drain()
        else:
            v.drain()
            v.tensor_scalar_mul(pool_tmp[0:1, 0:512], PB(3, 0, 512, 0, 1), 1.0 / 8192.0)
            inst = v.tensor_scalar_mul(pool_tmp[0:1, 512:1024], PB(7, 0, 512, 0, 1),
                                       1.0 / 8192.0)
            v.drain()
        inst.then_inc(r_dv, 1)
        v.wait_ge(r_pe, 1)
        inst = v.tensor_copy(r_pool_f[:, :], PB(0, 0, 8))
        inst.then_inc(r_dv, 1)
        v.wait_ge(r_pe, 3)
        v.tensor_add(r_logit[0:1, :], PB(2, 0, 8, 0, 1), b2r[0:1, :])
        v.drain()
        v.reduce_max(r_mx[0:1, :], r_logit[0:1, :], AX.X)
        v.drain()
        inst = v.tensor_scalar_mul(r_mxn[0:1, :], r_mx[0:1, :], -1.0)
        inst.then_inc(r_dv, 1)
        v.wait_ge(r_ac, 2)
        v.reciprocal(r_er[0:1, :], r_es[0:1, :])
        v.drain()
        v.tensor_scalar(r_probs[0:1, :], r_ex[0:1, :], r_er[0:1, 0:1], None, ALU.mult)
        inst = v.drain()
        inst.then_inc(r_dv, 1)
        v.wait_ge(r_pe, 4)
        v.tensor_copy(r_A[:, :], PB(3, 0, 8, 0, 8))
        v.tensor_copy(r_B[:, :], PB(3, 8, 8, 0, 8))
        v.drain()
        v.tensor_tensor(r_gt[:, :], r_A[:, :], r_B[:, :], ALU.is_gt)
        v.drain()
        v.tensor_tensor(r_A[:, :], r_A[:, :], r_B[:, :], ALU.is_equal)
        v.drain()
        v.tensor_mul(r_A[:, :], r_A[:, :], clow8[:, :])
        v.drain()
        v.tensor_add(r_gt[:, :], r_gt[:, :], r_A[:, :])
        v.drain()
        v.tensor_reduce(r_rank[:, :], r_gt[:, :], AX.X, ALU.add)
        v.drain()
        v.tensor_single_scalar(r_sel[:, :], r_rank[:, :], 3.0, ALU.is_lt)
        v.drain()
        v.tensor_mul(r_wun[:, :], r_sel[:, :], r_B[:, 0:1])
        inst = v.drain()
        inst.then_inc(r_dv, 1)
        v.wait_ge(r_pe, 5)
        v.reciprocal(r_rZ[0:1, :], PB(4, 0, 1, 0, 1))
        for s3 in range(3):
            v.tensor_single_scalar(r_oh[:, s3:s3 + 1], PB(4, 1, 1, 0, 8),
                                   float(s3), ALU.is_equal)
        v.drain()
        sel_b = bass.AP(r_sel, 0, [[1, 8], [0, 3]])
        v.tensor_tensor(r_oh[:, :], r_oh[:, :], sel_b, ALU.mult)
        inst = v.drain()
        inst.then_inc(r_dv, 1)
        v.wait_ge(r_pe, 6)
        v.tensor_scalar(r_wsel[0:1, :], PB(5, 0, 3, 0, 1), r_rZ[0:1, 0:1], None,
                        ALU.mult)
        v.tensor_copy(r_id[0:1, :], PB(5, 3, 3, 0, 1))
        inst = v.drain()
        inst.then_inc(r_dv, 1)
        v.wait_ge(r_pe, 7)
        v.tensor_copy(w_bc[:, :], PB(6, 0, 3))
        v.tensor_copy(e_bc[:, :], PB(6, 3, 3))
        v.drain()
        v.tensor_scalar_mul(e_bc[:, :], e_bc[:, :], 4096.0)
        v.drain()
        for s3 in range(3):
            v.tensor_scalar(idx_f[:, s3 * 32:(s3 + 1) * 32], ciofc[:, :],
                            e_bc[:, s3:s3 + 1], None, ALU.add)
        v.drain()
        v.tensor_copy(idx_i[:, :], idx_f[:, :])
        inst = v.drain()
        inst.then_inc(rdone, 1)

    @blk1.scalar
    def _(a):
        a.wait_ge(s_cst, 16 * NCST)
        NRs = NT if trunc is None else trunc

        def AAs(k):
            return (k % 2) * 4

        def sqrt1(k):
            pi = k % 2
            a.wait_ge(v_var1, k + 1)
            inst = a.activation(sqtt[:, pi:pi + 1], bnag[:, pi * 2 + 1:pi * 2 + 2],
                                AF.Sqrt, bias=ceps[:, 0:1])
            inst.then_inc(a_ln1, 1)

        def tr_evac(k):
            pi = k % 2
            A = AAs(k)
            u1f_p = u1f[:, pi * 1024:(pi + 1) * 1024]
            a.wait_ge(t_tr, k + 1)
            a.activation(u1f_p[:, 0:512], PB(A, 0, 512), AF.Copy)
            inst = a.activation(u1f_p[:, 512:1024], PB(A + 1, 0, 512), AF.Copy)
            inst.then_inc(v_tr, 1)

        def q_evac(k):
            pi = k % 2
            A = AAs(k)
            qf_p = qf[:, pi * 1024:(pi + 1) * 1024]
            a.wait_ge(t_q, k + 1)
            a.activation(qf_p[:, 0:512], PB(A + 2, 0, 512), AF.Copy)
            inst = a.activation(qf_p[:, 512:1024], PB(A, 0, 512), AF.Copy)
            inst.then_inc(v_q, 1)

        def o_evac(k):
            pi = k % 2
            A = AAs(k)
            of_p = o_f[:, pi * 1024:(pi + 1) * 1024]
            a.wait_ge(t_oh, 2 * (k + 1))
            a.activation(of_p[:, 0:512], PB(A + 1, 0, 512), AF.Copy)
            inst = a.activation(of_p[:, 512:1024], PB(A + 2, 0, 512), AF.Copy)
            inst.then_inc(v_o, 1)

        sqrt1(0)
        sqrt1(1) if NRs > 1 else None
        tr_evac(0)
        q_evac(0)
        for k in range(NRs):
            pi = k % 2
            if k + 1 < NRs:
                tr_evac(k + 1)
                q_evac(k + 1)
            if k + 2 < NRs:
                sqrt1(k + 2)
            a.wait_ge(v_sub, k + 1)
            inst = a.activation(exw[:, pi * 256:(pi + 1) * 256],
                                msub[:, pi * 256:(pi + 1) * 256], AF.Exp,
                                scale=0.125)
            inst.then_inc(a_exp, 1)
            o_evac(k)
            a.wait_ge(v_var2, k + 1)
            inst = a.activation(sqtt[:, 2 + pi:3 + pi], bnag[:, 4 + pi * 2 + 1:4 + pi * 2 + 2],
                                AF.Sqrt, bias=ceps[:, 0:1])
            inst.then_inc(a_ln2, 1)

        if trunc is not None:
            return
        a.wait_ge(r_pe, 2)
        for fc2 in range(2):
            inst = a.activation(r_h[:, fc2:fc2 + 1], PB(1, fc2, 1), AF.Tanh,
                                bias=b1f[:, fc2:fc2 + 1])
        inst.then_inc(r_ac, 1)
        a.wait_ge(r_dv, 3)
        inst = a.activation(r_ex[0:1, :], r_logit[0:1, :], AF.Exp,
                            bias=r_mxn[0:1, 0:1], accum_out=r_es[0:1, 0:1])
        inst.then_inc(r_ac, 1)

    es1.close()

    es3 = contextlib.ExitStack()
    sb3 = lambda name, shape, dt=F32: es3.enter_context(nc.sbuf_tensor(name, shape, dt))
    u_fm3 = sb3("u_fm3", [128, 8192])
    acc3 = sb3("acc3", [128, 8192])
    wa_buf = sb3("wa_buf", [128, 8192])
    wb_buf = sb3("wb_buf", [128, 8192])
    g_sb = sb3("g_sb", [128, 512])
    u2t3 = sb3("u2t3", [128, 2048])

    blk2 = es3.enter_context(nc.Block("phase3"))

    @blk2.gpsimd
    def _(g):
        if trunc is not None:
            return
        g.wait_ge(rdone, 1)
        g.wait_ge(s_ys0, 16 * NT // 2)
        g.wait_ge(s_ys1, 16 * NT // 2)
        nblk = 0
        for p in range(NPASS):
            for r in range(TPP):
                gi = p * TPP + r
                pi = gi % 2
                if gi >= 2:
                    g.wait_ge(v_yb, gi - 1)
                g.indirect_dma_start(
                    x_in[:, pi * 1024:(pi + 1) * 1024], None,
                    yscr_d[:, :],
                    bass.IndirectOffsetOnAxis(cyidx[:, gi:gi + 1], 0),
                ).then_inc(s_yr0 if pi == 0 else s_yr1, 16)
            for s in range(3):
                for fcb in range(8):
                    pi = nblk % 2
                    if nblk >= 2:
                        g.wait_ge(t_o3, 16 * (nblk - 1))
                    for j in range(4):
                        fc = fcb * 4 + j
                        g.indirect_dma_start(
                            wa_buf[:, pi * 4096 + j * 1024: pi * 4096 + (j + 1) * 1024],
                            None, wa_d[:, :],
                            bass.IndirectOffsetOnAxis(
                                idx_i[:, s * 32 + fc:s * 32 + fc + 1], 0),
                        ).then_inc(s_ga0 if pi == 0 else s_ga1, 16)
                        g.indirect_dma_start(
                            wb_buf[:, pi * 4096 + j * 1024: pi * 4096 + (j + 1) * 1024],
                            None, wb_d[:, :],
                            bass.IndirectOffsetOnAxis(
                                idx_i[:, s * 32 + fc:s * 32 + fc + 1], 0),
                        ).then_inc(s_ga0 if pi == 0 else s_ga1, 16)
                    nblk += 1

    @blk2.sync
    def _(sync):
        if trunc is not None:
            return
        for p in range(NPASS):
            sync.wait_ge(v_ev, 192 * (p + 1))
            for r in range(TPP):
                sync.dma_start(
                    out_d[(p * TPP + r) * 128:(p * TPP + r + 1) * 128, :],
                    in_=acc3[:, r * 1024:(r + 1) * 1024]).then_inc(s_out, 16)
        sync.wait_ge(s_out, 16 * NPASS * TPP)

    @blk2.tensor
    def _(t):
        if trunc is not None:
            return
        blk_cnt = 0
        for p in range(NPASS):
            for r in range(TPP):
                gi = p * TPP + r
                pi = gi % 2
                t.wait_ge(v_u23, gi + 1)
                if gi >= 1:
                    t.wait_ge(v_tr3, 2 * gi)
                for ic in range(8):
                    bank = 3 if ic < 4 else 7
                    inst = t.transpose(PB(bank, (ic % 4) * 128, 128),
                                       u2t3[:, pi * 1024 + ic * 128: pi * 1024 + (ic + 1) * 128],
                                       cident[:, :])
                    if ic == 3:
                        inst.then_inc(t_tr3, 1)
                inst.then_inc(t_tr3, 1)
            t.wait_ge(v_tr3, 2 * TPP * (p + 1))
            for s in range(3):
                for fcb in range(8):
                    pi = blk_cnt % 2
                    t.wait_ge(s_ga0 if pi == 0 else s_ga1,
                              128 * (blk_cnt // 2 + 1))
                    for tq in range(4):
                        unit = ((p * 3 + s) * 8 + fcb) * 4 + tq
                        for j in range(4):
                            J = unit * 4 + j
                            zb = 0 if J % 2 == 0 else 4
                            if J >= 2:
                                t.wait_ge(a_g, J - 1)
                            for ic in range(8):
                                inst = t.matmul(
                                    PB(zb, 0, 256),
                                    wa_buf[:, pi * 4096 + j * 1024 + ic * 128: pi * 4096 + j * 1024 + (ic + 1) * 128],
                                    u_fm3[:, ic * 1024 + tq * 256: ic * 1024 + tq * 256 + 256],
                                    start=(ic == 0), stop=(ic == 7))
                            inst.then_inc(t_z, 1)
                            t.wait_ge(a_g, J + 1)
                            if j == 0 and unit >= 1:
                                t.wait_ge(v_ev, 2 * unit)
                            for th in range(2):
                                obase = 512 + th * 2048
                                for nh in range(2):
                                    inst = t.matmul(
                                        ps_t[:, obase + nh * 512: obase + (nh + 1) * 512],
                                        g_sb[:, (J % 2) * 256 + th * 128:(J % 2) * 256 + th * 128 + 128],
                                        wb_buf[:, pi * 4096 + j * 1024 + nh * 512: pi * 4096 + j * 1024 + (nh + 1) * 512],
                                        start=(j == 0), stop=(j == 3))
                            inst.then_inc(t_o3, 1)
                    blk_cnt += 1

    @blk2.vector
    def _(v):
        if trunc is not None:
            return
        for p in range(NPASS):
            if p >= 1:
                v.wait_ge(s_out, 128 * p)
            for r in range(TPP):
                gi = p * TPP + r
                pi = gi % 2
                y_p = x_in[:, pi * 1024:(pi + 1) * 1024]
                u23_p = u2t3[:, pi * 1024:(pi + 1) * 1024]
                v.wait_ge(s_yr0 if pi == 0 else s_yr1, 16 * (gi // 2 + 1))
                if gi >= 2:
                    v.wait_ge(a_ln3, gi - 1)
                v.bn_stats(bnst[:, pi * 12:pi * 12 + 6], y_p[:, 0:512])
                v.bn_stats(bnst[:, pi * 12 + 6:pi * 12 + 12], y_p[:, 512:1024])
                v.drain()
                inst = v.bn_aggr(bnag[:, pi * 2:pi * 2 + 2],
                                 bnst[:, pi * 12:pi * 12 + 12])
                v.drain()
                inst.then_inc(v_var3, 1)
                v.wait_ge(a_ln3, gi + 1)
                v.reciprocal(rstdt[:, pi:pi + 1], sqtt[:, pi:pi + 1])
                v.drain()
                inst = v.tensor_scalar(u23_p, y_p, bnag[:, pi * 2:pi * 2 + 1],
                                       rstdt[:, pi:pi + 1], ALU.subtract, ALU.mult)
                inst.then_inc(v_u23, 1)
                v.drain()
                inst = v.tensor_add(acc3[:, r * 1024:(r + 1) * 1024], y_p, u23_p)
                inst.then_inc(v_yb, 1)
                v.wait_ge(t_tr3, 2 * gi + 1)
                for ic in range(4):
                    v.tensor_copy(u_fm3[:, ic * 1024 + r * 128: ic * 1024 + (r + 1) * 128],
                                  PB(3, ic * 128, 128))
                v.wait_ge(t_tr3, 2 * gi + 2)
                for ic in range(4, 8):
                    inst = v.tensor_copy(
                        u_fm3[:, ic * 1024 + r * 128: ic * 1024 + (r + 1) * 128],
                        PB(7, (ic - 4) * 128, 128))
                inst.then_inc(v_tr3, 2)
            for s in range(3):
                for fcb in range(8):
                    for tq in range(4):
                        unit = ((p * 3 + s) * 8 + fcb) * 4 + tq
                        v.wait_ge(t_o3, 4 * (unit + 1))
                        for th in range(2):
                            obase = 512 + th * 2048
                            tile_local = tq * 2 + th
                            inst = v.scalar_tensor_tensor(
                                acc3[:, tile_local * 1024:(tile_local + 1) * 1024],
                                ps_t[:, obase:obase + 1024],
                                w_bc[:, s:s + 1],
                                acc3[:, tile_local * 1024:(tile_local + 1) * 1024],
                                ALU.mult, ALU.add)
                            inst.then_inc(v_ev, 1)

    @blk2.scalar
    def _(a):
        if trunc is not None:
            return
        for p in range(NPASS):
            for r in range(TPP):
                gi = p * TPP + r
                pi = gi % 2
                a.wait_ge(v_var3, gi + 1)
                inst = a.activation(sqtt[:, pi:pi + 1], bnag[:, pi * 2 + 1:pi * 2 + 2],
                                    AF.Sqrt, bias=ceps[:, 0:1])
                inst.then_inc(a_ln3, 1)
            for s in range(3):
                for fcb in range(8):
                    for tq in range(4):
                        unit = ((p * 3 + s) * 8 + fcb) * 4 + tq
                        for j in range(4):
                            J = unit * 4 + j
                            zb = 0 if J % 2 == 0 else 4
                            a.wait_ge(t_z, J + 1)
                            if J >= 2:
                                a.wait_ge(t_o3, J - 1)
                            inst = a.activation(
                                g_sb[:, (J % 2) * 256:(J % 2 + 1) * 256],
                                PB(zb, 0, 256), AF.Gelu_apprx_tanh)
                            inst.then_inc(a_g, 1)

    es3.close()
    es.close()
    return nc


def prep_inputs(inputs):
    W_attn = np.asarray(inputs["L0_attn_W_attn"], dtype=np.float32)
    W_proj = np.asarray(inputs["L0_attn_W_proj"], dtype=np.float32)
    W1 = np.asarray(inputs["L0_ffn_W1"], dtype=np.float32)
    b1 = np.asarray(inputs["L0_ffn_b1"], dtype=np.float32)
    W2 = np.asarray(inputs["L0_ffn_W2"], dtype=np.float32)
    b2 = np.asarray(inputs["L0_ffn_b2"], dtype=np.float32)
    Wa = np.asarray(inputs["L0_ffn_Wa"], dtype=np.float32)
    Wb = np.asarray(inputs["L0_ffn_Wb"], dtype=np.float32)

    d = {}
    d["wq_t"] = np.ascontiguousarray(
        W_attn[0:1024].T.reshape(8, 128, 8, 128).transpose(1, 0, 2, 3).reshape(128, 8192))
    d["wk_t"] = np.ascontiguousarray(
        W_attn[1024:2048].T.reshape(8, 128, 1024).transpose(1, 0, 2).reshape(128, 8192))
    d["wv_t"] = np.ascontiguousarray(
        W_attn[2048:3072].T.reshape(8, 128, 1024).transpose(1, 0, 2).reshape(128, 8192))
    d["wp_t"] = np.ascontiguousarray(
        W_proj.T.reshape(8, 128, 1024).transpose(1, 0, 2).reshape(128, 8192))
    d["w1_t"] = np.ascontiguousarray(
        W1.reshape(8, 128, 2, 128).transpose(1, 0, 2, 3).reshape(128, 2048))
    d["b1_t"] = np.ascontiguousarray(b1.reshape(2, 128).T)
    d["w2_t"] = np.ascontiguousarray(
        W2.reshape(2, 128, 8).transpose(1, 0, 2).reshape(128, 16))
    d["b2_t"] = np.ascontiguousarray(b2.reshape(1, 8))
    d["wa_r"] = np.ascontiguousarray(
        Wa.reshape(8, 8, 128, 32, 128).transpose(0, 3, 2, 1, 4).reshape(32768, 1024))
    d["wb_r"] = np.ascontiguousarray(Wb.reshape(32768, 1024))
    d["cident"] = np.eye(128, dtype=np.float32)
    t_ar = np.arange(128)
    j_ar = np.arange(16)
    m = np.where(j_ar[None, :] <= t_ar[:, None], 0.0, NEG).astype(np.float32)
    d["cmask"] = np.ascontiguousarray(np.tile(m, (1, 16)))
    p_ar = np.arange(128, dtype=np.float32)
    d["ciota_fc"] = np.ascontiguousarray(
        (p_ar[:, None] + 128.0 * np.arange(32, dtype=np.float32)[None, :]).astype(np.float32))
    d["cones_r"] = np.ones((1, 128), dtype=np.float32)
    d["cones_c"] = np.ones((128, 1), dtype=np.float32)
    i8 = np.arange(8)
    d["clow8"] = (i8[None, :] < i8[:, None]).astype(np.float32)
    d["cup8"] = (i8[:, None] < i8[None, :]).astype(np.float32)
    d["ciota_p"] = p_ar.reshape(128, 1).astype(np.float32)
    d["ceps"] = np.full((128, 1), EPS, dtype=np.float32)
    return d


def core_inputs(inputs, shared, c):
    x = np.asarray(inputs["L0_attn_x"], dtype=np.float32)
    b, half = c // 2, c % 2
    m = dict(shared)
    m["x"] = np.ascontiguousarray(x[b].reshape(8192, 1024))
    base = half * 4096
    rows = (base + np.arange(4096)).reshape(32, 128).T
    m["cyidx"] = np.ascontiguousarray(rows.astype(np.int32))
    return m


from concourse.bass_utils import run_bass_kernel_spmd


def kernel(**inputs):
    shared = prep_inputs(inputs)
    in_maps = [core_inputs(inputs, shared, c) for c in range(8)]
    nc = build_nc()
    res = run_bass_kernel_spmd(nc, in_maps, core_ids=list(range(8)))
    out = np.empty((4, 8192, 1024), dtype=np.float32)
    for c in range(8):
        b, half = c // 2, c % 2
        out[b, half * 4096:(half + 1) * 4096, :] = res.results[c]["out"]
    return out


# version 07
# speedup vs baseline: 1.1862x; 1.1397x over previous
import contextlib
import numpy as np
import concourse.bass as bass
import concourse.mybir as mybir

F32 = mybir.dt.float32
I32 = mybir.dt.int32
AF = mybir.ActivationFunctionType
ALU = mybir.AluOpType
AX = mybir.AxisListType

NT = 64
NPASS = 2
TPP = 16
TQP = 8
EPS = 1e-5
NEG = -1.0e30


def build_nc(debug=False, pipe=True, pool_short=False, trunc=None):
    nc = bass.Bass("TRN2", target_bir_lowering=False)

    x_d = nc.dram_tensor("x", [8192, 1024], F32, kind="ExternalInput").ap()
    wq_d = nc.dram_tensor("wq_t", [128, 8192], F32, kind="ExternalInput").ap()
    wk_d = nc.dram_tensor("wk_t", [128, 8192], F32, kind="ExternalInput").ap()
    wv_d = nc.dram_tensor("wv_t", [128, 8192], F32, kind="ExternalInput").ap()
    wp_d = nc.dram_tensor("wp_t", [128, 8192], F32, kind="ExternalInput").ap()
    w1_d = nc.dram_tensor("w1_t", [128, 2048], F32, kind="ExternalInput").ap()
    b1_d = nc.dram_tensor("b1_t", [128, 2], F32, kind="ExternalInput").ap()
    w2_d = nc.dram_tensor("w2_t", [128, 16], F32, kind="ExternalInput").ap()
    b2_d = nc.dram_tensor("b2_t", [1, 8], F32, kind="ExternalInput").ap()
    wa_d = nc.dram_tensor("wa_r", [32768, 1024], F32, kind="ExternalInput").ap()
    wb_d = nc.dram_tensor("wb_r", [32768, 1024], F32, kind="ExternalInput").ap()
    cid_d = nc.dram_tensor("cident", [128, 128], F32, kind="ExternalInput").ap()
    cmask_d = nc.dram_tensor("cmask", [128, 256], F32, kind="ExternalInput").ap()
    ciofc_d = nc.dram_tensor("ciota_fc", [128, 32], F32, kind="ExternalInput").ap()
    conr_d = nc.dram_tensor("cones_r", [1, 128], F32, kind="ExternalInput").ap()
    conc_d = nc.dram_tensor("cones_c", [128, 1], F32, kind="ExternalInput").ap()
    clow_d = nc.dram_tensor("clow8", [8, 8], F32, kind="ExternalInput").ap()
    cup_d = nc.dram_tensor("cup8", [8, 8], F32, kind="ExternalInput").ap()
    ciop_d = nc.dram_tensor("ciota_p", [128, 1], F32, kind="ExternalInput").ap()
    cyidx_d = nc.dram_tensor("cyidx", [128, 32], I32, kind="ExternalInput").ap()
    ceps_d = nc.dram_tensor("ceps", [128, 1], F32, kind="ExternalInput").ap()

    yscr_d = nc.dram_tensor("y_scr", [8192, 1024], F32, kind="Internal").ap()
    out_d = nc.dram_tensor("out", [4096, 1024], F32, kind="ExternalOutput").ap()
    if debug:
        dbg_d = nc.dram_tensor("dbg", [128, 16], F32, kind="ExternalOutput").ap()

    es = contextlib.ExitStack()
    sb = lambda name, shape, dt=F32: es.enter_context(nc.sbuf_tensor(name, shape, dt))
    sem = lambda name: es.enter_context(nc.semaphore(name))

    cident = sb("s_cident", [128, 128])

    ciofc = sb("s_ciota_fc", [128, 32])
    conc = sb("s_cones_c", [128, 1])
    clow8 = sb("s_clow8", [8, 8])
    cup8 = sb("s_cup8", [8, 8])
    ciop = sb("s_ciota_p", [128, 1])
    cyidx = sb("s_cyidx", [128, 32], I32)
    ceps = sb("s_ceps", [128, 1])
    b2r = sb("s_b2r", [1, 8])

    x_in = sb("x_in", [128, 2048])
    bnst = sb("bnst", [128, 24])
    bnag = sb("bnag", [128, 8])
    rstdt = sb("rstdt", [128, 4])
    sqtt = sb("sqtt", [128, 4])

    w_bc = sb("w_bc", [128, 3])
    idx_i = sb("idx_i", [128, 96], I32)

    ps_t = es.enter_context(nc.psum_tensor("ps", [128, 4096], F32))

    def PB(bank, lo=0, n=512, p0=0, np_=128):
        return ps_t[p0:p0 + np_, bank * 512 + lo: bank * 512 + lo + n]

    s_xs = [sem(f"s_x{i}") for i in range(4)]
    s_wq, s_wkv, s_wp, s_cst = sem("s_wq"), sem("s_wkv"), sem("s_wp"), sem("s_cst")
    s_ys0, s_ys1 = sem("s_ys0"), sem("s_ys1")
    s_yr0, s_yr1 = sem("s_yr0"), sem("s_yr1")
    s_ga0, s_ga1 = sem("s_ga0"), sem("s_ga1")
    s_out = sem("s_out")

    a_sq = sem("a_sq")
    v_var1, a_ln1, v_u1 = sem("v_var1"), sem("a_ln1"), sem("v_u1")
    t_tr, v_tr = sem("t_tr"), sem("v_tr")
    t_q, v_q = sem("t_q"), sem("v_q")
    t_kv, v_kv = sem("t_kv"), sem("v_kv")
    t_lg = sem("t_lg")
    v_sub, a_exp, v_w = sem("v_sub"), sem("a_exp"), sem("v_w")
    t_wt, v_wt, t_oh = sem("t_wt"), sem("v_wt"), sem("t_oh")
    t_o, v_o = sem("t_o"), sem("v_o")
    t_proj, v_y = sem("t_proj"), sem("v_y")
    v_var2, a_ln2, v_u2 = sem("v_var2"), sem("a_ln2"), sem("v_u2")
    t_pool, v_pool = sem("t_pool"), sem("v_pool")
    r_pe, r_dv, r_ac, rdone = sem("r_pe"), sem("r_dv"), sem("r_ac"), sem("rdone")
    v_yb = sem("v_yb")
    v_u23, t_tr3, v_tr3 = sem("v_u23"), sem("t_tr3"), sem("v_tr3")
    a_sq3, v_var3, a_ln3 = sem("a_sq3"), sem("v_var3"), sem("a_ln3")
    t_z, a_g, t_o3, v_ev = sem("t_z"), sem("a_g"), sem("t_o3"), sem("v_ev")
    t_wfree = sem("t_wfree")

    NCST = 14

    def BK(k, i):
        return (k % 2) * 4 + i

    es1 = contextlib.ExitStack()
    sb1 = lambda name, shape, dt=F32: es1.enter_context(nc.sbuf_tensor(name, shape, dt))
    wq_s = sb1("wq_s", [128, 8192])
    wp_s = sb1("wp_s", [128, 8192])
    wkv_s = sb1("wkv_s", [128, 4096])
    u1 = sb1("u1", [128, 4096])
    u1f = sb1("u1f", [128, 2048])
    qf = sb1("qf", [128, 2048])
    k16f = sb1("k16f", [128, 128])
    k16t = sb1("k16t", [16, 1024])
    v16t = sb1("v16t", [16, 1024])
    msub = sb1("msub", [128, 512])
    mxg = sb1("mxg", [128, 32])
    exw = sb1("exw", [128, 512])
    esum = sb1("esum", [128, 32])
    erc = sb1("erc", [128, 32])
    w_tm = sb1("w_tm", [128, 512])
    wTh = sb1("wTh", [16, 2048])
    o_f = sb1("o_f", [128, 2048])
    y_t = sb1("y_t", [128, 2048])
    u2 = sb1("u2", [128, 2048])
    x_cd = sb1("x_cd", [128, 2048])
    cmask = sb1("s_cmask", [128, 256])
    w1f = sb1("s_w1f", [128, 2048])
    pooled_acc = sb1("pooled_acc", [1, 1024])
    pool_tmp = sb1("pool_tmp", [1, 1024])
    conr = sb1("s_cones_r", [1, 128])
    b1f = sb1("s_b1f", [128, 2])
    w2f = sb1("s_w2f", [128, 16])
    r_pool_f = sb1("r_pool_f", [128, 8])
    r_h = sb1("r_h", [128, 2])
    r_logit = sb1("r_logit", [1, 8])
    r_mx = sb1("r_mx", [1, 1])
    r_mxn = sb1("r_mxn", [1, 1])
    r_ex = sb1("r_ex", [1, 8])
    r_es = sb1("r_es", [1, 1])
    r_er = sb1("r_er", [1, 1])
    r_probs = sb1("r_probs", [1, 8])
    r_A = sb1("r_A", [8, 8])
    r_B = sb1("r_B", [8, 8])
    r_gt = sb1("r_gt", [8, 8])
    r_rank = sb1("r_rank", [8, 1])
    r_sel = sb1("r_sel", [8, 1])
    r_wun = sb1("r_wun", [8, 1])
    r_rZ = sb1("r_rZ", [1, 1])
    r_oh = sb1("r_oh", [8, 3])
    r_wsel = sb1("r_wsel", [1, 3])
    r_id = sb1("r_id", [1, 3])
    e_bc = sb1("e_bc", [128, 3])
    idx_f = sb1("idx_f", [128, 96])

    blk1 = es1.enter_context(nc.Block("phase1"))

    @blk1.sync
    def _(sync):
        for dst, src in [(cident, cid_d), (cmask, cmask_d), (ciofc, ciofc_d),
                         (conr, conr_d), (conc, conc_d), (clow8, clow_d),
                         (cup8, cup_d), (ciop, ciop_d), (cyidx, cyidx_d),
                         (ceps, ceps_d),
                         (b2r, b2_d), (b1f, b1_d), (w1f, w1_d), (w2f, w2_d)]:
            sync.dma_start(out=dst[:, :], in_=src[:, :]).then_inc(s_cst, 16)
        sync.dma_start(out=x_in[:, 0:1024], in_=x_d[0:128, :]).then_inc(s_xs[0], 16)
        sync.dma_start(out=x_in[:, 1024:2048], in_=x_d[128:256, :]).then_inc(s_xs[1], 16)
        sync.dma_start(out=wq_s[:, :], in_=wq_d[:, :]).then_inc(s_wq, 16)
        sync.dma_start(out=wkv_s[:, :], in_=wk_d[:, 0:4096]).then_inc(s_wkv, 16)
        sync.dma_start(out=wp_s[:, :], in_=wp_d[:, :]).then_inc(s_wp, 16)
        sync.wait_ge(t_kv, 1)
        sync.dma_start(out=wkv_s[:, :], in_=wk_d[:, 4096:8192]).then_inc(s_wkv, 16)
        sync.wait_ge(t_kv, 2)
        sync.dma_start(out=wkv_s[:, :], in_=wv_d[:, 0:4096]).then_inc(s_wkv, 16)
        sync.wait_ge(t_kv, 3)
        sync.dma_start(out=wkv_s[:, :], in_=wv_d[:, 4096:8192]).then_inc(s_wkv, 16)
        for k in range(2, NT if trunc is None else trunc):
            sl = k % 4
            if k >= 4:
                sync.wait_ge(v_y, k - 3)
            xbuf = x_in if sl < 2 else x_cd
            sync.dma_start(out=xbuf[:, (sl % 2) * 1024:(sl % 2 + 1) * 1024],
                           in_=x_d[k * 128:(k + 1) * 128, :]).then_inc(
                               s_xs[sl], 16)
        if trunc is not None:
            sync.wait_ge(v_y, trunc)
            sync.dma_start(out=out_d[0:128, :],
                           in_=y_t[:, 0:1024]).then_inc(s_out, 16)
            sync.wait_ge(s_out, 16)

    @blk1.gpsimd
    def _(g):
        for k in range(NT if trunc is None else trunc):
            pi = k % 2
            g.wait_ge(v_y, k + 1)
            g.dma_start(yscr_d[k * 128:(k + 1) * 128, :],
                        y_t[:, pi * 1024:(pi + 1) * 1024]).then_inc(
                            s_ys0 if pi == 0 else s_ys1, 16)

    @blk1.tensor
    def _(t):
        t.wait_ge(s_cst, 16 * NCST)

        def AA(k):
            return (k % 2) * 4

        def tr_stage(k):
            pi = k % 2
            A = AA(k)
            u1_p = u1[:, (k % 4) * 1024:(k % 4 + 1) * 1024]
            t.wait_ge(v_u1, k + 1)
            if k >= 2:
                t.wait_ge(v_y, k - 1)
            for ic in range(8):
                bank = A if ic < 4 else A + 1
                inst = t.transpose(PB(bank, (ic % 4) * 128, 128),
                                   u1_p[:, ic * 128:(ic + 1) * 128], cident[:, :])
            inst.then_inc(t_tr, 1)

        def q_stage(k):
            pi = k % 2
            A = AA(k)
            u1f_p = u1f[:, pi * 1024:(pi + 1) * 1024]
            if k == 0:
                t.wait_ge(s_wq, 16)
            t.wait_ge(v_tr, k + 1)
            if k >= 2:
                t.wait_ge(v_y, k - 1)
            for oc in range(8):
                bank = A + 2 if oc < 4 else A
                lo = (oc % 4) * 128
                for ic in range(8):
                    inst = t.matmul(PB(bank, lo, 128),
                                    wq_s[:, ic * 1024 + oc * 128: ic * 1024 + (oc + 1) * 128],
                                    u1f_p[:, ic * 128:(ic + 1) * 128],
                                    start=(ic == 0), stop=(ic == 7))
            inst.then_inc(t_q, 1)

        def kv_block():
            t.wait_ge(v_q, 1)
            for part in range(4):
                t.wait_ge(s_wkv, 16 * (part + 1))
                if part == 2:
                    t.wait_ge(v_kv, 1)
                for half in range(2):
                    for icl in range(4):
                        ic = (part % 2) * 4 + icl
                        inst = t.matmul(PB(1 + half, 0, 512, 0, 16),
                                        u1f[:, ic * 128: ic * 128 + 16],
                                        wkv_s[:, icl * 1024 + half * 512: icl * 1024 + (half + 1) * 512],
                                        start=(ic == 0), stop=(ic == 7))
                inst.then_inc(t_kv, 1)
            t.wait_ge(v_kv, 2)
            t.wait_ge(v_tr, 1)
            for ic in range(8):
                inst = t.transpose(PB(0, ic * 16, 16),
                                   k16t[0:16, ic * 128:(ic + 1) * 128],
                                   cident[0:16, 0:16])
            inst.then_inc(t_kv, 1)
            t.wait_ge(v_kv, 3)

        def logits_stage(k):
            pi = k % 2
            A = AA(k)
            qf_p = qf[:, pi * 1024:(pi + 1) * 1024]
            t.wait_ge(v_q, k + 1)
            for h in range(16):
                po = (h % 2) * 64
                ch = h // 2
                inst = t.matmul(PB(A + 1, h * 16, 16),
                                qf_p[po:po + 64, ch * 128:(ch + 1) * 128],
                                k16f[po:po + 64, ch * 16:(ch + 1) * 16],
                                start=True, stop=True)
            inst.then_inc(t_lg, 1)

        def pool_stage(k):
            pi = k % 2
            u2_pp = u2[:, pi * 1024:(pi + 1) * 1024]
            t.wait_ge(v_u2, k + 1)
            if pool_short:
                t.wait_ge(v_pool, k)
            st_ = (k == 0) if not pool_short else True
            sp_ = (k == NT - 1) if not pool_short else True
            inst = t.matmul(PB(3, 0, 512, 0, 1), conc[:, :], u2_pp[:, 0:512],
                            start=st_, stop=sp_)
            inst = t.matmul(PB(7, 0, 512, 0, 1), conc[:, :], u2_pp[:, 512:1024],
                            start=st_, stop=sp_)
            inst.then_inc(t_pool, 1)

        def wT_o_proj_stage(k, wt_base):
            pi = k % 2
            A = AA(k)
            wtm_p = w_tm[:, pi * 256:(pi + 1) * 256]
            of_p = o_f[:, pi * 1024:(pi + 1) * 1024]
            t.wait_ge(v_w, k + 1)
            t.wait_ge(v_sub, k + 1)
            for hh in range(8):
                bank = A + 2 if hh < 4 else A
                inst = t.transpose(PB(bank, (hh % 4) * 128, 128, 0, 16),
                                   wtm_p[:, hh * 16:(hh + 1) * 16], cident[:, :])
            inst.then_inc(t_wt, 1)
            if k >= 1:
                pool_stage(k - 1)
            t.wait_ge(v_wt, wt_base + 1)
            for hh in range(8):
                h = 8 + hh
                bank = A + 2 if hh < 4 else A
                inst = t.transpose(PB(bank, (hh % 4) * 128, 128, 0, 16),
                                   wtm_p[:, h * 16:(h + 1) * 16], cident[:, :])
            inst.then_inc(t_wt, 1)
            for h in range(8):
                p8 = h // 2
                inst = t.matmul(PB(A + 1, p8 * 128, 128, (h % 2) * 64, 64),
                                v16t[0:16, h * 64:(h + 1) * 64],
                                wTh[0:16, h * 128:(h + 1) * 128],
                                start=True, stop=True)
            inst.then_inc(t_oh, 1)
            t.wait_ge(v_wt, wt_base + 2)
            for h in range(8, 16):
                p8 = h // 2 - 4
                inst = t.matmul(PB(A + 2, p8 * 128, 128, (h % 2) * 64, 64),
                                v16t[0:16, h * 64:(h + 1) * 64],
                                wTh[0:16, h * 128:(h + 1) * 128],
                                start=True, stop=True)
            inst.then_inc(t_oh, 1)
            if k == 0:
                t.wait_ge(s_wp, 16)
            t.wait_ge(v_o, k + 1)
            for half in range(2):
                bank = A + 1 if half == 0 else A + 2
                for ic in range(8):
                    inst = t.matmul(PB(bank, 0, 512),
                                    of_p[:, ic * 128:(ic + 1) * 128],
                                    wp_s[:, ic * 1024 + half * 512: ic * 1024 + (half + 1) * 512],
                                    start=(ic == 0), stop=(ic == 7))
            inst.then_inc(t_proj, 1)

        NR = NT if trunc is None else trunc
        if pipe:
            tr_stage(0)
            q_stage(0)
            kv_block()
            logits_stage(0)
            for k in range(NR):
                if k + 1 < NR:
                    tr_stage(k + 1)
                    q_stage(k + 1)
                    logits_stage(k + 1)
                wT_o_proj_stage(k, 2 * k)
            if trunc is None:
                pool_stage(NT - 1)
        else:
            for k in range(NT):
                tr_stage(k)
                q_stage(k)
                if k == 0:
                    kv_block()
                logits_stage(k)
                wT_o_proj_stage(k, 2 * k)
            pool_stage(NT - 1)

        if trunc is not None:
            return
        t.wait_ge(r_dv, 1)
        for ic in range(8):
            inst = t.matmul(PB(0, ic, 1), pool_tmp[0:1, ic * 128:(ic + 1) * 128],
                            conc[0:1, 0:1], start=True, stop=True)
        inst.then_inc(r_pe, 1)
        t.wait_ge(r_dv, 2)
        for fc2 in range(2):
            for ic in range(8):
                inst = t.matmul(PB(1, fc2, 1),
                                w1f[:, ic * 256 + fc2 * 128: ic * 256 + (fc2 + 1) * 128],
                                r_pool_f[:, ic:ic + 1],
                                start=(ic == 0), stop=(ic == 7))
        inst.then_inc(r_pe, 1)
        t.wait_ge(r_ac, 1)
        for fc2 in range(2):
            inst = t.matmul(PB(2, 0, 8, 0, 1), r_h[:, fc2:fc2 + 1],
                            w2f[:, fc2 * 8:(fc2 + 1) * 8],
                            start=(fc2 == 0), stop=(fc2 == 1))
        inst.then_inc(r_pe, 1)
        t.wait_ge(r_dv, 4)
        inst = t.matmul(PB(3, 0, 8, 0, 8), conr[0:1, 0:8], r_probs[0:1, :],
                        start=True, stop=True)
        inst = t.matmul(PB(3, 8, 8, 0, 8), r_probs[0:1, :], conr[0:1, 0:8],
                        start=True, stop=True)
        inst.then_inc(r_pe, 1)
        t.wait_ge(r_dv, 5)
        inst = t.matmul(PB(4, 0, 1, 0, 1), r_wun[:, :], conc[0:8, 0:1],
                        start=True, stop=True)
        inst = t.matmul(PB(4, 1, 1, 0, 8), cup8[:, :], r_sel[:, :],
                        start=True, stop=True)
        inst.then_inc(r_pe, 1)
        t.wait_ge(r_dv, 6)
        inst = t.matmul(PB(5, 0, 3, 0, 1), r_wun[:, :], r_oh[:, :],
                        start=True, stop=True)
        inst = t.matmul(PB(5, 3, 3, 0, 1), ciop[0:8, 0:1], r_oh[:, :],
                        start=True, stop=True)
        inst.then_inc(r_pe, 1)
        t.wait_ge(r_dv, 7)
        inst = t.matmul(PB(6, 0, 3), conr[:, :], r_wsel[0:1, :],
                        start=True, stop=True)
        inst = t.matmul(PB(6, 3, 3), conr[:, :], r_id[0:1, :],
                        start=True, stop=True)
        inst.then_inc(r_pe, 1)

    @blk1.vector
    def _(v):
        v.wait_ge(s_cst, 16 * NCST)

        def AA(k):
            return (k % 2) * 4

        def ln(dst, src_x, pi, base, sq, var_sem, var_val, ln_sem, u_sem):
            if var_val >= 3:
                v.wait_ge(ln_sem, var_val - 2)
            v.bn_stats(bnst[:, pi * 12:pi * 12 + 6], src_x[:, 0:512])
            v.bn_stats(bnst[:, pi * 12 + 6:pi * 12 + 12], src_x[:, 512:1024])
            v.drain()
            inst = v.bn_aggr(bnag[:, base:base + 2], bnst[:, pi * 12:pi * 12 + 12])
            v.drain()
            inst.then_inc(var_sem, 1)
            v.wait_ge(ln_sem, var_val)
            v.reciprocal(rstdt[:, sq:sq + 1], sqtt[:, sq:sq + 1])
            v.drain()
            inst = v.tensor_scalar(dst, src_x, bnag[:, base:base + 1],
                                   rstdt[:, sq:sq + 1], ALU.subtract, ALU.mult)
            inst.then_inc(u_sem, 1)

        def ln1(k):
            pi = k % 2
            sl = k % 4
            xbuf = x_in if sl < 2 else x_cd
            v.wait_ge(s_xs[sl], 16 * (k // 4 + 1))
            ln(u1[:, sl * 1024:(sl + 1) * 1024],
               xbuf[:, (sl % 2) * 1024:(sl % 2 + 1) * 1024],
               pi, pi * 2, pi, v_var1, k + 1, a_ln1, v_u1)

        def kv_evacs():
            v.wait_ge(t_kv, 2)
            v.tensor_copy(k16t[0:16, 0:512], PB(1, 0, 512, 0, 16))
            inst = v.tensor_copy(k16t[0:16, 512:1024], PB(2, 0, 512, 0, 16))
            inst.then_inc(v_kv, 1)
            v.wait_ge(t_kv, 4)
            v.tensor_copy(v16t[0:16, 0:512], PB(1, 0, 512, 0, 16))
            inst = v.tensor_copy(v16t[0:16, 512:1024], PB(2, 0, 512, 0, 16))
            inst.then_inc(v_kv, 1)
            v.wait_ge(t_kv, 5)
            inst = v.tensor_copy(k16f[:, :], PB(0, 0, 128))
            inst.then_inc(v_kv, 1)

        def softmax(k):
            pi = k % 2
            A = AA(k)
            v.wait_ge(t_lg, k + 1)
            lg_ap = bass.AP(ps_t, (A + 1) * 512, [[4096, 128], [16, 16], [1, 16]])
            ms_g = bass.AP(msub, pi * 256, [[512, 128], [16, 16], [1, 16]])
            mx_b = bass.AP(mxg, pi * 16, [[32, 128], [1, 16], [0, 16]])
            if k == 0:
                v.tensor_add(msub[:, pi * 256:(pi + 1) * 256],
                             PB(A + 1, 0, 256), cmask[:, :])
            else:
                v.tensor_copy(msub[:, pi * 256:(pi + 1) * 256], PB(A + 1, 0, 256))
            v.drain()
            v.reduce_max(mxg[:, pi * 16:(pi + 1) * 16], ms_g, AX.X)
            v.drain()
            inst = v.tensor_tensor(ms_g, ms_g, mx_b, ALU.subtract)
            inst.then_inc(v_sub, 1)
            v.wait_ge(a_exp, k + 1)
            ex_gr = bass.AP(exw, pi * 256, [[512, 128], [16, 16], [1, 16]])
            v.tensor_reduce(esum[:, pi * 16:(pi + 1) * 16], ex_gr, AX.X, ALU.add)
            v.drain()
            v.reciprocal(erc[:, pi * 16:(pi + 1) * 16], esum[:, pi * 16:(pi + 1) * 16])
            v.drain()
            er_b = bass.AP(erc, pi * 16, [[32, 128], [1, 16], [0, 16]])
            ex_g = bass.AP(exw, pi * 256, [[512, 128], [16, 16], [1, 16]])
            w_g = bass.AP(w_tm, pi * 256, [[512, 128], [16, 16], [1, 16]])
            inst = v.tensor_tensor(w_g, ex_g, er_b, ALU.mult)
            inst.then_inc(v_w, 1)

        def wT_evacs(k, wt_base):
            A = AA(k)
            for grp in range(2):
                v.wait_ge(t_wt, wt_base + grp + 1)
                for hh in range(8):
                    h = grp * 8 + hh
                    bank = A + 2 if hh < 4 else A
                    inst = v.tensor_copy(wTh[0:16, h * 128:(h + 1) * 128],
                                         PB(bank, (hh % 4) * 128, 128, 0, 16))
                inst.then_inc(v_wt, 1)

        def y_ln2(k):
            pi = k % 2
            A = AA(k)
            sl = k % 4
            x_p = (x_in if sl < 2 else x_cd)[:, (sl % 2) * 1024:(sl % 2 + 1) * 1024]
            y_p = y_t[:, pi * 1024:(pi + 1) * 1024]
            u2_p = u2[:, pi * 1024:(pi + 1) * 1024]
            v.wait_ge(t_proj, k + 1)
            if k >= 2:
                v.wait_ge(s_ys0 if pi == 0 else s_ys1, 16 * (k // 2))
            v.tensor_add(y_p[:, 0:512], x_p[:, 0:512], PB(A + 1, 0, 512))
            inst = v.tensor_add(y_p[:, 512:1024], x_p[:, 512:1024], PB(A + 2, 0, 512))
            inst.then_inc(v_y, 1)
            v.drain()
            ln(u2_p, y_p, pi, 4 + pi * 2, 2 + pi, v_var2, k + 1, a_ln2, v_u2)
            if pool_short:
                v.wait_ge(t_pool, k + 1)
                if k == 0:
                    v.tensor_copy(pooled_acc[0:1, 0:512], PB(3, 0, 512, 0, 1))
                    inst = v.tensor_copy(pooled_acc[0:1, 512:1024], PB(7, 0, 512, 0, 1))
                else:
                    v.tensor_add(pooled_acc[0:1, 0:512], pooled_acc[0:1, 0:512],
                                 PB(3, 0, 512, 0, 1))
                    inst = v.tensor_add(pooled_acc[0:1, 512:1024],
                                        pooled_acc[0:1, 512:1024],
                                        PB(7, 0, 512, 0, 1))
                inst.then_inc(v_pool, 1)

        NR = NT if trunc is None else trunc
        if pipe:
            ln1(0)
            ln1(1)
            kv_evacs()
            for k in range(NR):
                softmax(k)
                wT_evacs(k, 2 * k)
                if k + 2 < NR:
                    ln1(k + 2)
                y_ln2(k)
        else:
            ln1(0)
            for k in range(NT):
                tr_evac(k)
                q_evac(k)
                if k == 0:
                    kv_evacs()
                softmax(k)
                if k + 1 < NT:
                    ln1(k + 1)
                wT_evacs(k, 2 * k)
                o_evac(k)
                y_ln2(k)

        if trunc is not None:
            return
        if pool_short:
            pass
        v.wait_ge(t_pool, NT)
        if pool_short:
            v.wait_ge(v_pool, NT)
            v.drain()
            v.tensor_scalar_mul(pool_tmp[0:1, :], pooled_acc[0:1, :], 1.0 / 8192.0)
            inst = v.drain()
        else:
            v.drain()
            v.tensor_scalar_mul(pool_tmp[0:1, 0:512], PB(3, 0, 512, 0, 1), 1.0 / 8192.0)
            inst = v.tensor_scalar_mul(pool_tmp[0:1, 512:1024], PB(7, 0, 512, 0, 1),
                                       1.0 / 8192.0)
            v.drain()
        inst.then_inc(r_dv, 1)
        v.wait_ge(r_pe, 1)
        inst = v.tensor_copy(r_pool_f[:, :], PB(0, 0, 8))
        inst.then_inc(r_dv, 1)
        v.wait_ge(r_pe, 3)
        v.tensor_add(r_logit[0:1, :], PB(2, 0, 8, 0, 1), b2r[0:1, :])
        v.drain()
        v.reduce_max(r_mx[0:1, :], r_logit[0:1, :], AX.X)
        v.drain()
        inst = v.tensor_scalar_mul(r_mxn[0:1, :], r_mx[0:1, :], -1.0)
        inst.then_inc(r_dv, 1)
        v.wait_ge(r_ac, 2)
        v.reciprocal(r_er[0:1, :], r_es[0:1, :])
        v.drain()
        v.tensor_scalar(r_probs[0:1, :], r_ex[0:1, :], r_er[0:1, 0:1], None, ALU.mult)
        inst = v.drain()
        inst.then_inc(r_dv, 1)
        v.wait_ge(r_pe, 4)
        v.tensor_copy(r_A[:, :], PB(3, 0, 8, 0, 8))
        v.tensor_copy(r_B[:, :], PB(3, 8, 8, 0, 8))
        v.drain()
        v.tensor_tensor(r_gt[:, :], r_A[:, :], r_B[:, :], ALU.is_gt)
        v.drain()
        v.tensor_tensor(r_A[:, :], r_A[:, :], r_B[:, :], ALU.is_equal)
        v.drain()
        v.tensor_mul(r_A[:, :], r_A[:, :], clow8[:, :])
        v.drain()
        v.tensor_add(r_gt[:, :], r_gt[:, :], r_A[:, :])
        v.drain()
        v.tensor_reduce(r_rank[:, :], r_gt[:, :], AX.X, ALU.add)
        v.drain()
        v.tensor_single_scalar(r_sel[:, :], r_rank[:, :], 3.0, ALU.is_lt)
        v.drain()
        v.tensor_mul(r_wun[:, :], r_sel[:, :], r_B[:, 0:1])
        inst = v.drain()
        inst.then_inc(r_dv, 1)
        v.wait_ge(r_pe, 5)
        v.reciprocal(r_rZ[0:1, :], PB(4, 0, 1, 0, 1))
        for s3 in range(3):
            v.tensor_single_scalar(r_oh[:, s3:s3 + 1], PB(4, 1, 1, 0, 8),
                                   float(s3), ALU.is_equal)
        v.drain()
        sel_b = bass.AP(r_sel, 0, [[1, 8], [0, 3]])
        v.tensor_tensor(r_oh[:, :], r_oh[:, :], sel_b, ALU.mult)
        inst = v.drain()
        inst.then_inc(r_dv, 1)
        v.wait_ge(r_pe, 6)
        v.tensor_scalar(r_wsel[0:1, :], PB(5, 0, 3, 0, 1), r_rZ[0:1, 0:1], None,
                        ALU.mult)
        v.tensor_copy(r_id[0:1, :], PB(5, 3, 3, 0, 1))
        inst = v.drain()
        inst.then_inc(r_dv, 1)
        v.wait_ge(r_pe, 7)
        v.tensor_copy(w_bc[:, :], PB(6, 0, 3))
        v.tensor_copy(e_bc[:, :], PB(6, 3, 3))
        v.drain()
        v.tensor_scalar_mul(e_bc[:, :], e_bc[:, :], 4096.0)
        v.drain()
        for s3 in range(3):
            v.tensor_scalar(idx_f[:, s3 * 32:(s3 + 1) * 32], ciofc[:, :],
                            e_bc[:, s3:s3 + 1], None, ALU.add)
        v.drain()
        v.tensor_copy(idx_i[:, :], idx_f[:, :])
        inst = v.drain()
        inst.then_inc(rdone, 1)

    @blk1.scalar
    def _(a):
        a.wait_ge(s_cst, 16 * NCST)
        NRs = NT if trunc is None else trunc

        def AAs(k):
            return (k % 2) * 4

        def sqrt1(k):
            pi = k % 2
            a.wait_ge(v_var1, k + 1)
            inst = a.activation(sqtt[:, pi:pi + 1], bnag[:, pi * 2 + 1:pi * 2 + 2],
                                AF.Sqrt, bias=ceps[:, 0:1])
            inst.then_inc(a_ln1, 1)

        def tr_evac(k):
            pi = k % 2
            A = AAs(k)
            u1f_p = u1f[:, pi * 1024:(pi + 1) * 1024]
            a.wait_ge(t_tr, k + 1)
            a.activation(u1f_p[:, 0:512], PB(A, 0, 512), AF.Copy)
            inst = a.activation(u1f_p[:, 512:1024], PB(A + 1, 0, 512), AF.Copy)
            inst.then_inc(v_tr, 1)

        def q_evac(k):
            pi = k % 2
            A = AAs(k)
            qf_p = qf[:, pi * 1024:(pi + 1) * 1024]
            a.wait_ge(t_q, k + 1)
            a.activation(qf_p[:, 0:512], PB(A + 2, 0, 512), AF.Copy)
            inst = a.activation(qf_p[:, 512:1024], PB(A, 0, 512), AF.Copy)
            inst.then_inc(v_q, 1)

        def o_evac(k):
            pi = k % 2
            A = AAs(k)
            of_p = o_f[:, pi * 1024:(pi + 1) * 1024]
            a.wait_ge(t_oh, 2 * (k + 1))
            a.activation(of_p[:, 0:512], PB(A + 1, 0, 512), AF.Copy)
            inst = a.activation(of_p[:, 512:1024], PB(A + 2, 0, 512), AF.Copy)
            inst.then_inc(v_o, 1)

        sqrt1(0)
        sqrt1(1) if NRs > 1 else None
        tr_evac(0)
        q_evac(0)
        for k in range(NRs):
            pi = k % 2
            if k + 1 < NRs:
                tr_evac(k + 1)
                q_evac(k + 1)
            a.wait_ge(v_sub, k + 1)
            inst = a.activation(exw[:, pi * 256:(pi + 1) * 256],
                                msub[:, pi * 256:(pi + 1) * 256], AF.Exp,
                                scale=0.125)
            inst.then_inc(a_exp, 1)
            if k + 2 < NRs:
                sqrt1(k + 2)
            o_evac(k)
            a.wait_ge(v_var2, k + 1)
            inst = a.activation(sqtt[:, 2 + pi:3 + pi], bnag[:, 4 + pi * 2 + 1:4 + pi * 2 + 2],
                                AF.Sqrt, bias=ceps[:, 0:1])
            inst.then_inc(a_ln2, 1)

        if trunc is not None:
            return
        a.wait_ge(r_pe, 2)
        for fc2 in range(2):
            inst = a.activation(r_h[:, fc2:fc2 + 1], PB(1, fc2, 1), AF.Tanh,
                                bias=b1f[:, fc2:fc2 + 1])
        inst.then_inc(r_ac, 1)
        a.wait_ge(r_dv, 3)
        inst = a.activation(r_ex[0:1, :], r_logit[0:1, :], AF.Exp,
                            bias=r_mxn[0:1, 0:1], accum_out=r_es[0:1, 0:1])
        inst.then_inc(r_ac, 1)

    es1.close()

    es3 = contextlib.ExitStack()
    sb3 = lambda name, shape, dt=F32: es3.enter_context(nc.sbuf_tensor(name, shape, dt))
    u_fm3 = sb3("u_fm3", [128, 16384])
    acc3 = sb3("acc3", [128, 16384])
    wa_buf = sb3("wa_buf", [128, 8192])
    wb_buf = sb3("wb_buf", [128, 8192])
    g_sb = sb3("g_sb", [128, 512])
    u2t3 = sb3("u2t3", [128, 1024])

    blk2 = es3.enter_context(nc.Block("phase3"))

    @blk2.gpsimd
    def _(g):
        if trunc is not None:
            return
        g.wait_ge(rdone, 1)
        g.wait_ge(s_ys0, 16 * NT // 2)
        g.wait_ge(s_ys1, 16 * NT // 2)
        nblk = 0
        for p in range(NPASS):
            for r in range(TPP):
                gi = p * TPP + r
                pi = gi % 2
                if gi >= 2:
                    g.wait_ge(v_yb, gi - 1)
                g.indirect_dma_start(
                    x_in[:, pi * 1024:(pi + 1) * 1024], None,
                    yscr_d[:, :],
                    bass.IndirectOffsetOnAxis(cyidx[:, gi:gi + 1], 0),
                ).then_inc(s_yr0 if pi == 0 else s_yr1, 16)
            for s in range(3):
                for fcb in range(8):
                    pi = nblk % 2
                    if nblk >= 2:
                        g.wait_ge(t_o3, 4 * TQP * (nblk - 1))
                    for j in range(4):
                        fc = fcb * 4 + j
                        g.indirect_dma_start(
                            wa_buf[:, pi * 4096 + j * 1024: pi * 4096 + (j + 1) * 1024],
                            None, wa_d[:, :],
                            bass.IndirectOffsetOnAxis(
                                idx_i[:, s * 32 + fc:s * 32 + fc + 1], 0),
                        ).then_inc(s_ga0 if pi == 0 else s_ga1, 16)
                        g.indirect_dma_start(
                            wb_buf[:, pi * 4096 + j * 1024: pi * 4096 + (j + 1) * 1024],
                            None, wb_d[:, :],
                            bass.IndirectOffsetOnAxis(
                                idx_i[:, s * 32 + fc:s * 32 + fc + 1], 0),
                        ).then_inc(s_ga0 if pi == 0 else s_ga1, 16)
                    nblk += 1

    @blk2.sync
    def _(sync):
        if trunc is not None:
            return
        EVP = 3 * 8 * TQP * 2
        for p in range(NPASS):
            for r in range(TPP):
                last = p * EVP + ((2 * 8 + 7) * TQP + r // 2) * 2 + (r % 2) + 1
                sync.wait_ge(v_ev, last)
                sync.dma_start(
                    out_d[(p * TPP + r) * 128:(p * TPP + r + 1) * 128, :],
                    in_=acc3[:, r * 1024:(r + 1) * 1024]).then_inc(s_out, 16)
        sync.wait_ge(s_out, 16 * NPASS * TPP)

    @blk2.tensor
    def _(t):
        if trunc is not None:
            return
        blk_cnt = 0
        for p in range(NPASS):
            for r in range(TPP):
                gi = p * TPP + r
                pi = gi % 2
                t.wait_ge(v_u23, gi + 1)
                if gi >= 1:
                    t.wait_ge(v_tr3, 2 * gi)
                for ic in range(8):
                    bank = 3 if ic < 4 else 7
                    inst = t.transpose(PB(bank, (ic % 4) * 128, 128),
                                       u2t3[:, ic * 128:(ic + 1) * 128],
                                       cident[:, :])
                    if ic == 3:
                        inst.then_inc(t_tr3, 1)
                inst.then_inc(t_tr3, 1)
            t.wait_ge(v_tr3, 2 * TPP * (p + 1))
            for s in range(3):
                for fcb in range(8):
                    pi = blk_cnt % 2
                    t.wait_ge(s_ga0 if pi == 0 else s_ga1,
                              128 * (blk_cnt // 2 + 1))
                    for tq in range(TQP):
                        unit = ((p * 3 + s) * 8 + fcb) * TQP + tq
                        for j in range(4):
                            J = unit * 4 + j
                            zb = 0 if J % 2 == 0 else 4
                            if J >= 2:
                                t.wait_ge(a_g, J - 1)
                            for ic in range(8):
                                inst = t.matmul(
                                    PB(zb, 0, 256),
                                    wa_buf[:, pi * 4096 + j * 1024 + ic * 128: pi * 4096 + j * 1024 + (ic + 1) * 128],
                                    u_fm3[:, ic * TPP * 128 + tq * 256: ic * TPP * 128 + tq * 256 + 256],
                                    start=(ic == 0), stop=(ic == 7))
                            inst.then_inc(t_z, 1)
                            t.wait_ge(a_g, J + 1)
                            if j == 0 and unit >= 1:
                                t.wait_ge(v_ev, 2 * unit)
                            for th in range(2):
                                obase = 512 + th * 2048
                                for nh in range(2):
                                    inst = t.matmul(
                                        ps_t[:, obase + nh * 512: obase + (nh + 1) * 512],
                                        g_sb[:, (J % 2) * 256 + th * 128:(J % 2) * 256 + th * 128 + 128],
                                        wb_buf[:, pi * 4096 + j * 1024 + nh * 512: pi * 4096 + j * 1024 + (nh + 1) * 512],
                                        start=(j == 0), stop=(j == 3))
                            inst.then_inc(t_o3, 1)
                    blk_cnt += 1

    @blk2.vector
    def _(v):
        if trunc is not None:
            return
        for p in range(NPASS):
            if p >= 1:
                v.wait_ge(s_out, 16 * TPP * p)
            for r in range(TPP):
                gi = p * TPP + r
                pi = gi % 2
                y_p = x_in[:, pi * 1024:(pi + 1) * 1024]
                u23_p = u2t3[:, 0:1024]
                if gi >= 1:
                    v.wait_ge(t_tr3, 2 * gi)
                v.wait_ge(s_yr0 if pi == 0 else s_yr1, 16 * (gi // 2 + 1))
                if gi >= 2:
                    v.wait_ge(a_ln3, gi - 1)
                v.bn_stats(bnst[:, pi * 12:pi * 12 + 6], y_p[:, 0:512])
                v.bn_stats(bnst[:, pi * 12 + 6:pi * 12 + 12], y_p[:, 512:1024])
                v.drain()
                inst = v.bn_aggr(bnag[:, pi * 2:pi * 2 + 2],
                                 bnst[:, pi * 12:pi * 12 + 12])
                v.drain()
                inst.then_inc(v_var3, 1)
                v.wait_ge(a_ln3, gi + 1)
                v.reciprocal(rstdt[:, pi:pi + 1], sqtt[:, pi:pi + 1])
                v.drain()
                inst = v.tensor_scalar(u23_p, y_p, bnag[:, pi * 2:pi * 2 + 1],
                                       rstdt[:, pi:pi + 1], ALU.subtract, ALU.mult)
                inst.then_inc(v_u23, 1)
                v.drain()
                inst = v.tensor_add(acc3[:, r * 1024:(r + 1) * 1024], y_p, u23_p)
                inst.then_inc(v_yb, 1)
                v.wait_ge(t_tr3, 2 * gi + 1)
                for ic in range(4):
                    v.tensor_copy(u_fm3[:, ic * TPP * 128 + r * 128: ic * TPP * 128 + (r + 1) * 128],
                                  PB(3, ic * 128, 128))
                v.wait_ge(t_tr3, 2 * gi + 2)
                for ic in range(4, 8):
                    inst = v.tensor_copy(
                        u_fm3[:, ic * TPP * 128 + r * 128: ic * TPP * 128 + (r + 1) * 128],
                        PB(7, (ic - 4) * 128, 128))
                inst.then_inc(v_tr3, 2)
            for s in range(3):
                for fcb in range(8):
                    for tq in range(TQP):
                        unit = ((p * 3 + s) * 8 + fcb) * TQP + tq
                        v.wait_ge(t_o3, 4 * (unit + 1))
                        for th in range(2):
                            obase = 512 + th * 2048
                            tile_local = tq * 2 + th
                            inst = v.scalar_tensor_tensor(
                                acc3[:, tile_local * 1024:(tile_local + 1) * 1024],
                                ps_t[:, obase:obase + 1024],
                                w_bc[:, s:s + 1],
                                acc3[:, tile_local * 1024:(tile_local + 1) * 1024],
                                ALU.mult, ALU.add)
                            inst.then_inc(v_ev, 1)

    @blk2.scalar
    def _(a):
        if trunc is not None:
            return
        for p in range(NPASS):
            for r in range(TPP):
                gi = p * TPP + r
                pi = gi % 2
                a.wait_ge(v_var3, gi + 1)
                inst = a.activation(sqtt[:, pi:pi + 1], bnag[:, pi * 2 + 1:pi * 2 + 2],
                                    AF.Sqrt, bias=ceps[:, 0:1])
                inst.then_inc(a_ln3, 1)
            for s in range(3):
                for fcb in range(8):
                    for tq in range(TQP):
                        unit = ((p * 3 + s) * 8 + fcb) * TQP + tq
                        for j in range(4):
                            J = unit * 4 + j
                            zb = 0 if J % 2 == 0 else 4
                            a.wait_ge(t_z, J + 1)
                            if J >= 2:
                                a.wait_ge(t_o3, J - 1)
                            inst = a.activation(
                                g_sb[:, (J % 2) * 256:(J % 2 + 1) * 256],
                                PB(zb, 0, 256), AF.Gelu_apprx_tanh)
                            inst.then_inc(a_g, 1)

    es3.close()
    es.close()
    return nc


def prep_inputs(inputs):
    W_attn = np.asarray(inputs["L0_attn_W_attn"], dtype=np.float32)
    W_proj = np.asarray(inputs["L0_attn_W_proj"], dtype=np.float32)
    W1 = np.asarray(inputs["L0_ffn_W1"], dtype=np.float32)
    b1 = np.asarray(inputs["L0_ffn_b1"], dtype=np.float32)
    W2 = np.asarray(inputs["L0_ffn_W2"], dtype=np.float32)
    b2 = np.asarray(inputs["L0_ffn_b2"], dtype=np.float32)
    Wa = np.asarray(inputs["L0_ffn_Wa"], dtype=np.float32)
    Wb = np.asarray(inputs["L0_ffn_Wb"], dtype=np.float32)

    d = {}
    d["wq_t"] = np.ascontiguousarray(
        W_attn[0:1024].T.reshape(8, 128, 8, 128).transpose(1, 0, 2, 3).reshape(128, 8192))
    d["wk_t"] = np.ascontiguousarray(
        W_attn[1024:2048].T.reshape(8, 128, 1024).transpose(1, 0, 2).reshape(128, 8192))
    d["wv_t"] = np.ascontiguousarray(
        W_attn[2048:3072].T.reshape(8, 128, 1024).transpose(1, 0, 2).reshape(128, 8192))
    d["wp_t"] = np.ascontiguousarray(
        W_proj.T.reshape(8, 128, 1024).transpose(1, 0, 2).reshape(128, 8192))
    d["w1_t"] = np.ascontiguousarray(
        W1.reshape(8, 128, 2, 128).transpose(1, 0, 2, 3).reshape(128, 2048))
    d["b1_t"] = np.ascontiguousarray(b1.reshape(2, 128).T)
    d["w2_t"] = np.ascontiguousarray(
        W2.reshape(2, 128, 8).transpose(1, 0, 2).reshape(128, 16))
    d["b2_t"] = np.ascontiguousarray(b2.reshape(1, 8))
    d["wa_r"] = np.ascontiguousarray(
        Wa.reshape(8, 8, 128, 32, 128).transpose(0, 3, 2, 1, 4).reshape(32768, 1024))
    d["wb_r"] = np.ascontiguousarray(Wb.reshape(32768, 1024))
    d["cident"] = np.eye(128, dtype=np.float32)
    t_ar = np.arange(128)
    j_ar = np.arange(16)
    m = np.where(j_ar[None, :] <= t_ar[:, None], 0.0, NEG).astype(np.float32)
    d["cmask"] = np.ascontiguousarray(np.tile(m, (1, 16)))
    p_ar = np.arange(128, dtype=np.float32)
    d["ciota_fc"] = np.ascontiguousarray(
        (p_ar[:, None] + 128.0 * np.arange(32, dtype=np.float32)[None, :]).astype(np.float32))
    d["cones_r"] = np.ones((1, 128), dtype=np.float32)
    d["cones_c"] = np.ones((128, 1), dtype=np.float32)
    i8 = np.arange(8)
    d["clow8"] = (i8[None, :] < i8[:, None]).astype(np.float32)
    d["cup8"] = (i8[:, None] < i8[None, :]).astype(np.float32)
    d["ciota_p"] = p_ar.reshape(128, 1).astype(np.float32)
    d["ceps"] = np.full((128, 1), EPS, dtype=np.float32)
    return d


def core_inputs(inputs, shared, c):
    x = np.asarray(inputs["L0_attn_x"], dtype=np.float32)
    b, half = c // 2, c % 2
    m = dict(shared)
    m["x"] = np.ascontiguousarray(x[b].reshape(8192, 1024))
    base = half * 4096
    rows = (base + np.arange(4096)).reshape(32, 128).T
    m["cyidx"] = np.ascontiguousarray(rows.astype(np.int32))
    return m


from concourse.bass_utils import run_bass_kernel_spmd


def kernel(**inputs):
    shared = prep_inputs(inputs)
    in_maps = [core_inputs(inputs, shared, c) for c in range(8)]
    nc = build_nc()
    res = run_bass_kernel_spmd(nc, in_maps, core_ids=list(range(8)))
    out = np.empty((4, 8192, 1024), dtype=np.float32)
    for c in range(8):
        b, half = c // 2, c % 2
        out[b, half * 4096:(half + 1) * 4096, :] = res.results[c]["out"]
    return out


# version 08
# speedup vs baseline: 1.2372x; 1.0430x over previous
import contextlib
import numpy as np
import concourse.bass as bass
import concourse.mybir as mybir

F32 = mybir.dt.float32
I32 = mybir.dt.int32
AF = mybir.ActivationFunctionType
ALU = mybir.AluOpType
AX = mybir.AxisListType

NT = 64
NPASS = 2
TPP = 16
TQP = 8
EPS = 1e-5
NEG = -1.0e30


def build_nc(debug=False, pipe=True, pool_short=False, trunc=None):
    nc = bass.Bass("TRN2", target_bir_lowering=False)

    x_d = nc.dram_tensor("x", [8192, 1024], F32, kind="ExternalInput").ap()
    wq_d = nc.dram_tensor("wq_t", [128, 8192], F32, kind="ExternalInput").ap()
    wk_d = nc.dram_tensor("wk_t", [128, 8192], F32, kind="ExternalInput").ap()
    wv_d = nc.dram_tensor("wv_t", [128, 8192], F32, kind="ExternalInput").ap()
    wp_d = nc.dram_tensor("wp_t", [128, 8192], F32, kind="ExternalInput").ap()
    w1_d = nc.dram_tensor("w1_t", [128, 2048], F32, kind="ExternalInput").ap()
    b1_d = nc.dram_tensor("b1_t", [128, 2], F32, kind="ExternalInput").ap()
    w2_d = nc.dram_tensor("w2_t", [128, 16], F32, kind="ExternalInput").ap()
    b2_d = nc.dram_tensor("b2_t", [1, 8], F32, kind="ExternalInput").ap()
    wa_d = nc.dram_tensor("wa_r", [32768, 1024], F32, kind="ExternalInput").ap()
    wb_d = nc.dram_tensor("wb_r", [32768, 1024], F32, kind="ExternalInput").ap()
    cid_d = nc.dram_tensor("cident", [128, 128], F32, kind="ExternalInput").ap()
    cmask_d = nc.dram_tensor("cmask", [128, 256], F32, kind="ExternalInput").ap()
    ciofc_d = nc.dram_tensor("ciota_fc", [128, 32], F32, kind="ExternalInput").ap()
    conr_d = nc.dram_tensor("cones_r", [1, 128], F32, kind="ExternalInput").ap()
    conc_d = nc.dram_tensor("cones_c", [128, 1], F32, kind="ExternalInput").ap()
    clow_d = nc.dram_tensor("clow8", [8, 8], F32, kind="ExternalInput").ap()
    cup_d = nc.dram_tensor("cup8", [8, 8], F32, kind="ExternalInput").ap()
    ciop_d = nc.dram_tensor("ciota_p", [128, 1], F32, kind="ExternalInput").ap()
    cyidx_d = nc.dram_tensor("cyidx", [128, 32], I32, kind="ExternalInput").ap()
    ceps_d = nc.dram_tensor("ceps", [128, 1], F32, kind="ExternalInput").ap()

    yscr_d = nc.dram_tensor("y_scr", [8192, 1024], F32, kind="Internal").ap()
    out_d = nc.dram_tensor("out", [4096, 1024], F32, kind="ExternalOutput").ap()
    if debug:
        dbg_d = nc.dram_tensor("dbg", [128, 16], F32, kind="ExternalOutput").ap()

    es = contextlib.ExitStack()
    sb = lambda name, shape, dt=F32: es.enter_context(nc.sbuf_tensor(name, shape, dt))
    sem = lambda name: es.enter_context(nc.semaphore(name))

    cident = sb("s_cident", [128, 128])

    ciofc = sb("s_ciota_fc", [128, 32])
    conc = sb("s_cones_c", [128, 1])
    clow8 = sb("s_clow8", [8, 8])
    cup8 = sb("s_cup8", [8, 8])
    ciop = sb("s_ciota_p", [128, 1])
    cyidx = sb("s_cyidx", [128, 32], I32)
    ceps = sb("s_ceps", [128, 1])
    b2r = sb("s_b2r", [1, 8])

    x_in = sb("x_in", [128, 2048])
    bnst = sb("bnst", [128, 24])
    bnag = sb("bnag", [128, 8])
    rstdt = sb("rstdt", [128, 4])
    sqtt = sb("sqtt", [128, 4])

    w_bc = sb("w_bc", [128, 3])
    idx_i = sb("idx_i", [128, 96], I32)

    ps_t = es.enter_context(nc.psum_tensor("ps", [128, 4096], F32))

    def PB(bank, lo=0, n=512, p0=0, np_=128):
        return ps_t[p0:p0 + np_, bank * 512 + lo: bank * 512 + lo + n]

    s_xs = [sem(f"s_x{i}") for i in range(4)]
    s_wq, s_wkv, s_wp, s_cst = sem("s_wq"), sem("s_wkv"), sem("s_wp"), sem("s_cst")
    s_ys0, s_ys1 = sem("s_ys0"), sem("s_ys1")
    s_yr0, s_yr1 = sem("s_yr0"), sem("s_yr1")
    s_ga0, s_ga1 = sem("s_ga0"), sem("s_ga1")
    s_out = sem("s_out")

    a_sq = sem("a_sq")
    v_var1, a_ln1, v_u1 = sem("v_var1"), sem("a_ln1"), sem("v_u1")
    t_tr, v_tr = sem("t_tr"), sem("v_tr")
    t_q, v_q = sem("t_q"), sem("v_q")
    t_kv, v_kv = sem("t_kv"), sem("v_kv")
    t_lg = sem("t_lg")
    v_sub, a_exp, v_w = sem("v_sub"), sem("a_exp"), sem("v_w")
    t_wt, v_wt, t_oh = sem("t_wt"), sem("v_wt"), sem("t_oh")
    t_o, v_o = sem("t_o"), sem("v_o")
    t_proj, v_y = sem("t_proj"), sem("v_y")
    v_var2, a_ln2, v_u2 = sem("v_var2"), sem("a_ln2"), sem("v_u2")
    t_pool, v_pool = sem("t_pool"), sem("v_pool")
    r_pe, r_dv, r_ac, rdone = sem("r_pe"), sem("r_dv"), sem("r_ac"), sem("rdone")
    v_yb = sem("v_yb")
    v_u23, t_tr3, v_tr3 = sem("v_u23"), sem("t_tr3"), sem("v_tr3")
    a_sq3, v_var3, a_ln3 = sem("a_sq3"), sem("v_var3"), sem("a_ln3")
    t_z, a_g, t_o3, v_ev = sem("t_z"), sem("a_g"), sem("t_o3"), sem("v_ev")
    t_wfree = sem("t_wfree")

    NCST = 14

    def BK(k, i):
        return (k % 2) * 4 + i

    es1 = contextlib.ExitStack()
    sb1 = lambda name, shape, dt=F32: es1.enter_context(nc.sbuf_tensor(name, shape, dt))
    wq_s = sb1("wq_s", [128, 8192])
    wp_s = sb1("wp_s", [128, 8192])
    wkv_s = sb1("wkv_s", [128, 4096])
    u1 = sb1("u1", [128, 4096])
    u1f = sb1("u1f", [128, 2048])
    qf = sb1("qf", [128, 2048])
    k16f = sb1("k16f", [128, 128])
    k16t = sb1("k16t", [16, 1024])
    v16t = sb1("v16t", [16, 1024])
    msub = sb1("msub", [128, 512])
    mxg = sb1("mxg", [128, 32])
    exw = sb1("exw", [128, 512])
    esum = sb1("esum", [128, 32])
    erc = sb1("erc", [128, 32])
    w_tm = sb1("w_tm", [128, 512])
    wTh = sb1("wTh", [16, 2048])
    o_f = sb1("o_f", [128, 2048])
    y_t = sb1("y_t", [128, 2048])
    u2 = sb1("u2", [128, 2048])
    x_cd = sb1("x_cd", [128, 2048])
    cmask = sb1("s_cmask", [128, 256])
    w1f = sb1("s_w1f", [128, 2048])
    pooled_acc = sb1("pooled_acc", [1, 1024])
    pool_tmp = sb1("pool_tmp", [1, 1024])
    conr = sb1("s_cones_r", [1, 128])
    b1f = sb1("s_b1f", [128, 2])
    w2f = sb1("s_w2f", [128, 16])
    r_pool_f = sb1("r_pool_f", [128, 8])
    r_h = sb1("r_h", [128, 2])
    r_logit = sb1("r_logit", [1, 8])
    r_mx = sb1("r_mx", [1, 1])
    r_mxn = sb1("r_mxn", [1, 1])
    r_ex = sb1("r_ex", [1, 8])
    r_es = sb1("r_es", [1, 1])
    r_er = sb1("r_er", [1, 1])
    r_probs = sb1("r_probs", [1, 8])
    r_A = sb1("r_A", [8, 8])
    r_B = sb1("r_B", [8, 8])
    r_gt = sb1("r_gt", [8, 8])
    r_rank = sb1("r_rank", [8, 1])
    r_sel = sb1("r_sel", [8, 1])
    r_wun = sb1("r_wun", [8, 1])
    r_rZ = sb1("r_rZ", [1, 1])
    r_oh = sb1("r_oh", [8, 3])
    r_wsel = sb1("r_wsel", [1, 3])
    r_id = sb1("r_id", [1, 3])
    e_bc = sb1("e_bc", [128, 3])
    idx_f = sb1("idx_f", [128, 96])

    blk1 = es1.enter_context(nc.Block("phase1"))

    @blk1.sync
    def _(sync):
        for dst, src in [(cident, cid_d), (cmask, cmask_d), (ciofc, ciofc_d),
                         (conr, conr_d), (conc, conc_d), (clow8, clow_d),
                         (cup8, cup_d), (ciop, ciop_d), (cyidx, cyidx_d),
                         (ceps, ceps_d),
                         (b2r, b2_d), (b1f, b1_d), (w1f, w1_d), (w2f, w2_d)]:
            sync.dma_start(out=dst[:, :], in_=src[:, :]).then_inc(s_cst, 16)
        sync.dma_start(out=x_in[:, 0:1024], in_=x_d[0:128, :]).then_inc(s_xs[0], 16)
        sync.dma_start(out=x_in[:, 1024:2048], in_=x_d[128:256, :]).then_inc(s_xs[1], 16)
        sync.dma_start(out=wq_s[:, :], in_=wq_d[:, :]).then_inc(s_wq, 16)
        sync.dma_start(out=wkv_s[:, :], in_=wk_d[:, 0:4096]).then_inc(s_wkv, 16)
        sync.dma_start(out=wp_s[:, :], in_=wp_d[:, :]).then_inc(s_wp, 16)
        sync.wait_ge(t_kv, 1)
        sync.dma_start(out=wkv_s[:, :], in_=wk_d[:, 4096:8192]).then_inc(s_wkv, 16)
        sync.wait_ge(t_kv, 2)
        sync.dma_start(out=wkv_s[:, :], in_=wv_d[:, 0:4096]).then_inc(s_wkv, 16)
        sync.wait_ge(t_kv, 3)
        sync.dma_start(out=wkv_s[:, :], in_=wv_d[:, 4096:8192]).then_inc(s_wkv, 16)
        for k in range(2, NT if trunc is None else trunc):
            sl = k % 4
            if k >= 4:
                sync.wait_ge(v_y, k - 3)
            xbuf = x_in if sl < 2 else x_cd
            sync.dma_start(out=xbuf[:, (sl % 2) * 1024:(sl % 2 + 1) * 1024],
                           in_=x_d[k * 128:(k + 1) * 128, :]).then_inc(
                               s_xs[sl], 16)
        if trunc is not None:
            sync.wait_ge(v_y, trunc)
            sync.dma_start(out=out_d[0:128, :],
                           in_=y_t[:, 0:1024]).then_inc(s_out, 16)
            sync.wait_ge(s_out, 16)

    @blk1.gpsimd
    def _(g):
        for k in range(NT if trunc is None else trunc):
            pi = k % 2
            g.wait_ge(v_y, k + 1)
            g.dma_start(yscr_d[k * 128:(k + 1) * 128, :],
                        y_t[:, pi * 1024:(pi + 1) * 1024]).then_inc(
                            s_ys0 if pi == 0 else s_ys1, 16)

    @blk1.tensor
    def _(t):
        t.wait_ge(s_cst, 16 * NCST)

        def AA(k):
            return (k % 2) * 4

        def tr_stage(k):
            pi = k % 2
            A = AA(k)
            u1_p = u1[:, (k % 4) * 1024:(k % 4 + 1) * 1024]
            t.wait_ge(v_u1, k + 1)
            if k >= 2:
                t.wait_ge(v_y, k - 1)
            for ic in range(8):
                bank = A if ic < 4 else A + 1
                inst = t.transpose(PB(bank, (ic % 4) * 128, 128),
                                   u1_p[:, ic * 128:(ic + 1) * 128], cident[:, :])
            inst.then_inc(t_tr, 1)

        def q_stage(k):
            pi = k % 2
            A = AA(k)
            u1f_p = u1f[:, pi * 1024:(pi + 1) * 1024]
            if k == 0:
                t.wait_ge(s_wq, 16)
            t.wait_ge(v_tr, k + 1)
            if k >= 2:
                t.wait_ge(v_y, k - 1)
            for oc in range(8):
                bank = A + 2 if oc < 4 else A
                lo = (oc % 4) * 128
                for ic in range(8):
                    inst = t.matmul(PB(bank, lo, 128),
                                    wq_s[:, ic * 1024 + oc * 128: ic * 1024 + (oc + 1) * 128],
                                    u1f_p[:, ic * 128:(ic + 1) * 128],
                                    start=(ic == 0), stop=(ic == 7))
            inst.then_inc(t_q, 1)

        def kv_block():
            t.wait_ge(v_q, 1)
            for part in range(4):
                t.wait_ge(s_wkv, 16 * (part + 1))
                if part == 2:
                    t.wait_ge(v_kv, 1)
                for half in range(2):
                    for icl in range(4):
                        ic = (part % 2) * 4 + icl
                        inst = t.matmul(PB(1 + half, 0, 512, 0, 16),
                                        u1f[:, ic * 128: ic * 128 + 16],
                                        wkv_s[:, icl * 1024 + half * 512: icl * 1024 + (half + 1) * 512],
                                        start=(ic == 0), stop=(ic == 7))
                inst.then_inc(t_kv, 1)
            t.wait_ge(v_kv, 2)
            t.wait_ge(v_tr, 1)
            for ic in range(8):
                inst = t.transpose(PB(0, ic * 16, 16),
                                   k16t[0:16, ic * 128:(ic + 1) * 128],
                                   cident[0:16, 0:16])
            inst.then_inc(t_kv, 1)
            t.wait_ge(v_kv, 3)

        def logits_stage(k):
            pi = k % 2
            A = AA(k)
            qf_p = qf[:, pi * 1024:(pi + 1) * 1024]
            t.wait_ge(v_q, k + 1)
            for h in range(16):
                po = (h % 2) * 64
                ch = h // 2
                inst = t.matmul(PB(A + 1, h * 16, 16),
                                qf_p[po:po + 64, ch * 128:(ch + 1) * 128],
                                k16f[po:po + 64, ch * 16:(ch + 1) * 16],
                                start=True, stop=True)
            inst.then_inc(t_lg, 1)

        def pool_stage(k):
            pi = k % 2
            u2_pp = u2[:, pi * 1024:(pi + 1) * 1024]
            t.wait_ge(v_u2, k + 1)
            if pool_short:
                t.wait_ge(v_pool, k)
            st_ = (k == 0) if not pool_short else True
            sp_ = (k == NT - 1) if not pool_short else True
            inst = t.matmul(PB(3, 0, 512, 0, 1), conc[:, :], u2_pp[:, 0:512],
                            start=st_, stop=sp_)
            inst = t.matmul(PB(7, 0, 512, 0, 1), conc[:, :], u2_pp[:, 512:1024],
                            start=st_, stop=sp_)
            inst.then_inc(t_pool, 1)

        def wT_o_proj_stage(k, wt_base):
            pi = k % 2
            A = AA(k)
            wtm_p = w_tm[:, pi * 256:(pi + 1) * 256]
            of_p = o_f[:, pi * 1024:(pi + 1) * 1024]
            t.wait_ge(v_w, k + 1)
            t.wait_ge(v_sub, k + 1)
            for hh in range(8):
                bank = A + 2 if hh < 4 else A
                inst = t.transpose(PB(bank, (hh % 4) * 128, 128, 0, 16),
                                   wtm_p[:, hh * 16:(hh + 1) * 16], cident[:, :])
            inst.then_inc(t_wt, 1)
            if k >= 1:
                pool_stage(k - 1)
            t.wait_ge(v_wt, wt_base + 1)
            for hh in range(8):
                h = 8 + hh
                bank = A + 2 if hh < 4 else A
                inst = t.transpose(PB(bank, (hh % 4) * 128, 128, 0, 16),
                                   wtm_p[:, h * 16:(h + 1) * 16], cident[:, :])
            inst.then_inc(t_wt, 1)
            for h in range(8):
                p8 = h // 2
                inst = t.matmul(PB(A + 1, p8 * 128, 128, (h % 2) * 64, 64),
                                v16t[0:16, h * 64:(h + 1) * 64],
                                wTh[0:16, h * 128:(h + 1) * 128],
                                start=True, stop=True)
            inst.then_inc(t_oh, 1)
            t.wait_ge(v_wt, wt_base + 2)
            for h in range(8, 16):
                p8 = h // 2 - 4
                inst = t.matmul(PB(A + 2, p8 * 128, 128, (h % 2) * 64, 64),
                                v16t[0:16, h * 64:(h + 1) * 64],
                                wTh[0:16, h * 128:(h + 1) * 128],
                                start=True, stop=True)
            inst.then_inc(t_oh, 1)
            if k == 0:
                t.wait_ge(s_wp, 16)
            t.wait_ge(v_o, k + 1)
            for half in range(2):
                bank = A + 1 if half == 0 else A + 2
                for ic in range(8):
                    inst = t.matmul(PB(bank, 0, 512),
                                    of_p[:, ic * 128:(ic + 1) * 128],
                                    wp_s[:, ic * 1024 + half * 512: ic * 1024 + (half + 1) * 512],
                                    start=(ic == 0), stop=(ic == 7))
            inst.then_inc(t_proj, 1)

        NR = NT if trunc is None else trunc
        if pipe:
            tr_stage(0)
            q_stage(0)
            kv_block()
            logits_stage(0)
            for k in range(NR):
                if k + 1 < NR:
                    tr_stage(k + 1)
                    q_stage(k + 1)
                    logits_stage(k + 1)
                wT_o_proj_stage(k, 2 * k)
            if trunc is None:
                pool_stage(NT - 1)
        else:
            for k in range(NT):
                tr_stage(k)
                q_stage(k)
                if k == 0:
                    kv_block()
                logits_stage(k)
                wT_o_proj_stage(k, 2 * k)
            pool_stage(NT - 1)

        if trunc is not None:
            return
        t.wait_ge(r_dv, 1)
        for ic in range(8):
            inst = t.matmul(PB(0, ic, 1), pool_tmp[0:1, ic * 128:(ic + 1) * 128],
                            conc[0:1, 0:1], start=True, stop=True)
        inst.then_inc(r_pe, 1)
        t.wait_ge(r_dv, 2)
        for fc2 in range(2):
            for ic in range(8):
                inst = t.matmul(PB(1, fc2, 1),
                                w1f[:, ic * 256 + fc2 * 128: ic * 256 + (fc2 + 1) * 128],
                                r_pool_f[:, ic:ic + 1],
                                start=(ic == 0), stop=(ic == 7))
        inst.then_inc(r_pe, 1)
        t.wait_ge(r_ac, 1)
        for fc2 in range(2):
            inst = t.matmul(PB(2, 0, 8, 0, 1), r_h[:, fc2:fc2 + 1],
                            w2f[:, fc2 * 8:(fc2 + 1) * 8],
                            start=(fc2 == 0), stop=(fc2 == 1))
        inst.then_inc(r_pe, 1)
        t.wait_ge(r_dv, 4)
        inst = t.matmul(PB(3, 0, 8, 0, 8), conr[0:1, 0:8], r_probs[0:1, :],
                        start=True, stop=True)
        inst = t.matmul(PB(3, 8, 8, 0, 8), r_probs[0:1, :], conr[0:1, 0:8],
                        start=True, stop=True)
        inst.then_inc(r_pe, 1)
        t.wait_ge(r_dv, 5)
        inst = t.matmul(PB(4, 0, 1, 0, 1), r_wun[:, :], conc[0:8, 0:1],
                        start=True, stop=True)
        inst = t.matmul(PB(4, 1, 1, 0, 8), cup8[:, :], r_sel[:, :],
                        start=True, stop=True)
        inst.then_inc(r_pe, 1)
        t.wait_ge(r_dv, 6)
        inst = t.matmul(PB(5, 0, 3, 0, 1), r_wun[:, :], r_oh[:, :],
                        start=True, stop=True)
        inst = t.matmul(PB(5, 3, 3, 0, 1), ciop[0:8, 0:1], r_oh[:, :],
                        start=True, stop=True)
        inst.then_inc(r_pe, 1)
        t.wait_ge(r_dv, 7)
        inst = t.matmul(PB(6, 0, 3), conr[:, :], r_wsel[0:1, :],
                        start=True, stop=True)
        inst = t.matmul(PB(6, 3, 3), conr[:, :], r_id[0:1, :],
                        start=True, stop=True)
        inst.then_inc(r_pe, 1)

    @blk1.vector
    def _(v):
        v.wait_ge(s_cst, 16 * NCST)

        def AA(k):
            return (k % 2) * 4

        def ln(dst, src_x, pi, base, sq, var_sem, var_val, ln_sem, u_sem):
            if var_val >= 3:
                v.wait_ge(ln_sem, var_val - 2)
            v.bn_stats(bnst[:, pi * 12:pi * 12 + 6], src_x[:, 0:512])
            v.bn_stats(bnst[:, pi * 12 + 6:pi * 12 + 12], src_x[:, 512:1024])
            v.drain()
            inst = v.bn_aggr(bnag[:, base:base + 2], bnst[:, pi * 12:pi * 12 + 12])
            v.drain()
            inst.then_inc(var_sem, 1)
            v.wait_ge(ln_sem, var_val)
            v.reciprocal(rstdt[:, sq:sq + 1], sqtt[:, sq:sq + 1])
            v.drain()
            inst = v.tensor_scalar(dst, src_x, bnag[:, base:base + 1],
                                   rstdt[:, sq:sq + 1], ALU.subtract, ALU.mult)
            inst.then_inc(u_sem, 1)

        def ln1(k):
            pi = k % 2
            sl = k % 4
            xbuf = x_in if sl < 2 else x_cd
            v.wait_ge(s_xs[sl], 16 * (k // 4 + 1))
            ln(u1[:, sl * 1024:(sl + 1) * 1024],
               xbuf[:, (sl % 2) * 1024:(sl % 2 + 1) * 1024],
               pi, pi * 2, pi, v_var1, k + 1, a_ln1, v_u1)

        def kv_evacs():
            v.wait_ge(t_kv, 2)
            v.tensor_copy(k16t[0:16, 0:512], PB(1, 0, 512, 0, 16))
            inst = v.tensor_copy(k16t[0:16, 512:1024], PB(2, 0, 512, 0, 16))
            inst.then_inc(v_kv, 1)
            v.wait_ge(t_kv, 4)
            v.tensor_copy(v16t[0:16, 0:512], PB(1, 0, 512, 0, 16))
            inst = v.tensor_copy(v16t[0:16, 512:1024], PB(2, 0, 512, 0, 16))
            inst.then_inc(v_kv, 1)
            v.wait_ge(t_kv, 5)
            inst = v.tensor_copy(k16f[:, :], PB(0, 0, 128))
            inst.then_inc(v_kv, 1)

        def softmax(k):
            pi = k % 2
            A = AA(k)
            v.wait_ge(t_lg, k + 1)
            lg_ap = bass.AP(ps_t, (A + 1) * 512, [[4096, 128], [16, 16], [1, 16]])
            ms_g = bass.AP(msub, pi * 256, [[512, 128], [16, 16], [1, 16]])
            mx_b = bass.AP(mxg, pi * 16, [[32, 128], [1, 16], [0, 16]])
            if k == 0:
                v.tensor_add(msub[:, pi * 256:(pi + 1) * 256],
                             PB(A + 1, 0, 256), cmask[:, :])
            else:
                v.tensor_copy(msub[:, pi * 256:(pi + 1) * 256], PB(A + 1, 0, 256))
            v.drain()
            v.reduce_max(mxg[:, pi * 16:(pi + 1) * 16], ms_g, AX.X)
            v.drain()
            inst = v.tensor_tensor(ms_g, ms_g, mx_b, ALU.subtract)
            inst.then_inc(v_sub, 1)
            v.wait_ge(a_exp, k + 1)
            ex_gr = bass.AP(exw, pi * 256, [[512, 128], [16, 16], [1, 16]])
            v.tensor_reduce(esum[:, pi * 16:(pi + 1) * 16], ex_gr, AX.X, ALU.add)
            v.drain()
            v.reciprocal(erc[:, pi * 16:(pi + 1) * 16], esum[:, pi * 16:(pi + 1) * 16])
            v.drain()
            er_b = bass.AP(erc, pi * 16, [[32, 128], [1, 16], [0, 16]])
            ex_g = bass.AP(exw, pi * 256, [[512, 128], [16, 16], [1, 16]])
            w_g = bass.AP(w_tm, pi * 256, [[512, 128], [16, 16], [1, 16]])
            inst = v.tensor_tensor(w_g, ex_g, er_b, ALU.mult)
            inst.then_inc(v_w, 1)

        def wT_evacs(k, wt_base):
            A = AA(k)
            for grp in range(2):
                v.wait_ge(t_wt, wt_base + grp + 1)
                for hh in range(8):
                    h = grp * 8 + hh
                    bank = A + 2 if hh < 4 else A
                    inst = v.tensor_copy(wTh[0:16, h * 128:(h + 1) * 128],
                                         PB(bank, (hh % 4) * 128, 128, 0, 16))
                inst.then_inc(v_wt, 1)

        def y_ln2(k):
            pi = k % 2
            A = AA(k)
            sl = k % 4
            x_p = (x_in if sl < 2 else x_cd)[:, (sl % 2) * 1024:(sl % 2 + 1) * 1024]
            y_p = y_t[:, pi * 1024:(pi + 1) * 1024]
            u2_p = u2[:, pi * 1024:(pi + 1) * 1024]
            v.wait_ge(t_proj, k + 1)
            if k >= 2:
                v.wait_ge(s_ys0 if pi == 0 else s_ys1, 16 * (k // 2))
            v.tensor_add(y_p[:, 0:512], x_p[:, 0:512], PB(A + 1, 0, 512))
            inst = v.tensor_add(y_p[:, 512:1024], x_p[:, 512:1024], PB(A + 2, 0, 512))
            inst.then_inc(v_y, 1)
            v.drain()
            ln(u2_p, y_p, pi, 4 + pi * 2, 2 + pi, v_var2, k + 1, a_ln2, v_u2)
            if pool_short:
                v.wait_ge(t_pool, k + 1)
                if k == 0:
                    v.tensor_copy(pooled_acc[0:1, 0:512], PB(3, 0, 512, 0, 1))
                    inst = v.tensor_copy(pooled_acc[0:1, 512:1024], PB(7, 0, 512, 0, 1))
                else:
                    v.tensor_add(pooled_acc[0:1, 0:512], pooled_acc[0:1, 0:512],
                                 PB(3, 0, 512, 0, 1))
                    inst = v.tensor_add(pooled_acc[0:1, 512:1024],
                                        pooled_acc[0:1, 512:1024],
                                        PB(7, 0, 512, 0, 1))
                inst.then_inc(v_pool, 1)

        NR = NT if trunc is None else trunc
        if pipe:
            ln1(0)
            ln1(1)
            kv_evacs()
            for k in range(NR):
                softmax(k)
                wT_evacs(k, 2 * k)
                if k + 2 < NR:
                    ln1(k + 2)
                y_ln2(k)
        else:
            ln1(0)
            for k in range(NT):
                tr_evac(k)
                q_evac(k)
                if k == 0:
                    kv_evacs()
                softmax(k)
                if k + 1 < NT:
                    ln1(k + 1)
                wT_evacs(k, 2 * k)
                o_evac(k)
                y_ln2(k)

        if trunc is not None:
            return
        if pool_short:
            pass
        v.wait_ge(t_pool, NT)
        if pool_short:
            v.wait_ge(v_pool, NT)
            v.drain()
            v.tensor_scalar_mul(pool_tmp[0:1, :], pooled_acc[0:1, :], 1.0 / 8192.0)
            inst = v.drain()
        else:
            v.drain()
            v.tensor_scalar_mul(pool_tmp[0:1, 0:512], PB(3, 0, 512, 0, 1), 1.0 / 8192.0)
            inst = v.tensor_scalar_mul(pool_tmp[0:1, 512:1024], PB(7, 0, 512, 0, 1),
                                       1.0 / 8192.0)
            v.drain()
        inst.then_inc(r_dv, 1)
        v.wait_ge(r_pe, 1)
        inst = v.tensor_copy(r_pool_f[:, :], PB(0, 0, 8))
        inst.then_inc(r_dv, 1)
        v.wait_ge(r_pe, 3)
        v.tensor_add(r_logit[0:1, :], PB(2, 0, 8, 0, 1), b2r[0:1, :])
        v.drain()
        v.reduce_max(r_mx[0:1, :], r_logit[0:1, :], AX.X)
        v.drain()
        inst = v.tensor_scalar_mul(r_mxn[0:1, :], r_mx[0:1, :], -1.0)
        inst.then_inc(r_dv, 1)
        v.wait_ge(r_ac, 2)
        v.reciprocal(r_er[0:1, :], r_es[0:1, :])
        v.drain()
        v.tensor_scalar(r_probs[0:1, :], r_ex[0:1, :], r_er[0:1, 0:1], None, ALU.mult)
        inst = v.drain()
        inst.then_inc(r_dv, 1)
        v.wait_ge(r_pe, 4)
        v.tensor_copy(r_A[:, :], PB(3, 0, 8, 0, 8))
        v.tensor_copy(r_B[:, :], PB(3, 8, 8, 0, 8))
        v.drain()
        v.tensor_tensor(r_gt[:, :], r_A[:, :], r_B[:, :], ALU.is_gt)
        v.drain()
        v.tensor_tensor(r_A[:, :], r_A[:, :], r_B[:, :], ALU.is_equal)
        v.drain()
        v.tensor_mul(r_A[:, :], r_A[:, :], clow8[:, :])
        v.drain()
        v.tensor_add(r_gt[:, :], r_gt[:, :], r_A[:, :])
        v.drain()
        v.tensor_reduce(r_rank[:, :], r_gt[:, :], AX.X, ALU.add)
        v.drain()
        v.tensor_single_scalar(r_sel[:, :], r_rank[:, :], 3.0, ALU.is_lt)
        v.drain()
        v.tensor_mul(r_wun[:, :], r_sel[:, :], r_B[:, 0:1])
        inst = v.drain()
        inst.then_inc(r_dv, 1)
        v.wait_ge(r_pe, 5)
        v.reciprocal(r_rZ[0:1, :], PB(4, 0, 1, 0, 1))
        for s3 in range(3):
            v.tensor_single_scalar(r_oh[:, s3:s3 + 1], PB(4, 1, 1, 0, 8),
                                   float(s3), ALU.is_equal)
        v.drain()
        sel_b = bass.AP(r_sel, 0, [[1, 8], [0, 3]])
        v.tensor_tensor(r_oh[:, :], r_oh[:, :], sel_b, ALU.mult)
        inst = v.drain()
        inst.then_inc(r_dv, 1)
        v.wait_ge(r_pe, 6)
        v.tensor_scalar(r_wsel[0:1, :], PB(5, 0, 3, 0, 1), r_rZ[0:1, 0:1], None,
                        ALU.mult)
        v.tensor_copy(r_id[0:1, :], PB(5, 3, 3, 0, 1))
        inst = v.drain()
        inst.then_inc(r_dv, 1)
        v.wait_ge(r_pe, 7)
        v.tensor_copy(w_bc[:, :], PB(6, 0, 3))
        v.tensor_copy(e_bc[:, :], PB(6, 3, 3))
        v.drain()
        v.tensor_scalar_mul(e_bc[:, :], e_bc[:, :], 4096.0)
        v.drain()
        for s3 in range(3):
            v.tensor_scalar(idx_f[:, s3 * 32:(s3 + 1) * 32], ciofc[:, :],
                            e_bc[:, s3:s3 + 1], None, ALU.add)
        v.drain()
        v.tensor_copy(idx_i[:, :], idx_f[:, :])
        inst = v.drain()
        inst.then_inc(rdone, 1)

    @blk1.scalar
    def _(a):
        a.wait_ge(s_cst, 16 * NCST)
        NRs = NT if trunc is None else trunc

        def AAs(k):
            return (k % 2) * 4

        def sqrt1(k):
            pi = k % 2
            a.wait_ge(v_var1, k + 1)
            inst = a.activation(sqtt[:, pi:pi + 1], bnag[:, pi * 2 + 1:pi * 2 + 2],
                                AF.Sqrt, bias=ceps[:, 0:1])
            inst.then_inc(a_ln1, 1)

        def tr_evac(k):
            pi = k % 2
            A = AAs(k)
            u1f_p = u1f[:, pi * 1024:(pi + 1) * 1024]
            a.wait_ge(t_tr, k + 1)
            a.activation(u1f_p[:, 0:512], PB(A, 0, 512), AF.Copy)
            inst = a.activation(u1f_p[:, 512:1024], PB(A + 1, 0, 512), AF.Copy)
            inst.then_inc(v_tr, 1)

        def q_evac(k):
            pi = k % 2
            A = AAs(k)
            qf_p = qf[:, pi * 1024:(pi + 1) * 1024]
            a.wait_ge(t_q, k + 1)
            a.activation(qf_p[:, 0:512], PB(A + 2, 0, 512), AF.Copy)
            inst = a.activation(qf_p[:, 512:1024], PB(A, 0, 512), AF.Copy)
            inst.then_inc(v_q, 1)

        def o_evac(k):
            pi = k % 2
            A = AAs(k)
            of_p = o_f[:, pi * 1024:(pi + 1) * 1024]
            a.wait_ge(t_oh, 2 * (k + 1))
            a.activation(of_p[:, 0:512], PB(A + 1, 0, 512), AF.Copy)
            inst = a.activation(of_p[:, 512:1024], PB(A + 2, 0, 512), AF.Copy)
            inst.then_inc(v_o, 1)

        sqrt1(0)
        sqrt1(1) if NRs > 1 else None
        tr_evac(0)
        q_evac(0)
        for k in range(NRs):
            pi = k % 2
            if k + 1 < NRs:
                tr_evac(k + 1)
                q_evac(k + 1)
            a.wait_ge(v_sub, k + 1)
            inst = a.activation(exw[:, pi * 256:(pi + 1) * 256],
                                msub[:, pi * 256:(pi + 1) * 256], AF.Exp,
                                scale=0.125)
            inst.then_inc(a_exp, 1)
            if k + 2 < NRs:
                sqrt1(k + 2)
            o_evac(k)
            a.wait_ge(v_var2, k + 1)
            inst = a.activation(sqtt[:, 2 + pi:3 + pi], bnag[:, 4 + pi * 2 + 1:4 + pi * 2 + 2],
                                AF.Sqrt, bias=ceps[:, 0:1])
            inst.then_inc(a_ln2, 1)

        if trunc is not None:
            return
        a.wait_ge(r_pe, 2)
        for fc2 in range(2):
            inst = a.activation(r_h[:, fc2:fc2 + 1], PB(1, fc2, 1), AF.Tanh,
                                bias=b1f[:, fc2:fc2 + 1])
        inst.then_inc(r_ac, 1)
        a.wait_ge(r_dv, 3)
        inst = a.activation(r_ex[0:1, :], r_logit[0:1, :], AF.Exp,
                            bias=r_mxn[0:1, 0:1], accum_out=r_es[0:1, 0:1])
        inst.then_inc(r_ac, 1)

    es1.close()

    es3 = contextlib.ExitStack()
    sb3 = lambda name, shape, dt=F32: es3.enter_context(nc.sbuf_tensor(name, shape, dt))
    u_fm3 = sb3("u_fm3", [128, 16384])
    acc3 = sb3("acc3", [128, 16384])
    wa_buf = sb3("wa_buf", [128, 8192])
    wb_buf = sb3("wb_buf", [128, 8192])
    g_sb = sb3("g_sb", [128, 512])
    u2t3 = sb3("u2t3", [128, 1024])

    blk2 = es3.enter_context(nc.Block("phase3"))

    @blk2.gpsimd
    def _(g):
        if trunc is not None:
            return
        g.wait_ge(rdone, 1)
        g.wait_ge(s_ys0, 16 * NT // 2)
        g.wait_ge(s_ys1, 16 * NT // 2)
        nblk = 0
        for p in range(NPASS):
            for r in range(TPP):
                gi = p * TPP + r
                pi = gi % 2
                if gi >= 2:
                    g.wait_ge(v_yb, gi - 1)
                g.indirect_dma_start(
                    x_in[:, pi * 1024:(pi + 1) * 1024], None,
                    yscr_d[:, :],
                    bass.IndirectOffsetOnAxis(cyidx[:, gi:gi + 1], 0),
                ).then_inc(s_yr0 if pi == 0 else s_yr1, 16)
            for s in range(3):
                for fcb in range(8):
                    pi = nblk % 2
                    if nblk >= 2:
                        g.wait_ge(t_o3, 4 * TQP * (nblk - 1))
                    for j in range(4):
                        fc = fcb * 4 + j
                        g.indirect_dma_start(
                            wa_buf[:, pi * 4096 + j * 1024: pi * 4096 + (j + 1) * 1024],
                            None, wa_d[:, :],
                            bass.IndirectOffsetOnAxis(
                                idx_i[:, s * 32 + fc:s * 32 + fc + 1], 0),
                        ).then_inc(s_ga0 if pi == 0 else s_ga1, 16)
                        g.indirect_dma_start(
                            wb_buf[:, pi * 4096 + j * 1024: pi * 4096 + (j + 1) * 1024],
                            None, wb_d[:, :],
                            bass.IndirectOffsetOnAxis(
                                idx_i[:, s * 32 + fc:s * 32 + fc + 1], 0),
                        ).then_inc(s_ga0 if pi == 0 else s_ga1, 16)
                    nblk += 1

    @blk2.sync
    def _(sync):
        if trunc is not None:
            return
        EVP = 3 * 8 * TQP * 2
        for p in range(NPASS):
            for r in range(TPP):
                last = p * EVP + ((2 * 8 + 7) * TQP + r // 2) * 2 + (r % 2) + 1
                sync.wait_ge(v_ev, last)
                sync.dma_start(
                    out_d[(p * TPP + r) * 128:(p * TPP + r + 1) * 128, :],
                    in_=acc3[:, r * 1024:(r + 1) * 1024]).then_inc(s_out, 16)
        sync.wait_ge(s_out, 16 * NPASS * TPP)

    @blk2.tensor
    def _(t):
        if trunc is not None:
            return
        blk_cnt = 0
        for p in range(NPASS):
            for r in range(TPP):
                gi = p * TPP + r
                pi = gi % 2
                t.wait_ge(v_u23, gi + 1)
                if gi >= 1:
                    t.wait_ge(v_tr3, 2 * gi)
                for ic in range(8):
                    bank = 3 if ic < 4 else 7
                    inst = t.transpose(PB(bank, (ic % 4) * 128, 128),
                                       u2t3[:, ic * 128:(ic + 1) * 128],
                                       cident[:, :])
                    if ic == 3:
                        inst.then_inc(t_tr3, 1)
                inst.then_inc(t_tr3, 1)
            t.wait_ge(v_tr3, 2 * TPP * (p + 1))
            steps = []
            for s in range(3):
                for fcb in range(8):
                    for tq in range(TQP):
                        for j in range(4):
                            steps.append((s, fcb, tq, j))

            def z_step(idx):
                s, fcb, tq, j = steps[idx]
                blk = (p * 3 + s) * 8 + fcb
                pi = blk % 2
                J = p * 3 * 8 * TQP * 4 + idx
                if j == 0 and tq == 0:
                    t.wait_ge(s_ga0 if pi == 0 else s_ga1, 128 * (blk // 2 + 1))
                zb = 0 if J % 2 == 0 else 4
                if J >= 2:
                    t.wait_ge(a_g, J - 1)
                for ic in range(8):
                    inst = t.matmul(
                        PB(zb, 0, 256),
                        wa_buf[:, pi * 4096 + j * 1024 + ic * 128: pi * 4096 + j * 1024 + (ic + 1) * 128],
                        u_fm3[:, ic * TPP * 128 + tq * 256: ic * TPP * 128 + tq * 256 + 256],
                        start=(ic == 0), stop=(ic == 7))
                inst.then_inc(t_z, 1)

            def o_step(idx):
                s, fcb, tq, j = steps[idx]
                blk = (p * 3 + s) * 8 + fcb
                pi = blk % 2
                J = p * 3 * 8 * TQP * 4 + idx
                unit = J // 4
                t.wait_ge(a_g, J + 1)
                if j == 0 and unit >= 1:
                    t.wait_ge(v_ev, 2 * unit)
                for th in range(2):
                    obase = 512 + th * 2048
                    for nh in range(2):
                        inst = t.matmul(
                            ps_t[:, obase + nh * 512: obase + (nh + 1) * 512],
                            g_sb[:, (J % 2) * 256 + th * 128:(J % 2) * 256 + th * 128 + 128],
                            wb_buf[:, pi * 4096 + j * 1024 + nh * 512: pi * 4096 + j * 1024 + (nh + 1) * 512],
                            start=(j == 0), stop=(j == 3))
                inst.then_inc(t_o3, 1)

            z_step(0)
            for idx in range(len(steps)):
                if idx + 1 < len(steps):
                    z_step(idx + 1)
                o_step(idx)
            blk_cnt += 24

    @blk2.vector
    def _(v):
        if trunc is not None:
            return
        for p in range(NPASS):
            if p >= 1:
                v.wait_ge(s_out, 16 * TPP * p)
            for r in range(TPP):
                gi = p * TPP + r
                pi = gi % 2
                y_p = x_in[:, pi * 1024:(pi + 1) * 1024]
                u23_p = u2t3[:, 0:1024]
                if gi >= 1:
                    v.wait_ge(t_tr3, 2 * gi)
                v.wait_ge(s_yr0 if pi == 0 else s_yr1, 16 * (gi // 2 + 1))
                if gi >= 2:
                    v.wait_ge(a_ln3, gi - 1)
                v.bn_stats(bnst[:, pi * 12:pi * 12 + 6], y_p[:, 0:512])
                v.bn_stats(bnst[:, pi * 12 + 6:pi * 12 + 12], y_p[:, 512:1024])
                v.drain()
                inst = v.bn_aggr(bnag[:, pi * 2:pi * 2 + 2],
                                 bnst[:, pi * 12:pi * 12 + 12])
                v.drain()
                inst.then_inc(v_var3, 1)
                v.wait_ge(a_ln3, gi + 1)
                v.reciprocal(rstdt[:, pi:pi + 1], sqtt[:, pi:pi + 1])
                v.drain()
                inst = v.tensor_scalar(u23_p, y_p, bnag[:, pi * 2:pi * 2 + 1],
                                       rstdt[:, pi:pi + 1], ALU.subtract, ALU.mult)
                inst.then_inc(v_u23, 1)
                v.drain()
                inst = v.tensor_add(acc3[:, r * 1024:(r + 1) * 1024], y_p, u23_p)
                inst.then_inc(v_yb, 1)
                v.wait_ge(t_tr3, 2 * gi + 1)
                for ic in range(4):
                    v.tensor_copy(u_fm3[:, ic * TPP * 128 + r * 128: ic * TPP * 128 + (r + 1) * 128],
                                  PB(3, ic * 128, 128))
                v.wait_ge(t_tr3, 2 * gi + 2)
                for ic in range(4, 8):
                    inst = v.tensor_copy(
                        u_fm3[:, ic * TPP * 128 + r * 128: ic * TPP * 128 + (r + 1) * 128],
                        PB(7, (ic - 4) * 128, 128))
                inst.then_inc(v_tr3, 2)
            for s in range(3):
                for fcb in range(8):
                    for tq in range(TQP):
                        unit = ((p * 3 + s) * 8 + fcb) * TQP + tq
                        v.wait_ge(t_o3, 4 * (unit + 1))
                        for th in range(2):
                            obase = 512 + th * 2048
                            tile_local = tq * 2 + th
                            inst = v.scalar_tensor_tensor(
                                acc3[:, tile_local * 1024:(tile_local + 1) * 1024],
                                ps_t[:, obase:obase + 1024],
                                w_bc[:, s:s + 1],
                                acc3[:, tile_local * 1024:(tile_local + 1) * 1024],
                                ALU.mult, ALU.add)
                            inst.then_inc(v_ev, 1)

    @blk2.scalar
    def _(a):
        if trunc is not None:
            return
        for p in range(NPASS):
            for r in range(TPP):
                gi = p * TPP + r
                pi = gi % 2
                a.wait_ge(v_var3, gi + 1)
                inst = a.activation(sqtt[:, pi:pi + 1], bnag[:, pi * 2 + 1:pi * 2 + 2],
                                    AF.Sqrt, bias=ceps[:, 0:1])
                inst.then_inc(a_ln3, 1)
            for s in range(3):
                for fcb in range(8):
                    for tq in range(TQP):
                        unit = ((p * 3 + s) * 8 + fcb) * TQP + tq
                        for j in range(4):
                            J = unit * 4 + j
                            zb = 0 if J % 2 == 0 else 4
                            a.wait_ge(t_z, J + 1)
                            if J >= 2:
                                a.wait_ge(t_o3, J - 1)
                            inst = a.activation(
                                g_sb[:, (J % 2) * 256:(J % 2 + 1) * 256],
                                PB(zb, 0, 256), AF.Gelu_apprx_tanh)
                            inst.then_inc(a_g, 1)

    es3.close()
    es.close()
    return nc


def prep_inputs(inputs):
    W_attn = np.asarray(inputs["L0_attn_W_attn"], dtype=np.float32)
    W_proj = np.asarray(inputs["L0_attn_W_proj"], dtype=np.float32)
    W1 = np.asarray(inputs["L0_ffn_W1"], dtype=np.float32)
    b1 = np.asarray(inputs["L0_ffn_b1"], dtype=np.float32)
    W2 = np.asarray(inputs["L0_ffn_W2"], dtype=np.float32)
    b2 = np.asarray(inputs["L0_ffn_b2"], dtype=np.float32)
    Wa = np.asarray(inputs["L0_ffn_Wa"], dtype=np.float32)
    Wb = np.asarray(inputs["L0_ffn_Wb"], dtype=np.float32)

    d = {}
    d["wq_t"] = np.ascontiguousarray(
        W_attn[0:1024].T.reshape(8, 128, 8, 128).transpose(1, 0, 2, 3).reshape(128, 8192))
    d["wk_t"] = np.ascontiguousarray(
        W_attn[1024:2048].T.reshape(8, 128, 1024).transpose(1, 0, 2).reshape(128, 8192))
    d["wv_t"] = np.ascontiguousarray(
        W_attn[2048:3072].T.reshape(8, 128, 1024).transpose(1, 0, 2).reshape(128, 8192))
    d["wp_t"] = np.ascontiguousarray(
        W_proj.T.reshape(8, 128, 1024).transpose(1, 0, 2).reshape(128, 8192))
    d["w1_t"] = np.ascontiguousarray(
        W1.reshape(8, 128, 2, 128).transpose(1, 0, 2, 3).reshape(128, 2048))
    d["b1_t"] = np.ascontiguousarray(b1.reshape(2, 128).T)
    d["w2_t"] = np.ascontiguousarray(
        W2.reshape(2, 128, 8).transpose(1, 0, 2).reshape(128, 16))
    d["b2_t"] = np.ascontiguousarray(b2.reshape(1, 8))
    d["wa_r"] = np.ascontiguousarray(
        Wa.reshape(8, 8, 128, 32, 128).transpose(0, 3, 2, 1, 4).reshape(32768, 1024))
    d["wb_r"] = np.ascontiguousarray(Wb.reshape(32768, 1024))
    d["cident"] = np.eye(128, dtype=np.float32)
    t_ar = np.arange(128)
    j_ar = np.arange(16)
    m = np.where(j_ar[None, :] <= t_ar[:, None], 0.0, NEG).astype(np.float32)
    d["cmask"] = np.ascontiguousarray(np.tile(m, (1, 16)))
    p_ar = np.arange(128, dtype=np.float32)
    d["ciota_fc"] = np.ascontiguousarray(
        (p_ar[:, None] + 128.0 * np.arange(32, dtype=np.float32)[None, :]).astype(np.float32))
    d["cones_r"] = np.ones((1, 128), dtype=np.float32)
    d["cones_c"] = np.ones((128, 1), dtype=np.float32)
    i8 = np.arange(8)
    d["clow8"] = (i8[None, :] < i8[:, None]).astype(np.float32)
    d["cup8"] = (i8[:, None] < i8[None, :]).astype(np.float32)
    d["ciota_p"] = p_ar.reshape(128, 1).astype(np.float32)
    d["ceps"] = np.full((128, 1), EPS, dtype=np.float32)
    return d


def core_inputs(inputs, shared, c):
    x = np.asarray(inputs["L0_attn_x"], dtype=np.float32)
    b, half = c // 2, c % 2
    m = dict(shared)
    m["x"] = np.ascontiguousarray(x[b].reshape(8192, 1024))
    base = half * 4096
    rows = (base + np.arange(4096)).reshape(32, 128).T
    m["cyidx"] = np.ascontiguousarray(rows.astype(np.int32))
    return m


from concourse.bass_utils import run_bass_kernel_spmd


def kernel(**inputs):
    shared = prep_inputs(inputs)
    in_maps = [core_inputs(inputs, shared, c) for c in range(8)]
    nc = build_nc()
    res = run_bass_kernel_spmd(nc, in_maps, core_ids=list(range(8)))
    out = np.empty((4, 8192, 1024), dtype=np.float32)
    for c in range(8):
        b, half = c // 2, c % 2
        out[b, half * 4096:(half + 1) * 4096, :] = res.results[c]["out"]
    return out


# version 09 (final)
# speedup vs baseline: 1.2780x; 1.0330x over previous
import contextlib
import numpy as np
import concourse.bass as bass
import concourse.mybir as mybir

F32 = mybir.dt.float32
I32 = mybir.dt.int32
AF = mybir.ActivationFunctionType
ALU = mybir.AluOpType
AX = mybir.AxisListType

NT = 64
NPASS = 2
TPP = 16
TQP = 8
EPS = 1e-5
NEG = -1.0e30


def build_nc(debug=False, pipe=True, pool_short=False, trunc=None):
    nc = bass.Bass("TRN2", target_bir_lowering=False)

    x_d = nc.dram_tensor("x", [8192, 1024], F32, kind="ExternalInput").ap()
    wq_d = nc.dram_tensor("wq_t", [128, 8192], F32, kind="ExternalInput").ap()
    wk_d = nc.dram_tensor("wk_t", [128, 8192], F32, kind="ExternalInput").ap()
    wv_d = nc.dram_tensor("wv_t", [128, 8192], F32, kind="ExternalInput").ap()
    wp_d = nc.dram_tensor("wp_t", [128, 8192], F32, kind="ExternalInput").ap()
    w1_d = nc.dram_tensor("w1_t", [128, 2048], F32, kind="ExternalInput").ap()
    b1_d = nc.dram_tensor("b1_t", [128, 2], F32, kind="ExternalInput").ap()
    w2_d = nc.dram_tensor("w2_t", [128, 16], F32, kind="ExternalInput").ap()
    b2_d = nc.dram_tensor("b2_t", [1, 8], F32, kind="ExternalInput").ap()
    wa_d = nc.dram_tensor("wa_r", [32768, 1024], F32, kind="ExternalInput").ap()
    wb_d = nc.dram_tensor("wb_r", [32768, 1024], F32, kind="ExternalInput").ap()
    cid_d = nc.dram_tensor("cident", [128, 128], F32, kind="ExternalInput").ap()
    cmask_d = nc.dram_tensor("cmask", [128, 256], F32, kind="ExternalInput").ap()
    ciofc_d = nc.dram_tensor("ciota_fc", [128, 32], F32, kind="ExternalInput").ap()
    conr_d = nc.dram_tensor("cones_r", [1, 128], F32, kind="ExternalInput").ap()
    conc_d = nc.dram_tensor("cones_c", [128, 1], F32, kind="ExternalInput").ap()
    clow_d = nc.dram_tensor("clow8", [8, 8], F32, kind="ExternalInput").ap()
    cup_d = nc.dram_tensor("cup8", [8, 8], F32, kind="ExternalInput").ap()
    ciop_d = nc.dram_tensor("ciota_p", [128, 1], F32, kind="ExternalInput").ap()
    cyidx_d = nc.dram_tensor("cyidx", [128, 32], I32, kind="ExternalInput").ap()
    ceps_d = nc.dram_tensor("ceps", [128, 1], F32, kind="ExternalInput").ap()

    yscr_d = nc.dram_tensor("y_scr", [8192, 1024], F32, kind="Internal").ap()
    out_d = nc.dram_tensor("out", [4096, 1024], F32, kind="ExternalOutput").ap()
    if debug:
        dbg_d = nc.dram_tensor("dbg", [128, 16], F32, kind="ExternalOutput").ap()

    es = contextlib.ExitStack()
    sb = lambda name, shape, dt=F32: es.enter_context(nc.sbuf_tensor(name, shape, dt))
    sem = lambda name: es.enter_context(nc.semaphore(name))

    cident = sb("s_cident", [128, 128])

    ciofc = sb("s_ciota_fc", [128, 32])
    conc = sb("s_cones_c", [128, 1])
    clow8 = sb("s_clow8", [8, 8])
    cup8 = sb("s_cup8", [8, 8])
    ciop = sb("s_ciota_p", [128, 1])
    cyidx = sb("s_cyidx", [128, 32], I32)
    ceps = sb("s_ceps", [128, 1])
    b2r = sb("s_b2r", [1, 8])

    x_in = sb("x_in", [128, 2048])
    bnst = sb("bnst", [128, 24])
    bnag = sb("bnag", [128, 8])
    rstdt = sb("rstdt", [128, 4])
    sqtt = sb("sqtt", [128, 4])

    w_bc = sb("w_bc", [128, 3])
    idx_i = sb("idx_i", [128, 96], I32)

    ps_t = es.enter_context(nc.psum_tensor("ps", [128, 4096], F32))

    def PB(bank, lo=0, n=512, p0=0, np_=128):
        return ps_t[p0:p0 + np_, bank * 512 + lo: bank * 512 + lo + n]

    s_xs = [sem(f"s_x{i}") for i in range(4)]
    s_wq, s_wkv, s_wp, s_cst = sem("s_wq"), sem("s_wkv"), sem("s_wp"), sem("s_cst")
    s_ys0, s_ys1 = sem("s_ys0"), sem("s_ys1")
    s_yr0, s_yr1 = sem("s_yr0"), sem("s_yr1")
    s_ga0, s_ga1 = sem("s_ga0"), sem("s_ga1")
    s_out = sem("s_out")

    a_sq = sem("a_sq")
    v_var1, a_ln1, v_u1 = sem("v_var1"), sem("a_ln1"), sem("v_u1")
    t_tr, v_tr = sem("t_tr"), sem("v_tr")
    t_q, v_q = sem("t_q"), sem("v_q")
    t_kv, v_kv = sem("t_kv"), sem("v_kv")
    t_lg = sem("t_lg")
    v_sub, a_exp, v_w = sem("v_sub"), sem("a_exp"), sem("v_w")
    t_wt, v_wt, t_oh = sem("t_wt"), sem("v_wt"), sem("t_oh")
    t_o, v_o = sem("t_o"), sem("v_o")
    t_proj, v_y = sem("t_proj"), sem("v_y")
    v_var2, a_ln2, v_u2 = sem("v_var2"), sem("a_ln2"), sem("v_u2")
    t_pool, v_pool = sem("t_pool"), sem("v_pool")
    r_pe, r_dv, r_ac, rdone = sem("r_pe"), sem("r_dv"), sem("r_ac"), sem("rdone")
    v_yb = sem("v_yb")
    v_u23, t_tr3, v_tr3 = sem("v_u23"), sem("t_tr3"), sem("v_tr3")
    a_sq3, v_var3, a_ln3 = sem("a_sq3"), sem("v_var3"), sem("a_ln3")
    t_z, a_g, t_o3, v_ev = sem("t_z"), sem("a_g"), sem("t_o3"), sem("v_ev")
    t_wfree = sem("t_wfree")

    NCST = 14

    def BK(k, i):
        return (k % 2) * 4 + i

    es1 = contextlib.ExitStack()
    sb1 = lambda name, shape, dt=F32: es1.enter_context(nc.sbuf_tensor(name, shape, dt))
    wq_s = sb1("wq_s", [128, 8192])
    wp_s = sb1("wp_s", [128, 8192])
    wkv_s = sb1("wkv_s", [128, 4096])
    u1 = sb1("u1", [128, 4096])
    u1f = sb1("u1f", [128, 2048])
    qf = sb1("qf", [128, 2048])
    k16f = sb1("k16f", [128, 128])
    k16t = sb1("k16t", [16, 1024])
    v16t = sb1("v16t", [16, 1024])
    msub = sb1("msub", [128, 512])
    mxg = sb1("mxg", [128, 32])
    exw = sb1("exw", [128, 512])
    esum = sb1("esum", [128, 32])
    erc = sb1("erc", [128, 32])
    w_tm = sb1("w_tm", [128, 512])
    wTh = sb1("wTh", [16, 2048])
    o_f = sb1("o_f", [128, 2048])
    y_t = sb1("y_t", [128, 2048])
    u2 = sb1("u2", [128, 2048])
    x_cd = sb1("x_cd", [128, 2048])
    cmask = sb1("s_cmask", [128, 256])
    w1f = sb1("s_w1f", [128, 2048])
    pooled_acc = sb1("pooled_acc", [1, 1024])
    pool_tmp = sb1("pool_tmp", [1, 1024])
    conr = sb1("s_cones_r", [1, 128])
    b1f = sb1("s_b1f", [128, 2])
    w2f = sb1("s_w2f", [128, 16])
    r_pool_f = sb1("r_pool_f", [128, 8])
    r_h = sb1("r_h", [128, 2])
    r_logit = sb1("r_logit", [1, 8])
    r_mx = sb1("r_mx", [1, 1])
    r_mxn = sb1("r_mxn", [1, 1])
    r_ex = sb1("r_ex", [1, 8])
    r_es = sb1("r_es", [1, 1])
    r_er = sb1("r_er", [1, 1])
    r_probs = sb1("r_probs", [1, 8])
    r_A = sb1("r_A", [8, 8])
    r_B = sb1("r_B", [8, 8])
    r_gt = sb1("r_gt", [8, 8])
    r_rank = sb1("r_rank", [8, 1])
    r_sel = sb1("r_sel", [8, 1])
    r_wun = sb1("r_wun", [8, 1])
    r_rZ = sb1("r_rZ", [1, 1])
    r_oh = sb1("r_oh", [8, 3])
    r_wsel = sb1("r_wsel", [1, 3])
    r_id = sb1("r_id", [1, 3])
    e_bc = sb1("e_bc", [128, 3])
    idx_f = sb1("idx_f", [128, 96])

    blk1 = es1.enter_context(nc.Block("phase1"))

    @blk1.sync
    def _(sync):
        for dst, src in [(cident, cid_d), (cmask, cmask_d), (ciofc, ciofc_d),
                         (conr, conr_d), (conc, conc_d), (clow8, clow_d),
                         (cup8, cup_d), (ciop, ciop_d), (cyidx, cyidx_d),
                         (ceps, ceps_d),
                         (b2r, b2_d), (b1f, b1_d), (w1f, w1_d), (w2f, w2_d)]:
            sync.dma_start(out=dst[:, :], in_=src[:, :]).then_inc(s_cst, 16)
        sync.dma_start(out=x_in[:, 0:1024], in_=x_d[0:128, :]).then_inc(s_xs[0], 16)
        sync.dma_start(out=x_in[:, 1024:2048], in_=x_d[128:256, :]).then_inc(s_xs[1], 16)
        sync.dma_start(out=wq_s[:, :], in_=wq_d[:, :]).then_inc(s_wq, 16)
        sync.dma_start(out=wkv_s[:, :], in_=wk_d[:, 0:4096]).then_inc(s_wkv, 16)
        sync.dma_start(out=wp_s[:, :], in_=wp_d[:, :]).then_inc(s_wp, 16)
        sync.wait_ge(t_kv, 1)
        sync.dma_start(out=wkv_s[:, :], in_=wk_d[:, 4096:8192]).then_inc(s_wkv, 16)
        sync.wait_ge(t_kv, 2)
        sync.dma_start(out=wkv_s[:, :], in_=wv_d[:, 0:4096]).then_inc(s_wkv, 16)
        sync.wait_ge(t_kv, 3)
        sync.dma_start(out=wkv_s[:, :], in_=wv_d[:, 4096:8192]).then_inc(s_wkv, 16)
        for k in range(2, NT if trunc is None else trunc):
            sl = k % 4
            if k >= 4:
                sync.wait_ge(v_y, k - 3)
            xbuf = x_in if sl < 2 else x_cd
            sync.dma_start(out=xbuf[:, (sl % 2) * 1024:(sl % 2 + 1) * 1024],
                           in_=x_d[k * 128:(k + 1) * 128, :]).then_inc(
                               s_xs[sl], 16)
        if trunc is not None:
            sync.wait_ge(v_y, trunc)
            sync.dma_start(out=out_d[0:128, :],
                           in_=y_t[:, 0:1024]).then_inc(s_out, 16)
            sync.wait_ge(s_out, 16)

    @blk1.gpsimd
    def _(g):
        for k in range(NT if trunc is None else trunc):
            pi = k % 2
            g.wait_ge(v_y, k + 1)
            g.dma_start(yscr_d[k * 128:(k + 1) * 128, :],
                        y_t[:, pi * 1024:(pi + 1) * 1024]).then_inc(
                            s_ys0 if pi == 0 else s_ys1, 16)

    @blk1.tensor
    def _(t):
        t.wait_ge(s_cst, 16 * NCST)

        def AA(k):
            return (k % 2) * 4

        def tr_stage(k):
            pi = k % 2
            A = AA(k)
            u1_p = u1[:, (k % 4) * 1024:(k % 4 + 1) * 1024]
            t.wait_ge(v_u1, k + 1)
            if k >= 2:
                t.wait_ge(v_y, k - 1)
            for ic in range(8):
                bank = A if ic < 4 else A + 1
                inst = t.transpose(PB(bank, (ic % 4) * 128, 128),
                                   u1_p[:, ic * 128:(ic + 1) * 128], cident[:, :])
            inst.then_inc(t_tr, 1)

        def q_stage(k):
            pi = k % 2
            A = AA(k)
            u1f_p = u1f[:, pi * 1024:(pi + 1) * 1024]
            if k == 0:
                t.wait_ge(s_wq, 16)
            t.wait_ge(v_tr, k + 1)
            if k >= 2:
                t.wait_ge(v_y, k - 1)
            for oc in range(8):
                bank = A + 2 if oc < 4 else A
                lo = (oc % 4) * 128
                for ic in range(8):
                    inst = t.matmul(PB(bank, lo, 128),
                                    wq_s[:, ic * 1024 + oc * 128: ic * 1024 + (oc + 1) * 128],
                                    u1f_p[:, ic * 128:(ic + 1) * 128],
                                    start=(ic == 0), stop=(ic == 7))
            inst.then_inc(t_q, 1)

        def kv_block():
            t.wait_ge(v_q, 1)
            for part in range(4):
                t.wait_ge(s_wkv, 16 * (part + 1))
                if part == 2:
                    t.wait_ge(v_kv, 1)
                for half in range(2):
                    for icl in range(4):
                        ic = (part % 2) * 4 + icl
                        inst = t.matmul(PB(1 + half, 0, 512, 0, 16),
                                        u1f[:, ic * 128: ic * 128 + 16],
                                        wkv_s[:, icl * 1024 + half * 512: icl * 1024 + (half + 1) * 512],
                                        start=(ic == 0), stop=(ic == 7))
                inst.then_inc(t_kv, 1)
            t.wait_ge(v_kv, 2)
            t.wait_ge(v_tr, 1)
            for ic in range(8):
                inst = t.transpose(PB(0, ic * 16, 16),
                                   k16t[0:16, ic * 128:(ic + 1) * 128],
                                   cident[0:16, 0:16])
            inst.then_inc(t_kv, 1)
            t.wait_ge(v_kv, 3)

        def logits_stage(k):
            pi = k % 2
            A = AA(k)
            qf_p = qf[:, pi * 1024:(pi + 1) * 1024]
            t.wait_ge(v_q, k + 1)
            for h in range(16):
                po = (h % 2) * 64
                ch = h // 2
                inst = t.matmul(PB(A + 1, h * 16, 16),
                                qf_p[po:po + 64, ch * 128:(ch + 1) * 128],
                                k16f[po:po + 64, ch * 16:(ch + 1) * 16],
                                start=True, stop=True)
            inst.then_inc(t_lg, 1)

        def pool_stage(k):
            pi = k % 2
            u2_pp = u2[:, pi * 1024:(pi + 1) * 1024]
            t.wait_ge(v_u2, k + 1)
            if pool_short:
                t.wait_ge(v_pool, k)
            st_ = (k == 0) if not pool_short else True
            sp_ = (k == NT - 1) if not pool_short else True
            inst = t.matmul(PB(3, 0, 512, 0, 1), conc[:, :], u2_pp[:, 0:512],
                            start=st_, stop=sp_)
            inst = t.matmul(PB(7, 0, 512, 0, 1), conc[:, :], u2_pp[:, 512:1024],
                            start=st_, stop=sp_)
            inst.then_inc(t_pool, 1)

        def wT_o_proj_stage(k, wt_base):
            pi = k % 2
            A = AA(k)
            wtm_p = w_tm[:, pi * 256:(pi + 1) * 256]
            of_p = o_f[:, pi * 1024:(pi + 1) * 1024]
            t.wait_ge(v_w, k + 1)
            t.wait_ge(v_sub, k + 1)
            for hh in range(8):
                bank = A + 2 if hh < 4 else A
                inst = t.transpose(PB(bank, (hh % 4) * 128, 128, 0, 16),
                                   wtm_p[:, hh * 16:(hh + 1) * 16], cident[:, :])
            inst.then_inc(t_wt, 1)
            if k >= 1:
                pool_stage(k - 1)
            t.wait_ge(v_wt, wt_base + 1)
            for hh in range(8):
                h = 8 + hh
                bank = A + 2 if hh < 4 else A
                inst = t.transpose(PB(bank, (hh % 4) * 128, 128, 0, 16),
                                   wtm_p[:, h * 16:(h + 1) * 16], cident[:, :])
            inst.then_inc(t_wt, 1)
            for h in range(8):
                p8 = h // 2
                inst = t.matmul(PB(A + 1, p8 * 128, 128, (h % 2) * 64, 64),
                                v16t[0:16, h * 64:(h + 1) * 64],
                                wTh[0:16, h * 128:(h + 1) * 128],
                                start=True, stop=True)
            inst.then_inc(t_oh, 1)
            t.wait_ge(v_wt, wt_base + 2)
            for h in range(8, 16):
                p8 = h // 2 - 4
                inst = t.matmul(PB(A + 2, p8 * 128, 128, (h % 2) * 64, 64),
                                v16t[0:16, h * 64:(h + 1) * 64],
                                wTh[0:16, h * 128:(h + 1) * 128],
                                start=True, stop=True)
            inst.then_inc(t_oh, 1)
            if k == 0:
                t.wait_ge(s_wp, 16)
            t.wait_ge(v_o, k + 1)
            for half in range(2):
                bank = A + 1 if half == 0 else A + 2
                for ic in range(8):
                    inst = t.matmul(PB(bank, 0, 512),
                                    of_p[:, ic * 128:(ic + 1) * 128],
                                    wp_s[:, ic * 1024 + half * 512: ic * 1024 + (half + 1) * 512],
                                    start=(ic == 0), stop=(ic == 7))
            inst.then_inc(t_proj, 1)

        NR = NT if trunc is None else trunc
        if pipe:
            tr_stage(0)
            q_stage(0)
            kv_block()
            logits_stage(0)
            for k in range(NR):
                if k + 1 < NR:
                    tr_stage(k + 1)
                    q_stage(k + 1)
                    logits_stage(k + 1)
                wT_o_proj_stage(k, 2 * k)
            if trunc is None:
                pool_stage(NT - 1)
        else:
            for k in range(NT):
                tr_stage(k)
                q_stage(k)
                if k == 0:
                    kv_block()
                logits_stage(k)
                wT_o_proj_stage(k, 2 * k)
            pool_stage(NT - 1)

        if trunc is not None:
            return
        t.wait_ge(r_dv, 1)
        for ic in range(8):
            inst = t.matmul(PB(0, ic, 1), pool_tmp[0:1, ic * 128:(ic + 1) * 128],
                            conc[0:1, 0:1], start=True, stop=True)
        inst.then_inc(r_pe, 1)
        t.wait_ge(r_dv, 2)
        for fc2 in range(2):
            for ic in range(8):
                inst = t.matmul(PB(1, fc2, 1),
                                w1f[:, ic * 256 + fc2 * 128: ic * 256 + (fc2 + 1) * 128],
                                r_pool_f[:, ic:ic + 1],
                                start=(ic == 0), stop=(ic == 7))
        inst.then_inc(r_pe, 1)
        t.wait_ge(r_ac, 1)
        for fc2 in range(2):
            inst = t.matmul(PB(2, 0, 8, 0, 1), r_h[:, fc2:fc2 + 1],
                            w2f[:, fc2 * 8:(fc2 + 1) * 8],
                            start=(fc2 == 0), stop=(fc2 == 1))
        inst.then_inc(r_pe, 1)
        t.wait_ge(r_dv, 4)
        inst = t.matmul(PB(3, 0, 8, 0, 8), conr[0:1, 0:8], r_probs[0:1, :],
                        start=True, stop=True)
        inst = t.matmul(PB(3, 8, 8, 0, 8), r_probs[0:1, :], conr[0:1, 0:8],
                        start=True, stop=True)
        inst.then_inc(r_pe, 1)
        t.wait_ge(r_dv, 5)
        inst = t.matmul(PB(4, 0, 1, 0, 1), r_wun[:, :], conc[0:8, 0:1],
                        start=True, stop=True)
        inst = t.matmul(PB(4, 1, 1, 0, 8), cup8[:, :], r_sel[:, :],
                        start=True, stop=True)
        inst.then_inc(r_pe, 1)
        t.wait_ge(r_dv, 6)
        inst = t.matmul(PB(5, 0, 3, 0, 1), r_wun[:, :], r_oh[:, :],
                        start=True, stop=True)
        inst = t.matmul(PB(5, 3, 3, 0, 1), ciop[0:8, 0:1], r_oh[:, :],
                        start=True, stop=True)
        inst.then_inc(r_pe, 1)
        t.wait_ge(r_dv, 7)
        inst = t.matmul(PB(6, 0, 3), conr[:, :], r_wsel[0:1, :],
                        start=True, stop=True)
        inst = t.matmul(PB(6, 3, 3), conr[:, :], r_id[0:1, :],
                        start=True, stop=True)
        inst.then_inc(r_pe, 1)

    @blk1.vector
    def _(v):
        v.wait_ge(s_cst, 16 * NCST)

        def AA(k):
            return (k % 2) * 4

        def ln(dst, src_x, pi, base, sq, var_sem, var_val, ln_sem, u_sem):
            if var_val >= 3:
                v.wait_ge(ln_sem, var_val - 2)
            v.bn_stats(bnst[:, pi * 12:pi * 12 + 6], src_x[:, 0:512])
            v.bn_stats(bnst[:, pi * 12 + 6:pi * 12 + 12], src_x[:, 512:1024])
            v.drain()
            inst = v.bn_aggr(bnag[:, base:base + 2], bnst[:, pi * 12:pi * 12 + 12])
            v.drain()
            inst.then_inc(var_sem, 1)
            v.wait_ge(ln_sem, var_val)
            v.reciprocal(rstdt[:, sq:sq + 1], sqtt[:, sq:sq + 1])
            v.drain()
            inst = v.tensor_scalar(dst, src_x, bnag[:, base:base + 1],
                                   rstdt[:, sq:sq + 1], ALU.subtract, ALU.mult)
            inst.then_inc(u_sem, 1)

        def ln1(k):
            pi = k % 2
            sl = k % 4
            xbuf = x_in if sl < 2 else x_cd
            v.wait_ge(s_xs[sl], 16 * (k // 4 + 1))
            ln(u1[:, sl * 1024:(sl + 1) * 1024],
               xbuf[:, (sl % 2) * 1024:(sl % 2 + 1) * 1024],
               pi, pi * 2, pi, v_var1, k + 1, a_ln1, v_u1)

        def kv_evacs():
            v.wait_ge(t_kv, 2)
            v.tensor_copy(k16t[0:16, 0:512], PB(1, 0, 512, 0, 16))
            inst = v.tensor_copy(k16t[0:16, 512:1024], PB(2, 0, 512, 0, 16))
            inst.then_inc(v_kv, 1)
            v.wait_ge(t_kv, 4)
            v.tensor_copy(v16t[0:16, 0:512], PB(1, 0, 512, 0, 16))
            inst = v.tensor_copy(v16t[0:16, 512:1024], PB(2, 0, 512, 0, 16))
            inst.then_inc(v_kv, 1)
            v.wait_ge(t_kv, 5)
            inst = v.tensor_copy(k16f[:, :], PB(0, 0, 128))
            inst.then_inc(v_kv, 1)

        def softmax(k):
            pi = k % 2
            A = AA(k)
            v.wait_ge(t_lg, k + 1)
            lg_ap = bass.AP(ps_t, (A + 1) * 512, [[4096, 128], [16, 16], [1, 16]])
            ms_g = bass.AP(msub, pi * 256, [[512, 128], [16, 16], [1, 16]])
            mx_b = bass.AP(mxg, pi * 16, [[32, 128], [1, 16], [0, 16]])
            if k == 0:
                v.tensor_add(msub[:, pi * 256:(pi + 1) * 256],
                             PB(A + 1, 0, 256), cmask[:, :])
            else:
                v.tensor_copy(msub[:, pi * 256:(pi + 1) * 256], PB(A + 1, 0, 256))
            v.drain()
            v.reduce_max(mxg[:, pi * 16:(pi + 1) * 16], ms_g, AX.X)
            v.drain()
            inst = v.tensor_tensor(ms_g, ms_g, mx_b, ALU.subtract)
            inst.then_inc(v_sub, 1)
            v.wait_ge(a_exp, k + 1)
            ex_gr = bass.AP(exw, pi * 256, [[512, 128], [16, 16], [1, 16]])
            v.tensor_reduce(esum[:, pi * 16:(pi + 1) * 16], ex_gr, AX.X, ALU.add)
            v.drain()
            v.reciprocal(erc[:, pi * 16:(pi + 1) * 16], esum[:, pi * 16:(pi + 1) * 16])
            v.drain()
            er_b = bass.AP(erc, pi * 16, [[32, 128], [1, 16], [0, 16]])
            ex_g = bass.AP(exw, pi * 256, [[512, 128], [16, 16], [1, 16]])
            w_g = bass.AP(w_tm, pi * 256, [[512, 128], [16, 16], [1, 16]])
            inst = v.tensor_tensor(w_g, ex_g, er_b, ALU.mult)
            inst.then_inc(v_w, 1)

        def wT_evacs(k, wt_base):
            A = AA(k)
            for grp in range(2):
                v.wait_ge(t_wt, wt_base + grp + 1)
                for hh in range(8):
                    h = grp * 8 + hh
                    bank = A + 2 if hh < 4 else A
                    inst = v.tensor_copy(wTh[0:16, h * 128:(h + 1) * 128],
                                         PB(bank, (hh % 4) * 128, 128, 0, 16))
                inst.then_inc(v_wt, 1)

        def y_ln2(k):
            pi = k % 2
            A = AA(k)
            sl = k % 4
            x_p = (x_in if sl < 2 else x_cd)[:, (sl % 2) * 1024:(sl % 2 + 1) * 1024]
            y_p = y_t[:, pi * 1024:(pi + 1) * 1024]
            u2_p = u2[:, pi * 1024:(pi + 1) * 1024]
            v.wait_ge(t_proj, k + 1)
            if k >= 2:
                v.wait_ge(s_ys0 if pi == 0 else s_ys1, 16 * (k // 2))
            v.tensor_add(y_p[:, 0:512], x_p[:, 0:512], PB(A + 1, 0, 512))
            inst = v.tensor_add(y_p[:, 512:1024], x_p[:, 512:1024], PB(A + 2, 0, 512))
            inst.then_inc(v_y, 1)
            v.drain()
            ln(u2_p, y_p, pi, 4 + pi * 2, 2 + pi, v_var2, k + 1, a_ln2, v_u2)
            if pool_short:
                v.wait_ge(t_pool, k + 1)
                if k == 0:
                    v.tensor_copy(pooled_acc[0:1, 0:512], PB(3, 0, 512, 0, 1))
                    inst = v.tensor_copy(pooled_acc[0:1, 512:1024], PB(7, 0, 512, 0, 1))
                else:
                    v.tensor_add(pooled_acc[0:1, 0:512], pooled_acc[0:1, 0:512],
                                 PB(3, 0, 512, 0, 1))
                    inst = v.tensor_add(pooled_acc[0:1, 512:1024],
                                        pooled_acc[0:1, 512:1024],
                                        PB(7, 0, 512, 0, 1))
                inst.then_inc(v_pool, 1)

        NR = NT if trunc is None else trunc
        if pipe:
            ln1(0)
            ln1(1)
            kv_evacs()
            for k in range(NR):
                softmax(k)
                wT_evacs(k, 2 * k)
                if k + 2 < NR:
                    ln1(k + 2)
                y_ln2(k)
        else:
            ln1(0)
            for k in range(NT):
                tr_evac(k)
                q_evac(k)
                if k == 0:
                    kv_evacs()
                softmax(k)
                if k + 1 < NT:
                    ln1(k + 1)
                wT_evacs(k, 2 * k)
                o_evac(k)
                y_ln2(k)

        if trunc is not None:
            return
        if pool_short:
            pass
        v.wait_ge(t_pool, NT)
        if pool_short:
            v.wait_ge(v_pool, NT)
            v.drain()
            v.tensor_scalar_mul(pool_tmp[0:1, :], pooled_acc[0:1, :], 1.0 / 8192.0)
            inst = v.drain()
        else:
            v.drain()
            v.tensor_scalar_mul(pool_tmp[0:1, 0:512], PB(3, 0, 512, 0, 1), 1.0 / 8192.0)
            inst = v.tensor_scalar_mul(pool_tmp[0:1, 512:1024], PB(7, 0, 512, 0, 1),
                                       1.0 / 8192.0)
            v.drain()
        inst.then_inc(r_dv, 1)
        v.wait_ge(r_pe, 1)
        inst = v.tensor_copy(r_pool_f[:, :], PB(0, 0, 8))
        inst.then_inc(r_dv, 1)
        v.wait_ge(r_pe, 3)
        v.tensor_add(r_logit[0:1, :], PB(2, 0, 8, 0, 1), b2r[0:1, :])
        v.drain()
        v.reduce_max(r_mx[0:1, :], r_logit[0:1, :], AX.X)
        v.drain()
        inst = v.tensor_scalar_mul(r_mxn[0:1, :], r_mx[0:1, :], -1.0)
        inst.then_inc(r_dv, 1)
        v.wait_ge(r_ac, 2)
        v.reciprocal(r_er[0:1, :], r_es[0:1, :])
        v.drain()
        v.tensor_scalar(r_probs[0:1, :], r_ex[0:1, :], r_er[0:1, 0:1], None, ALU.mult)
        inst = v.drain()
        inst.then_inc(r_dv, 1)
        v.wait_ge(r_pe, 4)
        v.tensor_copy(r_A[:, :], PB(3, 0, 8, 0, 8))
        v.tensor_copy(r_B[:, :], PB(3, 8, 8, 0, 8))
        v.drain()
        v.tensor_tensor(r_gt[:, :], r_A[:, :], r_B[:, :], ALU.is_gt)
        v.drain()
        v.tensor_tensor(r_A[:, :], r_A[:, :], r_B[:, :], ALU.is_equal)
        v.drain()
        v.tensor_mul(r_A[:, :], r_A[:, :], clow8[:, :])
        v.drain()
        v.tensor_add(r_gt[:, :], r_gt[:, :], r_A[:, :])
        v.drain()
        v.tensor_reduce(r_rank[:, :], r_gt[:, :], AX.X, ALU.add)
        v.drain()
        v.tensor_single_scalar(r_sel[:, :], r_rank[:, :], 3.0, ALU.is_lt)
        v.drain()
        v.tensor_mul(r_wun[:, :], r_sel[:, :], r_B[:, 0:1])
        inst = v.drain()
        inst.then_inc(r_dv, 1)
        v.wait_ge(r_pe, 5)
        v.reciprocal(r_rZ[0:1, :], PB(4, 0, 1, 0, 1))
        for s3 in range(3):
            v.tensor_single_scalar(r_oh[:, s3:s3 + 1], PB(4, 1, 1, 0, 8),
                                   float(s3), ALU.is_equal)
        v.drain()
        sel_b = bass.AP(r_sel, 0, [[1, 8], [0, 3]])
        v.tensor_tensor(r_oh[:, :], r_oh[:, :], sel_b, ALU.mult)
        inst = v.drain()
        inst.then_inc(r_dv, 1)
        v.wait_ge(r_pe, 6)
        v.tensor_scalar(r_wsel[0:1, :], PB(5, 0, 3, 0, 1), r_rZ[0:1, 0:1], None,
                        ALU.mult)
        v.tensor_copy(r_id[0:1, :], PB(5, 3, 3, 0, 1))
        inst = v.drain()
        inst.then_inc(r_dv, 1)
        v.wait_ge(r_pe, 7)
        v.tensor_copy(w_bc[:, :], PB(6, 0, 3))
        v.tensor_copy(e_bc[:, :], PB(6, 3, 3))
        v.drain()
        v.tensor_scalar_mul(e_bc[:, :], e_bc[:, :], 4096.0)
        v.drain()
        for s3 in range(3):
            v.tensor_scalar(idx_f[:, s3 * 32:(s3 + 1) * 32], ciofc[:, :],
                            e_bc[:, s3:s3 + 1], None, ALU.add)
        v.drain()
        v.tensor_copy(idx_i[:, :], idx_f[:, :])
        inst = v.drain()
        inst.then_inc(rdone, 1)

    @blk1.scalar
    def _(a):
        a.wait_ge(s_cst, 16 * NCST)
        NRs = NT if trunc is None else trunc

        def AAs(k):
            return (k % 2) * 4

        def sqrt1(k):
            pi = k % 2
            a.wait_ge(v_var1, k + 1)
            inst = a.activation(sqtt[:, pi:pi + 1], bnag[:, pi * 2 + 1:pi * 2 + 2],
                                AF.Sqrt, bias=ceps[:, 0:1])
            inst.then_inc(a_ln1, 1)

        def tr_evac(k):
            pi = k % 2
            A = AAs(k)
            u1f_p = u1f[:, pi * 1024:(pi + 1) * 1024]
            a.wait_ge(t_tr, k + 1)
            a.activation(u1f_p[:, 0:512], PB(A, 0, 512), AF.Copy)
            inst = a.activation(u1f_p[:, 512:1024], PB(A + 1, 0, 512), AF.Copy)
            inst.then_inc(v_tr, 1)

        def q_evac(k):
            pi = k % 2
            A = AAs(k)
            qf_p = qf[:, pi * 1024:(pi + 1) * 1024]
            a.wait_ge(t_q, k + 1)
            a.activation(qf_p[:, 0:512], PB(A + 2, 0, 512), AF.Copy)
            inst = a.activation(qf_p[:, 512:1024], PB(A, 0, 512), AF.Copy)
            inst.then_inc(v_q, 1)

        def o_evac(k):
            pi = k % 2
            A = AAs(k)
            of_p = o_f[:, pi * 1024:(pi + 1) * 1024]
            a.wait_ge(t_oh, 2 * (k + 1))
            a.activation(of_p[:, 0:512], PB(A + 1, 0, 512), AF.Copy)
            inst = a.activation(of_p[:, 512:1024], PB(A + 2, 0, 512), AF.Copy)
            inst.then_inc(v_o, 1)

        def sqrt2(k):
            pi = k % 2
            a.wait_ge(v_var2, k + 1)
            inst = a.activation(sqtt[:, 2 + pi:3 + pi],
                                bnag[:, 4 + pi * 2 + 1:4 + pi * 2 + 2],
                                AF.Sqrt, bias=ceps[:, 0:1])
            inst.then_inc(a_ln2, 1)

        sqrt1(0)
        sqrt1(1) if NRs > 1 else None
        tr_evac(0)
        q_evac(0)
        for k in range(NRs):
            pi = k % 2
            if k + 1 < NRs:
                tr_evac(k + 1)
                q_evac(k + 1)
            if k >= 1:
                sqrt2(k - 1)
            a.wait_ge(v_sub, k + 1)
            inst = a.activation(exw[:, pi * 256:(pi + 1) * 256],
                                msub[:, pi * 256:(pi + 1) * 256], AF.Exp,
                                scale=0.125)
            inst.then_inc(a_exp, 1)
            o_evac(k)
            if k + 2 < NRs:
                sqrt1(k + 2)
        sqrt2(NRs - 1)

        if trunc is not None:
            return
        a.wait_ge(r_pe, 2)
        for fc2 in range(2):
            inst = a.activation(r_h[:, fc2:fc2 + 1], PB(1, fc2, 1), AF.Tanh,
                                bias=b1f[:, fc2:fc2 + 1])
        inst.then_inc(r_ac, 1)
        a.wait_ge(r_dv, 3)
        inst = a.activation(r_ex[0:1, :], r_logit[0:1, :], AF.Exp,
                            bias=r_mxn[0:1, 0:1], accum_out=r_es[0:1, 0:1])
        inst.then_inc(r_ac, 1)

    es1.close()

    es3 = contextlib.ExitStack()
    sb3 = lambda name, shape, dt=F32: es3.enter_context(nc.sbuf_tensor(name, shape, dt))
    u_fm3 = sb3("u_fm3", [128, 16384])
    acc3 = sb3("acc3", [128, 16384])
    wa_buf = sb3("wa_buf", [128, 8192])
    wb_buf = sb3("wb_buf", [128, 8192])
    g_sb = sb3("g_sb", [128, 512])
    u2t3 = sb3("u2t3", [128, 1024])

    blk2 = es3.enter_context(nc.Block("phase3"))

    @blk2.gpsimd
    def _(g):
        if trunc is not None:
            return
        g.wait_ge(rdone, 1)
        g.wait_ge(s_ys0, 16 * NT // 2)
        g.wait_ge(s_ys1, 16 * NT // 2)
        nblk = 0
        for p in range(NPASS):
            for r in range(TPP):
                gi = p * TPP + r
                pi = gi % 2
                if gi >= 2:
                    g.wait_ge(v_yb, gi - 1)
                g.indirect_dma_start(
                    x_in[:, pi * 1024:(pi + 1) * 1024], None,
                    yscr_d[:, :],
                    bass.IndirectOffsetOnAxis(cyidx[:, gi:gi + 1], 0),
                ).then_inc(s_yr0 if pi == 0 else s_yr1, 16)
            for s in range(3):
                for fcb in range(8):
                    pi = nblk % 2
                    if nblk >= 2:
                        g.wait_ge(t_o3, 4 * TQP * (nblk - 1))
                    for j in range(4):
                        fc = fcb * 4 + j
                        g.indirect_dma_start(
                            wa_buf[:, pi * 4096 + j * 1024: pi * 4096 + (j + 1) * 1024],
                            None, wa_d[:, :],
                            bass.IndirectOffsetOnAxis(
                                idx_i[:, s * 32 + fc:s * 32 + fc + 1], 0),
                        ).then_inc(s_ga0 if pi == 0 else s_ga1, 16)
                        g.indirect_dma_start(
                            wb_buf[:, pi * 4096 + j * 1024: pi * 4096 + (j + 1) * 1024],
                            None, wb_d[:, :],
                            bass.IndirectOffsetOnAxis(
                                idx_i[:, s * 32 + fc:s * 32 + fc + 1], 0),
                        ).then_inc(s_ga0 if pi == 0 else s_ga1, 16)
                    nblk += 1

    @blk2.sync
    def _(sync):
        if trunc is not None:
            return
        EVP = 3 * 8 * TQP * 2
        for p in range(NPASS):
            for r in range(TPP):
                last = p * EVP + ((2 * 8 + 7) * TQP + r // 2) * 2 + (r % 2) + 1
                sync.wait_ge(v_ev, last)
                sync.dma_start(
                    out_d[(p * TPP + r) * 128:(p * TPP + r + 1) * 128, :],
                    in_=acc3[:, r * 1024:(r + 1) * 1024]).then_inc(s_out, 16)
        sync.wait_ge(s_out, 16 * NPASS * TPP)

    @blk2.tensor
    def _(t):
        if trunc is not None:
            return
        blk_cnt = 0
        for p in range(NPASS):
            for r in range(TPP):
                gi = p * TPP + r
                pi = gi % 2
                t.wait_ge(v_u23, gi + 1)
                if gi >= 1:
                    t.wait_ge(v_tr3, 2 * gi)
                for ic in range(8):
                    bank = 3 if ic < 4 else 7
                    inst = t.transpose(PB(bank, (ic % 4) * 128, 128),
                                       u2t3[:, ic * 128:(ic + 1) * 128],
                                       cident[:, :])
                    if ic == 3:
                        inst.then_inc(t_tr3, 1)
                inst.then_inc(t_tr3, 1)
            t.wait_ge(v_tr3, 2 * TPP * (p + 1))
            steps = []
            for s in range(3):
                for fcb in range(8):
                    for tq in range(TQP):
                        for j in range(4):
                            steps.append((s, fcb, tq, j))

            def z_step(idx):
                s, fcb, tq, j = steps[idx]
                blk = (p * 3 + s) * 8 + fcb
                pi = blk % 2
                J = p * 3 * 8 * TQP * 4 + idx
                if j == 0 and tq == 0:
                    t.wait_ge(s_ga0 if pi == 0 else s_ga1, 128 * (blk // 2 + 1))
                zb = 0 if J % 2 == 0 else 4
                if J >= 2:
                    t.wait_ge(a_g, J - 1)
                for ic in range(8):
                    inst = t.matmul(
                        PB(zb, 0, 256),
                        wa_buf[:, pi * 4096 + j * 1024 + ic * 128: pi * 4096 + j * 1024 + (ic + 1) * 128],
                        u_fm3[:, ic * TPP * 128 + tq * 256: ic * TPP * 128 + tq * 256 + 256],
                        start=(ic == 0), stop=(ic == 7))
                inst.then_inc(t_z, 1)

            def o_step(idx):
                s, fcb, tq, j = steps[idx]
                blk = (p * 3 + s) * 8 + fcb
                pi = blk % 2
                J = p * 3 * 8 * TQP * 4 + idx
                unit = J // 4
                t.wait_ge(a_g, J + 1)
                if j == 0 and unit >= 1:
                    t.wait_ge(v_ev, 2 * unit)
                for th in range(2):
                    obase = 512 + th * 2048
                    for nh in range(2):
                        inst = t.matmul(
                            ps_t[:, obase + nh * 512: obase + (nh + 1) * 512],
                            g_sb[:, (J % 2) * 256 + th * 128:(J % 2) * 256 + th * 128 + 128],
                            wb_buf[:, pi * 4096 + j * 1024 + nh * 512: pi * 4096 + j * 1024 + (nh + 1) * 512],
                            start=(j == 0), stop=(j == 3))
                inst.then_inc(t_o3, 1)

            z_step(0)
            for idx in range(len(steps)):
                if idx + 1 < len(steps):
                    z_step(idx + 1)
                o_step(idx)
            blk_cnt += 24

    @blk2.vector
    def _(v):
        if trunc is not None:
            return
        for p in range(NPASS):
            if p >= 1:
                v.wait_ge(s_out, 16 * TPP * p)
            for r in range(TPP):
                gi = p * TPP + r
                pi = gi % 2
                y_p = x_in[:, pi * 1024:(pi + 1) * 1024]
                u23_p = u2t3[:, 0:1024]
                if gi >= 1:
                    v.wait_ge(t_tr3, 2 * gi)
                v.wait_ge(s_yr0 if pi == 0 else s_yr1, 16 * (gi // 2 + 1))
                if gi >= 2:
                    v.wait_ge(a_ln3, gi - 1)
                v.bn_stats(bnst[:, pi * 12:pi * 12 + 6], y_p[:, 0:512])
                v.bn_stats(bnst[:, pi * 12 + 6:pi * 12 + 12], y_p[:, 512:1024])
                v.drain()
                inst = v.bn_aggr(bnag[:, pi * 2:pi * 2 + 2],
                                 bnst[:, pi * 12:pi * 12 + 12])
                v.drain()
                inst.then_inc(v_var3, 1)
                v.wait_ge(a_ln3, gi + 1)
                v.reciprocal(rstdt[:, pi:pi + 1], sqtt[:, pi:pi + 1])
                v.drain()
                inst = v.tensor_scalar(u23_p, y_p, bnag[:, pi * 2:pi * 2 + 1],
                                       rstdt[:, pi:pi + 1], ALU.subtract, ALU.mult)
                inst.then_inc(v_u23, 1)
                v.drain()
                inst = v.tensor_add(acc3[:, r * 1024:(r + 1) * 1024], y_p, u23_p)
                inst.then_inc(v_yb, 1)
                v.wait_ge(t_tr3, 2 * gi + 1)
                for ic in range(4):
                    v.tensor_copy(u_fm3[:, ic * TPP * 128 + r * 128: ic * TPP * 128 + (r + 1) * 128],
                                  PB(3, ic * 128, 128))
                v.wait_ge(t_tr3, 2 * gi + 2)
                for ic in range(4, 8):
                    inst = v.tensor_copy(
                        u_fm3[:, ic * TPP * 128 + r * 128: ic * TPP * 128 + (r + 1) * 128],
                        PB(7, (ic - 4) * 128, 128))
                inst.then_inc(v_tr3, 2)
            for s in range(3):
                for fcb in range(8):
                    for tq in range(TQP):
                        unit = ((p * 3 + s) * 8 + fcb) * TQP + tq
                        v.wait_ge(t_o3, 4 * (unit + 1))
                        for th in range(2):
                            obase = 512 + th * 2048
                            tile_local = tq * 2 + th
                            inst = v.scalar_tensor_tensor(
                                acc3[:, tile_local * 1024:(tile_local + 1) * 1024],
                                ps_t[:, obase:obase + 1024],
                                w_bc[:, s:s + 1],
                                acc3[:, tile_local * 1024:(tile_local + 1) * 1024],
                                ALU.mult, ALU.add)
                            inst.then_inc(v_ev, 1)

    @blk2.scalar
    def _(a):
        if trunc is not None:
            return
        for p in range(NPASS):
            for r in range(TPP):
                gi = p * TPP + r
                pi = gi % 2
                a.wait_ge(v_var3, gi + 1)
                inst = a.activation(sqtt[:, pi:pi + 1], bnag[:, pi * 2 + 1:pi * 2 + 2],
                                    AF.Sqrt, bias=ceps[:, 0:1])
                inst.then_inc(a_ln3, 1)
            for s in range(3):
                for fcb in range(8):
                    for tq in range(TQP):
                        unit = ((p * 3 + s) * 8 + fcb) * TQP + tq
                        for j in range(4):
                            J = unit * 4 + j
                            zb = 0 if J % 2 == 0 else 4
                            a.wait_ge(t_z, J + 1)
                            if J >= 2:
                                a.wait_ge(t_o3, J - 1)
                            inst = a.activation(
                                g_sb[:, (J % 2) * 256:(J % 2 + 1) * 256],
                                PB(zb, 0, 256), AF.Gelu_apprx_tanh)
                            inst.then_inc(a_g, 1)

    es3.close()
    es.close()
    return nc


def prep_inputs(inputs):
    W_attn = np.asarray(inputs["L0_attn_W_attn"], dtype=np.float32)
    W_proj = np.asarray(inputs["L0_attn_W_proj"], dtype=np.float32)
    W1 = np.asarray(inputs["L0_ffn_W1"], dtype=np.float32)
    b1 = np.asarray(inputs["L0_ffn_b1"], dtype=np.float32)
    W2 = np.asarray(inputs["L0_ffn_W2"], dtype=np.float32)
    b2 = np.asarray(inputs["L0_ffn_b2"], dtype=np.float32)
    Wa = np.asarray(inputs["L0_ffn_Wa"], dtype=np.float32)
    Wb = np.asarray(inputs["L0_ffn_Wb"], dtype=np.float32)

    d = {}
    d["wq_t"] = np.ascontiguousarray(
        W_attn[0:1024].T.reshape(8, 128, 8, 128).transpose(1, 0, 2, 3).reshape(128, 8192))
    d["wk_t"] = np.ascontiguousarray(
        W_attn[1024:2048].T.reshape(8, 128, 1024).transpose(1, 0, 2).reshape(128, 8192))
    d["wv_t"] = np.ascontiguousarray(
        W_attn[2048:3072].T.reshape(8, 128, 1024).transpose(1, 0, 2).reshape(128, 8192))
    d["wp_t"] = np.ascontiguousarray(
        W_proj.T.reshape(8, 128, 1024).transpose(1, 0, 2).reshape(128, 8192))
    d["w1_t"] = np.ascontiguousarray(
        W1.reshape(8, 128, 2, 128).transpose(1, 0, 2, 3).reshape(128, 2048))
    d["b1_t"] = np.ascontiguousarray(b1.reshape(2, 128).T)
    d["w2_t"] = np.ascontiguousarray(
        W2.reshape(2, 128, 8).transpose(1, 0, 2).reshape(128, 16))
    d["b2_t"] = np.ascontiguousarray(b2.reshape(1, 8))
    d["wa_r"] = np.ascontiguousarray(
        Wa.reshape(8, 8, 128, 32, 128).transpose(0, 3, 2, 1, 4).reshape(32768, 1024))
    d["wb_r"] = np.ascontiguousarray(Wb.reshape(32768, 1024))
    d["cident"] = np.eye(128, dtype=np.float32)
    t_ar = np.arange(128)
    j_ar = np.arange(16)
    m = np.where(j_ar[None, :] <= t_ar[:, None], 0.0, NEG).astype(np.float32)
    d["cmask"] = np.ascontiguousarray(np.tile(m, (1, 16)))
    p_ar = np.arange(128, dtype=np.float32)
    d["ciota_fc"] = np.ascontiguousarray(
        (p_ar[:, None] + 128.0 * np.arange(32, dtype=np.float32)[None, :]).astype(np.float32))
    d["cones_r"] = np.ones((1, 128), dtype=np.float32)
    d["cones_c"] = np.ones((128, 1), dtype=np.float32)
    i8 = np.arange(8)
    d["clow8"] = (i8[None, :] < i8[:, None]).astype(np.float32)
    d["cup8"] = (i8[:, None] < i8[None, :]).astype(np.float32)
    d["ciota_p"] = p_ar.reshape(128, 1).astype(np.float32)
    d["ceps"] = np.full((128, 1), EPS, dtype=np.float32)
    return d


def core_inputs(inputs, shared, c):
    x = np.asarray(inputs["L0_attn_x"], dtype=np.float32)
    b, half = c // 2, c % 2
    m = dict(shared)
    m["x"] = np.ascontiguousarray(x[b].reshape(8192, 1024))
    base = half * 4096
    rows = (base + np.arange(4096)).reshape(32, 128).T
    m["cyidx"] = np.ascontiguousarray(rows.astype(np.int32))
    return m



from concourse.bass_utils import run_bass_kernel_spmd


def kernel(**inputs):
    shared = prep_inputs(inputs)
    in_maps = [core_inputs(inputs, shared, c) for c in range(8)]
    nc = build_nc()
    res = run_bass_kernel_spmd(nc, in_maps, core_ids=list(range(8)))
    out = np.empty((4, 8192, 1024), dtype=np.float32)
    for c in range(8):
        b, half = c // 2, c % 2
        out[b, half * 4096:(half + 1) * 4096, :] = res.results[c]["out"]
    return out
